# version 17
# baseline (speedup 1.0000x reference)
import numpy as np
import ml_dtypes
import concourse.bass as bass
import concourse.mybir as mybir
from concourse.bass_utils import run_bass_kernel_spmd

F32 = mybir.dt.float32
BF16 = mybir.dt.bfloat16
AF = mybir.ActivationFunctionType
ALU = mybir.AluOpType
AX = mybir.AxisListType

P = 128
D = 1024
NCH = D // P
TPC = 2048
TT = TPC // P
NQG = TPC // 512
NCORES = 8
SEQ = 8192
EPS_LN = 1e-5
EPS_RMS = float(np.finfo(np.float32).eps)

ENGINES = ("sync", "gpsimd", "vector", "scalar", "tensor")


class Sched:

    def __init__(self, nc):
        self.nc = nc
        self.sem = {}
        self.count = {}
        self._fenced = {}
        self._owner = {}
        for e in ENGINES:
            self._key(e, e)
        self.streams = {e: [] for e in ENGINES}
        self._nofence = set()

    def _key(self, key, owner):
        if key not in self.sem:
            self.sem[key] = self.nc.alloc_semaphore(name=f"s_{key}")
            self.count[key] = 0
            self._fenced[key] = 0
            self._owner[key] = owner
        return key

    def to(self, ename, fn, waits=(), inc=0, serialize=None, ch=None,
           fence=True):
        if serialize is None:
            serialize = ename in ("vector", "scalar", "gpsimd")
        wl = list(waits)
        if serialize:
            if self.count[ename] > 0:
                wl.append((self.sem[ename], self.count[ename]))
            inc = max(inc, 1)
        key = self._key(ch, ename) if ch else ename
        if not fence:
            self._nofence.add(key)
        mark = None
        if inc:
            self.count[key] += inc
            mark = (self.sem[key], self.count[key])
        self.streams[ename].append((tuple(wl), fn, inc, self.sem[key]))
        return mark

    def dma(self, ename, out, in_, waits=(), ch=None, fn=None, fence=True):
        key = self._key(ch, ename) if ch else self._key(ename + ":d", ename)
        if not fence:
            self._nofence.add(key)
        if fn is None:
            fn = lambda e: e.dma_start(out, in_)
        self.count[key] += 16
        mark = (self.sem[key], self.count[key])
        self.streams[ename].append(((tuple(waits)), fn, 16, self.sem[key]))
        return mark

    def barrier(self):
        for key, owner in list(self._owner.items()):
            if key in self._nofence:
                continue
            if self.count[key] > self._fenced[key]:
                sem, val = self.sem[key], self.count[key]
                self.streams[owner].append(
                    ((), lambda eng, sem=sem, val=val: eng.wait_ge(sem, val) and None, 0, sem))
                self._fenced[key] = val
        nc = self.nc
        streams = self.streams
        self.streams = {e: [] for e in ENGINES}
        active = [e for e in ENGINES if streams[e]]
        if not active:
            return
        with nc.Block() as blk:
            for e in active:
                def make(ops):
                    def emit(eng):
                        for waits, fn, inc, sem in ops:
                            for (wsem, wval) in waits:
                                eng.wait_ge(wsem, wval)
                            instr = fn(eng)
                            if inc:
                                instr.then_inc(sem, inc, skip_validation=True)
                    return emit
                getattr(blk, e)(make(streams[e]))


def tile_view(dram, i, rows=P, cols=None):
    ap = dram.ap().rearrange("(n p) c -> n p c", p=rows)
    return ap[i]



def emit_l0_front(nc, S, dr):
    id32 = nc.inline_tensor(np.eye(P, dtype=np.float32), "id32")
    idbf = nc.inline_tensor(np.eye(P, dtype=ml_dtypes.bfloat16), "idbf")

    from contextlib import ExitStack
    with ExitStack() as ctx:
        def sb(name, shape, dt):
            return ctx.enter_context(nc.sbuf_tensor(name, shape, dt))

        def pt(name, shape, dt):
            return ctx.enter_context(nc.psum_tensor(name, shape, dt))

        x_sb = sb("x_sb", [P, TT, D], F32)
        xnfm = sb("xnfm_s1", [P, NCH, TPC], BF16)
        w3 = sb("w3", [P, 3, NCH, D], BF16)
        w3b = sb("w3b", [1, 3, D], BF16)
        ones_bf = sb("ones_bf", [1, TPC], BF16)
        idbf_sb = sb("idbf_sb", [P, P], BF16)
        id32_sb = sb("id32_sb", [P, P], F32)
        xn_t = sb("xn_t", [P, 2, D], F32)
        r1_t = sb("r1_t", [P, 2, D], F32)
        xnb = sb("xnb", [P, 2, D], BF16)
        xsq = sb("xsq", [P, D], F32)
        st = sb("st", [P, 8], F32)
        qstage = sb("qstage", [P, 2, TPC], BF16)
        kvstage = sb("kvstage", [P, 2, D], BF16)
        ksq = sb("ksq", [P, D], F32)
        knall = sb("knall", [P, TT], F32)
        knfin = sb("knfin", [1, 4], F32)
        ceps = sb("ceps", [P, 1], F32)
        pmm0 = pt("pmm0", [P, 512], F32)
        pmm1 = pt("pmm1", [P, 512], F32)
        pmm2 = pt("pmm2", [P, 512], F32)
        pmm3 = pt("pmm3", [P, 512], F32)
        ptr0 = pt("ptr0", [P, P], BF16)
        ptr1 = pt("ptr1", [P, P], BF16)
        ptr32 = pt("ptr32", [P, P], F32)
        pmm = [pmm0, pmm1, pmm2, pmm3]
        ptr = [ptr0, ptr1]

        S.dma("sync", x_sb[:, :, :], dr["x"].ap().rearrange("(n p) c -> p n c", p=P))
        for wi, nm in enumerate(("wqT", "wkT", "wvT")):
            wd = dr[nm]
            S.dma("gpsimd", w3[:, wi, :, :],
                  wd.ap()[0:D, :].rearrange("(c p) n -> p c n", p=P))
            S.dma("gpsimd", w3b[:, wi, :], wd.ap()[D:D + 1, :])
        S.dma("sync", idbf_sb[:, :], idbf[:, :])
        S.dma("sync", id32_sb[:, :], id32[:, :])
        S.to("vector", lambda e: e.memset(ones_bf[:, :], 1.0))
        S.to("vector", lambda e: e.memset(ceps[:, :], EPS_LN))
        S.barrier()

        dma_xn = {}
        dma_r1 = {}
        evac_marks = {}
        xn_marks = {}
        tpose_marks = {}
        s1 = st[:, 0:1]; s2 = st[:, 1:2]; m_ = st[:, 2:3]; ms = st[:, 3:4]
        var = st[:, 4:5]; std = st[:, 5:6]; rstd = st[:, 6:7]; nb = st[:, 7:8]
        for i in range(TT):
            par = i % 2
            x_t = x_sb[:, i, :]
            S.to("vector", lambda e, x_t=x_t: e.reduce_sum(s1, x_t, axis=AX.X))
            msq = S.to("scalar", lambda e, x_t=x_t: e.activation(
                xsq[:, :], x_t, AF.Square, accum_out=s2), inc=1)
            S.to("vector", lambda e: e.tensor_scalar(m_, s1, 1.0 / D, None, ALU.mult))
            S.to("vector", lambda e: e.tensor_scalar(ms, s2, 1.0 / D, None, ALU.mult),
                 waits=[msq])
            S.to("vector", lambda e: e.tensor_scalar(s1, m_, m_, None, ALU.mult))
            mvar = S.to("vector", lambda e: e.tensor_scalar(var, ms, s1, None, ALU.subtract), inc=1)
            mstd = S.to("scalar", lambda e: e.activation(std, var, AF.Sqrt, bias=ceps[:, 0:1]),
                        waits=[mvar], inc=1)
            S.to("vector", lambda e: e.reciprocal(rstd, std), waits=[mstd])
            mnb = S.to("vector", lambda e: e.tensor_scalar(nb, m_, rstd, -1.0,
                                                           ALU.mult, ALU.mult), inc=1)
            w_xn = [mnb]
            if i >= 2:
                w_xn.append(dma_xn[i - 2])
            mxn = S.to("scalar", lambda e, x_t=x_t, par=par: e.activation(
                xn_t[:, par, :], x_t, AF.Identity, bias=nb, scale=rstd),
                waits=w_xn, inc=1)
            xn_marks[i] = mxn
            dma_xn[i] = S.dma("sync", tile_view(dr["xn"], i), xn_t[:, par, :],
                              waits=[mxn], ch=f"xn{par}")
            w_r1 = [mxn]
            if i >= 2:
                w_r1.append(dma_r1[i - 2])
            mr1 = S.to("vector", lambda e, x_t=x_t, par=par: e.tensor_add(
                r1_t[:, par, :], x_t, xn_t[:, par, :]), waits=w_r1, inc=1)
            dma_r1[i] = S.dma("sync", tile_view(dr["r1"], i), r1_t[:, par, :],
                              waits=[mr1], ch=f"r1{par}")
            w_xb = [mxn]
            if i >= 2:
                w_xb.append(tpose_marks[(i - 2, NCH - 1)])
            mxb = S.to("vector", lambda e, par=par: e.tensor_copy(
                xnb[:, par, :], xn_t[:, par, :]), waits=w_xb, inc=1)
            for c in range(NCH):
                pslot = c % 2
                w_t = [mxb]
                if (i, c - 2) in evac_marks:
                    w_t.append(evac_marks[(i, c - 2)])
                elif (i - 1, c + NCH - 2) in evac_marks:
                    w_t.append(evac_marks[(i - 1, c + NCH - 2)])
                mt = S.to("tensor", lambda e, par=par, c=c, pslot=pslot: e.transpose(
                    ptr[pslot][:, :], xnb[:, par, c * P:(c + 1) * P], idbf_sb[:, :]),
                    waits=w_t, inc=1)
                tpose_marks[(i, c)] = mt
                evac_marks[(i, c)] = S.to("vector", lambda e, i=i, c=c, pslot=pslot:
                    e.tensor_copy(xnfm[:, c, i * P:(i + 1) * P], ptr[pslot][:, :]),
                    waits=[mt], inc=1)
        S.barrier()

        unit = 0
        evq = {}
        dmaq = {}
        for oc in range(NCH):
            for tg in range(NQG):
                ps = pmm[unit % 4]
                w_mm = []
                if unit >= 4:
                    w_mm.append(evq[unit - 4])
                for c in range(NCH):
                    mmk = S.to("tensor", lambda e, oc=oc, tg=tg, c=c, ps=ps: e.matmul(
                        ps[:, :], w3[:, 0, c, oc * P:(oc + 1) * P],
                        xnfm[:, c, tg * 512:(tg + 1) * 512],
                        start=(c == 0), stop=False),
                        waits=(w_mm if c == 0 else []))
                mlast = S.to("tensor", lambda e, oc=oc, tg=tg, ps=ps: e.matmul(
                    ps[:, :], w3b[:, 0, oc * P:(oc + 1) * P],
                    ones_bf[:, tg * 512:(tg + 1) * 512],
                    start=False, stop=True), inc=1)
                w_ev = [mlast]
                if oc >= 2 and tg == 0:
                    w_ev.append(dmaq[oc - 2])
                evq[unit] = S.to("vector", lambda e, oc=oc, tg=tg, ps=ps: e.tensor_copy(
                    qstage[:, oc % 2, tg * 512:(tg + 1) * 512], ps[:, :]),
                    waits=w_ev, inc=1)
                unit += 1
            dmaq[oc] = S.dma("sync",
                             dr["qfm"].ap().rearrange("p (c t) -> p c t", c=NCH)[:, oc, :],
                             qstage[:, oc % 2, :], waits=[evq[unit - 1]],
                             ch=f"q{oc % 2}")

        kv_unit = 0
        evkv = {}
        dmakv = {}
        knmarks = {}
        for which, wi, dname in (("k", 1, "ktm"), ("v", 2, "vtm")):
            for tt in range(TT):
                par = tt % 2
                for half in range(2):
                    ps = pmm[kv_unit % 4]
                    w_mm = []
                    if kv_unit >= 4:
                        w_mm.append(evkv[kv_unit - 4])
                    else:
                        w_mm.append(evq[unit - 4 + kv_unit])
                    if kv_unit // 2 >= 2 and half == 0:
                        w_mm.append(dmakv[kv_unit // 2 - 2])
                    for c in range(NCH):
                        S.to("tensor", lambda e, wi=wi, tt=tt, half=half, c=c, ps=ps:
                             e.matmul(ps[:, :], xnfm[:, c, tt * P:(tt + 1) * P],
                                      w3[:, wi, c, half * 512:(half + 1) * 512],
                                      start=(c == 0), stop=False),
                             waits=(w_mm if c == 0 else []))
                    mlast = S.to("tensor", lambda e, wi=wi, tt=tt, half=half, ps=ps:
                                 e.matmul(ps[:, :], ones_bf[:, tt * P:(tt + 1) * P],
                                          w3b[:, wi, half * 512:(half + 1) * 512],
                                          start=False, stop=True), inc=1)
                    w_ev = [mlast]
                    tvg = kv_unit // 2
                    if half == 0 and tvg >= 2 and (tvg - 2) in knmarks:
                        w_ev.append(knmarks[tvg - 2])
                    evkv[kv_unit] = S.to("vector", lambda e, par=par, half=half, ps=ps:
                        e.tensor_copy(kvstage[:, par, half * 512:(half + 1) * 512],
                                      ps[:, :]), waits=w_ev, inc=1)
                    kv_unit += 1
                dmakv[kv_unit // 2 - 1] = S.dma(
                    "sync", tile_view(dr[dname], tt), kvstage[:, par, :],
                    waits=[evkv[kv_unit - 1]], ch=f"kv{par}")
                if which == "k":
                    knmarks[tt] = S.to("scalar", lambda e, par=par, tt=tt: e.activation(
                        ksq[:, :], kvstage[:, par, :], AF.Square,
                        accum_out=knall[:, tt:tt + 1]),
                        waits=[evkv[kv_unit - 1]], inc=1)
        S.dma("gpsimd", dr["xnfm"].ap().rearrange("p (c t) -> p c t", c=NCH),
              xnfm[:, :, :], waits=[evac_marks[(TT - 1, NCH - 1)]])
        mred = S.to("vector", lambda e: e.tensor_reduce(
            st[:, 0:1], knall[:, :], AX.X, ALU.max),
            waits=[knmarks[TT - 1]], inc=1)
        mtr = S.to("tensor", lambda e: e.transpose(ptr32[0:1, :], st[:, 0:1], id32_sb[:, :]),
                   waits=[mred], inc=1)
        mred2 = S.to("vector", lambda e: e.tensor_reduce(
            knfin[:, 0:1], ptr32[0:1, 0:P], AX.X, ALU.max), waits=[mtr], inc=1)
        msqr = S.to("scalar", lambda e: e.activation(knfin[:, 1:2], knfin[:, 0:1], AF.Sqrt),
                    waits=[mred2], inc=1)
        mend = S.dma("sync", dr["knmax"].ap(), knfin[:, 1:2], waits=[msqr])
        S.to("sync", lambda e: e.wait_ge(mend[0], mend[1]))
        S.barrier()


def build_stage1():
    nc = bass.Bass("TRN2", target_bir_lowering=False)
    dr = {}
    dr["x"] = nc.dram_tensor("x", [TPC, D], F32, kind="ExternalInput")
    dr["wqT"] = nc.dram_tensor("wqT", [D + 1, D], F32, kind="ExternalInput")
    dr["wkT"] = nc.dram_tensor("wkT", [D + 1, D], F32, kind="ExternalInput")
    dr["wvT"] = nc.dram_tensor("wvT", [D + 1, D], F32, kind="ExternalInput")
    dr["xn"] = nc.dram_tensor("xn", [TPC, D], F32, kind="ExternalOutput")
    dr["r1"] = nc.dram_tensor("r1", [TPC, D], F32, kind="ExternalOutput")
    dr["qfm"] = nc.dram_tensor("qfm", [P, NCH * TPC], BF16, kind="ExternalOutput")
    dr["xnfm"] = nc.dram_tensor("xnfm", [P, NCH * TPC], BF16, kind="ExternalOutput")
    dr["ktm"] = nc.dram_tensor("ktm", [TPC, D], BF16, kind="ExternalOutput")
    dr["vtm"] = nc.dram_tensor("vtm", [TPC, D], BF16, kind="ExternalOutput")
    dr["knmax"] = nc.dram_tensor("knmax", [1, 1], F32, kind="ExternalOutput")
    S = Sched(nc)
    emit_l0_front(nc, S, dr)
    return nc



def emit_l0_attn(nc, S, dr):
    idbf = nc.inline_tensor(np.eye(P, dtype=ml_dtypes.bfloat16), "idbf2")
    KT = SEQ // 1024
    from contextlib import ExitStack
    with ExitStack() as ctx:
        def sb(name, shape, dt):
            return ctx.enter_context(nc.sbuf_tensor(name, shape, dt))

        def pt(name, shape, dt):
            return ctx.enter_context(nc.psum_tensor(name, shape, dt))

        qfm = sb("qfm_sb", [P, NCH, TPC], BF16)
        ptall = sb("ptall", [P, 64, 512], BF16)
        kfm = sb("kfm", [P, 2, NCH, 1024], BF16)
        vh = sb("vh", [P, 2, NCH, 512], BF16)
        zfm = sb("zfm", [P, NCH, 512], BF16)
        lb = sb("lb", [P, 512], F32)
        r1_t = sb("r1_t2", [P, 2, D], F32)
        r2_t = sb("r2_t2", [P, 2, D], F32)
        qsq = sb("qsq", [P, 2, 512], BF16)
        onesc = sb("onesc", [P, 1], BF16)
        onesr = sb("onesr", [1, P], BF16)
        bqbf = sb("bqbf", [1, NQG * 512], BF16)
        rows = sb("rows", [1, 8], F32)
        lrow = sb("lrow", [1, 512], F32)
        lrowb = sb("lrowb", [1, 512], BF16)
        knin = sb("knin", [1, 4], F32)
        idbf_sb = sb("idbf_sb2", [P, P], BF16)
        pst0 = pt("pst0", [P, 512], F32)
        pst1 = pt("pst1", [P, 512], F32)
        pz = [pt(f"pz{i}", [P, 512], F32) for i in range(4)]
        pl = pt("pl", [1, 512], F32)
        ptr_ = pt("ptrb", [P, P], BF16)
        pst = [pst0, pst1]

        S.dma("sync", qfm[:, :, :],
              dr["qfm"].ap().rearrange("p (c t) -> p c t", c=NCH))
        S.dma("sync", knin[:, :], dr["knmax4"].ap().rearrange("r c -> c r"))
        S.dma("sync", idbf_sb[:, :], idbf[:, :])
        S.to("vector", lambda e: e.memset(onesc[:, :], 1.0))
        S.to("vector", lambda e: e.memset(onesr[:, :], 1.0))
        S.barrier()

        S.to("vector", lambda e: e.reduce_max(rows[:, 0:1], knin[:, :], axis=AX.X))
        S.to("vector", lambda e: e.tensor_scalar(rows[:, 1:2], rows[:, 0:1],
                                                 -1.002, None, ALU.mult))
        mm_sq = {}
        sqrt_m = None
        mbq_prev = None
        for qg in range(NQG):
            for c in range(NCH):
                par = c % 2
                w_sq = []
                if (qg, c - 2) in mm_sq:
                    w_sq.append(mm_sq[(qg, c - 2)])
                elif (qg - 1, c + NCH - 2) in mm_sq:
                    w_sq.append(mm_sq[(qg - 1, c + NCH - 2)])
                msq = S.to("scalar", lambda e, qg=qg, c=c, par=par: e.activation(
                    qsq[:, par, :], qfm[:, c, qg * 512:(qg + 1) * 512], AF.Square),
                    waits=w_sq, inc=1)
                w_mm = [msq]
                if c == 0 and sqrt_m is not None:
                    w_mm.append(sqrt_m)
                mm_sq[(qg, c)] = S.to("tensor", lambda e, par=par, c=c: e.matmul(
                    pst0[0:1, :], onesc[:, :], qsq[:, par, :],
                    start=(c == 0), stop=(c == NCH - 1)),
                    waits=w_mm, inc=1)
            w_sqrt = [mm_sq[(qg, NCH - 1)]]
            if mbq_prev is not None:
                w_sqrt.append(mbq_prev)
            sqrt_m = S.to("scalar", lambda e: e.activation(
                lrow[:, :], pst0[0:1, :], AF.Sqrt),
                waits=w_sqrt, inc=1)
            mbq_prev = S.to("vector", lambda e, qg=qg: e.tensor_scalar(
                bqbf[:, qg * 512:(qg + 1) * 512], lrow[:, :], rows[:, 1:2],
                None, ALU.mult), waits=[sqrt_m], inc=1)
        S.barrier()

        ex_marks = {}
        phase_end = []
        zn_marks = {}
        pe_kt = {}
        tr_add = {}
        dmar2 = {}
        mrec_prev = None
        for qg in range(NQG):
            my_phase_end = []
            for kt in range(KT):
                par = kt % 2
                ldm = []
                for c in range(NCH):
                    w_ld = []
                    if (qg, (kt - 2) * 8 + 7) in ex_marks:
                        w_ld.append(ex_marks[(qg, (kt - 2) * 8 + 7)])
                    elif (qg - 1, (kt + KT - 2) * 8 + 7) in ex_marks:
                        w_ld.append(ex_marks[(qg - 1, (kt + KT - 2) * 8 + 7)])
                    ldm.append(S.dma(
                        "sync", None, None, ch=f"kf{par}{c}", waits=w_ld,
                        fn=lambda e, par=par, c=c, kt=kt: e.dma_start(
                            kfm[:, par, c, :],
                            dr["kfull"].ap()[kt * 1024:(kt + 1) * 1024,
                                             c * P:(c + 1) * P],
                            transpose=True)))
                for sub in range(8):
                    ktsub = kt * 8 + sub
                    ps = pst[ktsub % 2]
                    w_mm = list(ldm) if sub == 0 else []
                    if (qg, ktsub - 2) in ex_marks:
                        w_mm.append(ex_marks[(qg, ktsub - 2)])
                    elif (qg - 1, 62 + ktsub) in ex_marks:
                        w_mm.append(ex_marks[(qg - 1, 62 + ktsub)])
                    if qg > 0 and ktsub < 2:
                        w_mm += phase_end
                    for c in range(NCH):
                        S.to("tensor", lambda e, par=par, c=c, sub=sub, qg=qg, ps=ps:
                             e.matmul(ps[:, :], kfm[:, par, c, sub * P:(sub + 1) * P],
                                      qfm[:, c, qg * 512:(qg + 1) * 512],
                                      start=(c == 0), stop=False),
                             waits=(w_mm if c == 0 else []))
                    mst = S.to("tensor", lambda e, qg=qg, ps=ps: e.matmul(
                        ps[:, :], onesr[:, :],
                        bqbf[:, qg * 512:(qg + 1) * 512],
                        start=False, stop=True), inc=1)
                    ex_marks[(qg, ktsub)] = S.to(
                        "scalar", lambda e, ktsub=ktsub, ps=ps: e.activation(
                            ptall[:, ktsub, :], ps[:, :], AF.Exp, scale=1.0 / 32.0),
                        waits=[mst], inc=1)
            mlbe = None
            for half in range(2):
                pv_done = None
                for kt in range(KT):
                    par = kt % 2
                    w_v = []
                    prev_pv = (qg, half, kt - 2)
                    if half == 1 and kt < 2:
                        prev_pv = (qg, 0, kt + KT - 2)
                    elif half == 0 and kt < 2:
                        prev_pv = (qg - 1, 1, kt + KT - 2)
                    if prev_pv in pe_kt:
                        w_v.append(pe_kt[prev_pv])
                    mv = S.dma("gpsimd", vh[:, par, :, :],
                               dr["vfull"].ap()[kt * 1024:(kt + 1) * 1024,
                                                half * 512:(half + 1) * 512]
                               .rearrange("(s p) d -> p s d", p=P),
                               ch=f"vh{par}", waits=w_v)
                    for s in range(8):
                        first = (kt == 0 and s == 0)
                        last = (kt == KT - 1 and s == 7)
                        w_mm = [mv, ex_marks[(qg, kt * 8 + 7)]]
                        if first:
                            w_mm += [zn_marks[k] for k in
                                     [((qg, half - 1, ds) if half == 1 else
                                       (qg - 1, 1, ds)) for ds in range(4)]
                                     if k in zn_marks]
                        for ds in range(4):
                            mm = S.to("tensor", lambda e, par=par, s=s, ds=ds, kt=kt,
                                      first=first, last=last:
                                      e.matmul(pz[ds][:, :],
                                               vh[:, par, s, ds * P:(ds + 1) * P],
                                               ptall[:, kt * 8 + s, :],
                                               start=first, stop=last),
                                      waits=(w_mm if ds == 0 else []),
                                      inc=(1 if (s == 7 and ds == 3) else 0))
                            if s == 7 and ds == 3:
                                pe_kt[(qg, half, kt)] = mm
                            if last:
                                pv_done = mm
                        if half == 0:
                            w_l = []
                            if kt == 0 and s == 0 and mrec_prev is not None:
                                w_l.append(mrec_prev)
                            ml = S.to("tensor", lambda e, s=s, kt=kt: e.matmul(
                                pl[:, :], onesc[:, :], ptall[:, kt * 8 + s, :],
                                start=(kt == 0 and s == 0),
                                stop=(kt == KT - 1 and s == 7)),
                                waits=w_l,
                                inc=(1 if (kt == KT - 1 and s == 7) else 0))
                            if kt == KT - 1 and s == 7:
                                l_done = ml
                if half == 0:
                    mrec = S.to("vector", lambda e: e.reciprocal(lrow[:, :], pl[0:1, :]),
                                waits=[l_done], inc=1)
                    mrec_prev = mrec
                    mcv = S.to("vector", lambda e: e.tensor_copy(lrowb[:, :], lrow[:, :]),
                               inc=1)
                    w_lb = [mcv, ex_marks[(qg, 63)]]
                    mlb = S.to("tensor", lambda e: e.matmul(
                        pst1[:, :], onesr[:, :], lrowb[:, :], start=True, stop=True),
                        waits=w_lb, inc=1)
                    mlbe = S.to("vector", lambda e: e.tensor_copy(lb[:, :], pst1[:, :]),
                                waits=[mlb], inc=1)
                for ds in range(4):
                    sl = half * 4 + ds
                    w_zn = [pv_done, mlbe]
                    if (qg - 1, "tr", sl) in zn_marks:
                        w_zn.append(zn_marks[(qg - 1, "tr", sl)])
                    zn_marks[(qg, half, ds)] = S.to(
                        "vector", lambda e, sl=sl, ds=ds: e.tensor_tensor(
                            zfm[:, sl, :], pz[ds][:, :], lb[:, :], ALU.mult),
                        waits=w_zn, inc=1)
            phase_end = [mlbe, zn_marks[(qg, 1, 3)]]
            prev_add = None
            for qt in range(4):
                par = qt % 2
                ti = qg * 4 + qt
                w_r1 = [dmar2[ti - 2]] if (ti - 2) in dmar2 else []
                mr1 = S.dma("sync", r1_t[:, par, :], tile_view(dr["r1"], ti),
                            ch=f"r1{par}", waits=w_r1)
                for ds in range(NCH):
                    w_t = [zn_marks[(qg, ds // 4, ds % 4)]]
                    if prev_add is not None:
                        w_t.append(prev_add)
                    mt = S.to("tensor", lambda e, ds=ds, qt=qt: e.transpose(
                        ptr_[:, :], zfm[:, ds, qt * P:(qt + 1) * P], idbf_sb[:, :]),
                        waits=w_t, inc=1)
                    prev_add = S.to(
                        "vector", lambda e, ds=ds, par=par: e.scalar_tensor_tensor(
                            r2_t[:, par, ds * P:(ds + 1) * P], ptr_[:, :], 1.0,
                            r1_t[:, par, ds * P:(ds + 1) * P], ALU.mult, ALU.add),
                        waits=[mt, mr1], inc=1)
                    if qt == 3:
                        zn_marks[(qg, "tr", ds)] = prev_add
                dmar2[ti] = S.dma("sync", tile_view(dr["r2"], ti), r2_t[:, par, :],
                                  ch=f"r2{par}", waits=[prev_add])
        S.barrier()


def build_stage2():
    nc = bass.Bass("TRN2", target_bir_lowering=False)
    dr = {}
    dr["qfm"] = nc.dram_tensor("qfm", [P, NCH * TPC], BF16, kind="ExternalInput")
    dr["kfull"] = nc.dram_tensor("kfull", [SEQ, D], BF16, kind="ExternalInput")
    dr["vfull"] = nc.dram_tensor("vfull", [SEQ, D], BF16, kind="ExternalInput")
    dr["r1"] = nc.dram_tensor("r1", [TPC, D], F32, kind="ExternalInput")
    dr["knmax4"] = nc.dram_tensor("knmax4", [4, 1], F32, kind="ExternalInput")
    dr["r2"] = nc.dram_tensor("r2", [TPC, D], F32, kind="ExternalOutput")
    S = Sched(nc)
    emit_l0_attn(nc, S, dr)
    return nc



def emit_l0_ffn(nc, S, dr):
    NE = 8
    from contextlib import ExitStack
    with ExitStack() as ctx:
        def sb(name, shape, dt):
            return ctx.enter_context(nc.sbuf_tensor(name, shape, dt))

        def pt(name, shape, dt):
            return ctx.enter_context(nc.psum_tensor(name, shape, dt))

        xnfm = sb("xnfm3", [P, NCH, TPC], BF16)
        hfm = xnfm
        w1sb = sb("w1sb", [P, NCH, 2048], BF16)
        w1row = sb("w1row", [1, 2048], BF16)
        w2sb = sb("w2sb", [P, NCH, D], BF16)
        w2row = sb("w2row", [1, D], BF16)
        gsb = sb("gsb", [P, NCH, NE], BF16)
        grow = sb("grow", [1, NE], BF16)
        nwsb = sb("nwsb", [P, NCH], F32)
        outacc = sb("outacc", [P, TT, D], F32)
        acts = sb("acts", [P, NCH, 512], BF16)
        onesr = sb("onesr3", [1, TPC], BF16)
        onesc = sb("onesc3", [P, 1], BF16)
        rrow = sb("rrow", [1, 512], F32)
        rrowb = sb("rrowb", [1, 512], BF16)
        rrb = sb("rrb", [P, TPC], BF16)
        xnsq = sb("xnsq", [P, 2, 512], BF16)
        lg = sb("lg", [P, 8], F32)
        stt = sb("stt", [P, 8], F32)
        wsel = sb("wsel", [P, TT, NE], F32)
        swg = sb("swg", [P, 3, 512], F32)
        r2_t = sb("r2_t3", [P, 2, D], F32)
        x1_t = r2_t
        ceps = sb("ceps3", [P, 1], F32)
        pml = [pt(f"pml{i}", [P, 512], F32) for i in range(4)]
        pg = pt("pg", [P, NE], F32)
        pn = pt("pn", [1, 512], F32)

        S.dma("sync", xnfm[:, :, :],
              dr["xnfm"].ap().rearrange("p (c t) -> p c t", c=NCH))
        S.dma("gpsimd", gsb[:, :, :],
              dr["gateT"].ap()[0:D, :].rearrange("(c p) n -> p c n", p=P))
        S.dma("gpsimd", grow[:, :], dr["gateT"].ap()[D:D + 1, :])
        for c in range(NCH):
            S.dma("gpsimd", nwsb[:, c:c + 1],
                  dr["normw"].ap().rearrange("(c p one) -> c p one", c=NCH, one=1)[c])
        S.to("vector", lambda e: e.memset(onesr[:, :], 1.0))
        S.to("vector", lambda e: e.memset(onesc[:, :], 1.0))
        S.to("vector", lambda e: e.memset(ceps[:, :], EPS_RMS))
        S.barrier()

        sqm = {}
        rr_done = {}
        for tg in range(NQG):
            for c in range(NCH):
                par = c % 2
                w_sq = []
                if (tg, c - 2) in sqm:
                    w_sq.append(sqm[(tg, c - 2)])
                elif (tg - 1, c + NCH - 2) in sqm:
                    w_sq.append(sqm[(tg - 1, c + NCH - 2)])
                msq = S.to("scalar", lambda e, tg=tg, c=c, par=par: e.activation(
                    xnsq[:, par, :], xnfm[:, c, tg * 512:(tg + 1) * 512], AF.Square),
                    waits=w_sq, inc=1)
                sqm[(tg, c)] = S.to("tensor", lambda e, par=par, c=c: e.matmul(
                    pn[:, :], onesc[:, :], xnsq[:, par, :],
                    start=(c == 0), stop=(c == NCH - 1)), waits=[msq], inc=1)
            msqr = S.to("scalar", lambda e: e.activation(
                rrow[:, :], pn[0:1, :], AF.Sqrt, scale=1.0 / D, bias=ceps[0:1, 0:1]),
                waits=[sqm[(tg, NCH - 1)]], inc=1)
            sqm[(tg, NCH)] = msqr
            mrc = S.to("vector", lambda e: e.reciprocal(rrow[:, :], rrow[:, :]),
                       waits=[msqr], inc=1)
            mcv = S.to("vector", lambda e: e.tensor_copy(rrowb[:, :], rrow[:, :]), inc=1)
            mmb = S.to("tensor", lambda e: e.matmul(
                pml[0][:, :], onesr[:, 0:P], rrowb[:, :], start=True, stop=True),
                waits=[mcv], inc=1)
            rr_done[tg] = S.to("vector", lambda e, tg=tg: e.tensor_copy(
                rrb[:, tg * 512:(tg + 1) * 512], pml[0][:, :]), waits=[mmb], inc=1)
        for qg in range(1, NQG):
            pass
        S.barrier()

        hm = {}
        for c in range(NCH):
            m1 = S.to("vector", lambda e, c=c: e.tensor_scalar(
                hfm[:, c, :], xnfm[:, c, :], nwsb[:, c:c + 1], None, ALU.mult), inc=1)
            hm[c] = S.to("vector", lambda e, c=c: e.tensor_tensor(
                hfm[:, c, :], hfm[:, c, :], rrb[:, :], ALU.mult), inc=1)
        S.barrier()

        m1_ = stt[:, 0:1]; m2_ = stt[:, 1:2]; dd = stt[:, 2:3]
        w1_ = stt[:, 3:4]; w2_ = stt[:, 4:5]
        msk1 = swg[:, 0, 0:NE]; msk2 = swg[:, 1, 0:NE]; lg2 = swg[:, 2, 0:NE]
        prev_ev = None
        for tt in range(TT):
            w_mm = [prev_ev] if prev_ev else []
            for c in range(NCH):
                S.to("tensor", lambda e, c=c, tt=tt: e.matmul(
                    pg[:, :], hfm[:, c, tt * P:(tt + 1) * P], gsb[:, c, :],
                    start=(c == 0), stop=False), waits=(w_mm if c == 0 else []))
            mgl = S.to("tensor", lambda e, tt=tt: e.matmul(
                pg[:, :], onesr[:, tt * P:(tt + 1) * P], grow[:, :],
                start=False, stop=True), inc=1)
            mev = S.to("vector", lambda e: e.tensor_copy(lg[:, :], pg[:, :]),
                       waits=[mgl], inc=1)
            prev_ev = mev
            S.to("vector", lambda e: e.reduce_max(m1_, lg[:, :], axis=AX.X))
            S.to("vector", lambda e: e.tensor_scalar(msk1, lg[:, :], m1_, None, ALU.is_ge))
            S.to("vector", lambda e: e.scalar_tensor_tensor(
                lg2, msk1, -1e30, lg[:, :], ALU.mult, ALU.add))
            S.to("vector", lambda e: e.reduce_max(m2_, lg2, axis=AX.X))
            S.to("vector", lambda e: e.tensor_scalar(msk2, lg2, m2_, None, ALU.is_ge))
            mdd = S.to("vector", lambda e: e.tensor_scalar(dd, m2_, m1_, None,
                                                           ALU.subtract), inc=1)
            msg = S.to("scalar", lambda e: e.activation(w1_, dd, AF.Sigmoid, scale=-1.0),
                       waits=[mdd], inc=1)
            S.to("vector", lambda e: e.tensor_scalar(w2_, w1_, -1.0, 1.0,
                                                     ALU.mult, ALU.add), waits=[msg])
            S.to("vector", lambda e: e.tensor_scalar(msk1, msk1, w1_, None, ALU.mult))
            S.to("vector", lambda e: e.tensor_scalar(msk2, msk2, w2_, None, ALU.mult))
            S.to("vector", lambda e, tt=tt: e.tensor_tensor(
                wsel[:, tt, :], msk1, msk2, ALU.add))
            S.to("vector", lambda e, tt=tt: e.memset(outacc[:, tt, :], 0.0))
        S.barrier()

        LIM = 7.0
        mlp2_done = None
        mp1_prev = None
        sw_prev_mark = None
        t1_done_prev = {}
        mlp2_pe_prev = None
        for ei in range(NE):
            wd1 = dr[f"w1T_{ei}"]
            wd2 = dr[f"w2T_{ei}"]
            lwb = {}
            for j in range(NCH):
                w_lw = [t1_done_prev[j]] if j in t1_done_prev else []
                lwb[j] = S.dma("gpsimd", None, None, ch=f"w1b{j}", waits=w_lw,
                               fn=lambda e, j=j, wd1=wd1: e.dma_start(
                                   w1sb[:, :, 2 * j * P:2 * (j + 1) * P],
                                   wd1.ap()[0:D, 2 * j * P:2 * (j + 1) * P]
                                   .rearrange("(c p) n -> p c n", p=P)))
            w_lw2 = [mlp2_pe_prev] if mlp2_pe_prev else []
            lw = [S.dma("gpsimd", w1row[:, :], wd1.ap()[D:D + 1, :], ch="w1r",
                        waits=[t1_done_prev[NCH]] if NCH in t1_done_prev else []),
                  S.dma("gpsimd", w2sb[:, :, :],
                        wd2.ap()[0:D, :].rearrange("(c p) n -> p c n", p=P), ch="w2",
                        waits=w_lw2),
                  S.dma("gpsimd", w2row[:, :], wd2.ap()[D:D + 1, :], ch="w2r",
                        waits=w_lw2)]
            last_acc = {}
            t1_next = {}
            for tg in range(NQG):
                sw_done = {}
                for j in range(NCH):
                    pglu = pml[(2 * j) % 4]
                    plin = pml[(2 * j + 1) % 4]
                    w_mm = ([lwb[j], lw[0]] if tg == 0 else [])
                    if (j - 2) in sw_done:
                        w_mm.append(sw_done[j - 2])
                    elif j == 0 and mlp2_done is not None:
                        w_mm.append(mlp2_done)
                    elif j == 1 and (tg > 0 or ei > 0):
                        w_mm.append(prev_sw[NCH - 1])
                    for fi, ps in ((2 * j, pglu), (2 * j + 1, plin)):
                        for c in range(NCH):
                            S.to("tensor", lambda e, c=c, fi=fi, tg=tg, ps=ps: e.matmul(
                                ps[:, :], w1sb[:, c, fi * P:(fi + 1) * P],
                                hfm[:, c, tg * 512:(tg + 1) * 512],
                                start=(c == 0), stop=False),
                                waits=(w_mm if c == 0 else []))
                        S.to("tensor", lambda e, fi=fi, tg=tg, ps=ps: e.matmul(
                            ps[:, :], w1row[:, fi * P:(fi + 1) * P],
                            onesr[:, tg * 512:(tg + 1) * 512],
                            start=False, stop=True),
                            inc=(1 if fi % 2 == 1 else 0))
                    mboth = S.count["tensor"]
                    mt1 = (S.sem["tensor"], mboth)
                    if tg == NQG - 1:
                        t1_next[j] = mt1
                    w_gm = [mt1]
                    if mp1_prev is not None:
                        w_gm.append(mp1_prev)
                    mgm = S.to("vector", lambda e, pglu=pglu: e.tensor_scalar(
                        swg[:, 0, :], pglu[:, :], LIM, None, ALU.min),
                        waits=w_gm, inc=1)
                    w_p1 = [mgm]
                    if sw_prev_mark is not None:
                        w_p1.append(sw_prev_mark)
                    mp1 = S.to("scalar", lambda e: e.activation(
                        swg[:, 1, :], swg[:, 0, :], AF.Gelu_apprx_sigmoid),
                        waits=w_p1, inc=1)
                    mp1_prev = mp1
                    w_lc = [mt1]
                    if sw_prev_mark is not None:
                        w_lc.append(sw_prev_mark)
                    mlc = S.to("vector", lambda e, plin=plin: e.tensor_scalar(
                        swg[:, 2, :], plin[:, :], LIM, -LIM, ALU.min, ALU.max),
                        waits=w_lc, inc=1)
                    w_act = [mp1, mlc]
                    if (j, tg, ei) != (0, 0, 0):
                        w_act.append(mlp2_done if j == 0 else sw_prev_mark)
                    sw_done[j] = S.to("vector", lambda e, j=j: e.scalar_tensor_tensor(
                        acts[:, j, :], swg[:, 2, :], 1.0, swg[:, 1, :],
                        ALU.add, ALU.mult), waits=w_act, inc=1)
                    sw_prev_mark = sw_done[j]
                prev_sw = sw_done
                for tq in range(4):
                    tt = tg * 4 + tq
                    for half in range(2):
                        ps = pml[half]
                        w_mm = [sw_done[NCH - 1]]
                        if tg == 0 and tq == 0 and half == 0:
                            w_mm += [lw[1], lw[2]]
                        if (tt - 1, half) in last_acc:
                            w_mm.append(last_acc[(tt - 1, half)])
                        for c in range(NCH):
                            S.to("tensor", lambda e, c=c, tq=tq, half=half, ps=ps:
                                 e.matmul(ps[:, :], acts[:, c, tq * P:(tq + 1) * P],
                                          w2sb[:, c, half * 512:(half + 1) * 512],
                                          start=(c == 0), stop=False),
                                 waits=(w_mm if c == 0 else []))
                        my2 = S.to("tensor", lambda e, tt=tt, half=half, ps=ps: e.matmul(
                            ps[:, :], onesr[:, tt * P:(tt + 1) * P],
                            w2row[:, half * 512:(half + 1) * 512],
                            start=False, stop=True), inc=1)
                        last_acc[(tt, half)] = S.to(
                            "vector", lambda e, tt=tt, half=half, ei=ei, ps=ps:
                            e.scalar_tensor_tensor(
                                outacc[:, tt, half * 512:(half + 1) * 512], ps[:, :],
                                wsel[:, tt, ei:ei + 1],
                                outacc[:, tt, half * 512:(half + 1) * 512],
                                ALU.mult, ALU.add), waits=[my2], inc=1)
                mlp2_done = last_acc[(tg * 4 + 3, 1)]
                if tg == NQG - 1:
                    mlp2_pe_prev = my2
            t1_next[NCH] = t1_next[NCH - 1]
            t1_done_prev = t1_next
        S.barrier()

        dmx = {}
        for tt in range(TT):
            par = tt % 2
            w_r2 = [dmx[tt - 2]] if (tt - 2) in dmx else []
            mr2 = S.dma("sync", r2_t[:, par, :], tile_view(dr["r2"], tt),
                        ch=f"r2l{par}", waits=w_r2)
            ma = S.to("vector", lambda e, tt=tt, par=par: e.tensor_add(
                r2_t[:, par, :], r2_t[:, par, :], outacc[:, tt, :]),
                waits=[mr2], inc=1)
            dmx[tt] = S.dma("sync", tile_view(dr["x1"], tt), x1_t[:, par, :],
                            ch=f"x1{par}", waits=[ma])
        S.barrier()


def build_stage3():
    nc = bass.Bass("TRN2", target_bir_lowering=False)
    dr = {}
    dr["xnfm"] = nc.dram_tensor("xnfm", [P, NCH * TPC], BF16, kind="ExternalInput")
    dr["r2"] = nc.dram_tensor("r2", [TPC, D], F32, kind="ExternalInput")
    dr["normw"] = nc.dram_tensor("normw", [D], F32, kind="ExternalInput")
    dr["gateT"] = nc.dram_tensor("gateT", [D + 1, 8], F32, kind="ExternalInput")
    for ei in range(8):
        dr[f"w1T_{ei}"] = nc.dram_tensor(f"w1T_{ei}", [D + 1, 2048], F32,
                                         kind="ExternalInput")
        dr[f"w2T_{ei}"] = nc.dram_tensor(f"w2T_{ei}", [D + 1, D], F32,
                                         kind="ExternalInput")
    dr["x1"] = nc.dram_tensor("x1", [TPC, D], F32, kind="ExternalOutput")
    S = Sched(nc)
    emit_l0_ffn(nc, S, dr)
    return nc



def emit_l1_front(nc, S, dr):
    idbf = nc.inline_tensor(np.eye(P, dtype=ml_dtypes.bfloat16), "idbf4")
    id32 = nc.inline_tensor(np.eye(P, dtype=np.float32), "id325")
    from contextlib import ExitStack
    with ExitStack() as ctx:
        def sb(name, shape, dt):
            return ctx.enter_context(nc.sbuf_tensor(name, shape, dt))

        def pt(name, shape, dt):
            return ctx.enter_context(nc.psum_tensor(name, shape, dt))

        xnfm = sb("xnfm4", [P, NCH, TPC], BF16)
        cosf = sb("cosf_sb", [P, 4, TPC], BF16)
        sinf = sb("sinf_sb", [P, 4, TPC], BF16)
        w1 = sb("w14", [P, NCH, D], BF16)
        w2 = sb("w24", [P, NCH, D], BF16)
        w2r = sb("w2r4", [1, D], BF16)
        ones_bf = sb("ones4", [1, TPC], BF16)
        idbf_sb = sb("idbf4_sb", [P, P], BF16)
        st = sb("st4", [P, 8], F32)
        stage = sb("stage4", [P, 2, TPC], BF16)
        kvstage = sb("kvst4", [P, 2, D], BF16)
        khsq = sb("khsq4", [P, D], F32)
        knh = sb("knh4", [P, 16], F32)
        knh2 = sb("knh24", [P, 16], F32)
        knfin = sb("knfin4", [16, 1], F32)
        ceps = sb("ceps4", [P, 1], F32)
        id32_sb = sb("id324", [P, P], F32)
        inner = ExitStack()
        x1_t = inner.enter_context(nc.sbuf_tensor("x1t4", [P, 2, D], F32))
        xn_t = inner.enter_context(nc.sbuf_tensor("xnt4", [P, 2, D], F32))
        xnb = inner.enter_context(nc.sbuf_tensor("xnb4", [P, 2, D], BF16))
        xsq = inner.enter_context(nc.sbuf_tensor("xsq4", [P, D], F32))
        pmm = [pt(f"pmm4{i}", [P, 512], F32) for i in range(4)]
        ptrb = [pt(f"ptrb4{i}", [P, P], BF16) for i in range(2)]
        ptr32 = pt("ptr324", [16, P], F32)

        S.dma("gpsimd", cosf[:, :, :],
              dr["cosf"].ap().rearrange("(c p) t -> p c t", p=P))
        S.dma("gpsimd", sinf[:, :, :],
              dr["sinf"].ap().rearrange("(c p) t -> p c t", p=P))
        S.dma("sync", idbf_sb[:, :], idbf[:, :])
        S.dma("sync", id32_sb[:, :], id32[:, :])
        S.to("vector", lambda e: e.memset(ones_bf[:, :], 1.0))
        S.to("vector", lambda e: e.memset(ceps[:, :], EPS_LN))
        S.to("vector", lambda e: e.memset(knh[:, :], 0.0))
        S.barrier()

        dma_xn = {}
        evac_marks = {}
        tpose_marks = {}
        s1 = st[:, 0:1]; s2 = st[:, 1:2]; m_ = st[:, 2:3]; ms = st[:, 3:4]
        var = st[:, 4:5]; std = st[:, 5:6]; rstd = st[:, 6:7]; nb = st[:, 7:8]
        ldx = {}
        for i in range(TT):
            par = i % 2
            w_ld = [tpose_marks[(i - 2, NCH - 1)]] if (i - 2, NCH - 1) in tpose_marks else []
            ldx[i] = S.dma("sync", x1_t[:, par, :], tile_view(dr["x1"], i),
                           ch=f"x1l{par}", waits=w_ld)
            x_t = x1_t[:, par, :]
            S.to("vector", lambda e, x_t=x_t: e.reduce_sum(s1, x_t, axis=AX.X),
                 waits=[ldx[i]])
            msq = S.to("scalar", lambda e, x_t=x_t: e.activation(
                xsq[:, :], x_t, AF.Square, accum_out=s2), waits=[ldx[i]], inc=1)
            S.to("vector", lambda e: e.tensor_scalar(m_, s1, 1.0 / D, None, ALU.mult))
            S.to("vector", lambda e: e.tensor_scalar(ms, s2, 1.0 / D, None, ALU.mult),
                 waits=[msq])
            S.to("vector", lambda e: e.tensor_scalar(s1, m_, m_, None, ALU.mult))
            mvar = S.to("vector", lambda e: e.tensor_scalar(var, ms, s1, None,
                                                            ALU.subtract), inc=1)
            mstd = S.to("scalar", lambda e: e.activation(std, var, AF.Sqrt,
                                                         bias=ceps[:, 0:1]),
                        waits=[mvar], inc=1)
            S.to("vector", lambda e: e.reciprocal(rstd, std), waits=[mstd])
            mnb = S.to("vector", lambda e: e.tensor_scalar(nb, m_, rstd, -1.0,
                                                           ALU.mult, ALU.mult), inc=1)
            mxn = S.to("scalar", lambda e, x_t=x_t, par=par: e.activation(
                xn_t[:, par, :], x_t, AF.Identity, bias=nb, scale=rstd),
                waits=[mnb], inc=1)
            w_xb = [mxn]
            if (i - 2, NCH - 1) in tpose_marks:
                w_xb.append(tpose_marks[(i - 2, NCH - 1)])
            mxb = S.to("vector", lambda e, par=par: e.tensor_copy(
                xnb[:, par, :], xn_t[:, par, :]), waits=w_xb, inc=1)
            for c in range(NCH):
                pslot = c % 2
                w_t = [mxb]
                if (i, c - 2) in evac_marks:
                    w_t.append(evac_marks[(i, c - 2)])
                elif (i - 1, c + NCH - 2) in evac_marks:
                    w_t.append(evac_marks[(i - 1, c + NCH - 2)])
                mt = S.to("tensor", lambda e, par=par, c=c, pslot=pslot: e.transpose(
                    ptrb[pslot][:, :], xnb[:, par, c * P:(c + 1) * P], idbf_sb[:, :]),
                    waits=w_t, inc=1)
                tpose_marks[(i, c)] = mt
                evac_marks[(i, c)] = S.to("vector", lambda e, i=i, c=c, pslot=pslot:
                    e.tensor_copy(xnfm[:, c, i * P:(i + 1) * P], ptrb[pslot][:, :]),
                    waits=[mt], inc=1)
        S.dma("gpsimd", dr["x1nfm"].ap().rearrange("p (c t) -> p c t", c=NCH),
              xnfm[:, :, :], waits=[evac_marks[(TT - 1, NCH - 1)]])
        S.barrier()
        inner.close()
        yfm = ctx.enter_context(nc.sbuf_tensor("yfm", [P, NCH, TPC], BF16))
        pfm = ctx.enter_context(nc.sbuf_tensor("pfm", [P, NCH, TPC], BF16))
        tmp = ctx.enter_context(nc.sbuf_tensor("tmp4", [P, TPC], BF16))

        prev_consumed = None
        prev_chain_guards = {0: [], 1: []}
        for wi, (w1n, w2n) in enumerate((("wqT", "wqiT"), ("wkT", "wkiT"),
                                         ("wvT", "wviT"))):
            w_lw = [prev_consumed] if prev_consumed is not None else []
            lw = [S.dma("gpsimd", w1[:, :, :],
                        dr[w1n].ap().rearrange("(c p) n -> p c n", p=P), ch="w14c",
                        waits=w_lw),
                  S.dma("gpsimd", w2[:, :, :],
                        dr[w2n].ap()[0:D, :].rearrange("(c p) n -> p c n", p=P),
                        ch="w24c", waits=w_lw),
                  S.dma("gpsimd", w2r[:, :], dr[w2n].ap()[D:D + 1, :], ch="w24r",
                        waits=w_lw)]
            ev1 = {}
            unit = 0
            for oc in range(NCH):
                for tg in range(NQG):
                    ps = pmm[unit % 4]
                    w_mm = list(lw)
                    if unit >= 4:
                        w_mm.append(ev1[unit - 4])
                    if prev_consumed is not None:
                        w_mm.append(prev_consumed)
                    for c in range(NCH):
                        S.to("tensor", lambda e, oc=oc, tg=tg, c=c, ps=ps: e.matmul(
                            ps[:, :], w1[:, c, oc * P:(oc + 1) * P],
                            xnfm[:, c, tg * 512:(tg + 1) * 512],
                            start=(c == 0), stop=(c == NCH - 1)),
                            waits=(w_mm if c == 0 else []), inc=(1 if c == NCH - 1 else 0))
                    mlast = (S.sem["tensor"], S.count["tensor"])
                    ev1[unit] = S.to("vector", lambda e, oc=oc, tg=tg, ps=ps:
                        e.tensor_copy(pfm[:, oc, tg * 512:(tg + 1) * 512], ps[:, :]),
                        waits=[mlast], inc=1)
                    unit += 1
            prev_consumed = None
            rope_m = {}
            for j in range(4):
                pe_ = pfm[:, j, :]
                po_ = pfm[:, j + 4, :]
                cj = cosf[:, j, :]
                sj = sinf[:, j, :]
                S.to("vector", lambda e, pe_=pe_, cj=cj: e.tensor_tensor(
                    tmp[:, :], pe_, cj, ALU.mult), waits=([ev1[31]] if j == 0 else []))
                S.to("vector", lambda e, j=j, po_=po_, sj=sj: e.tensor_tensor(
                    yfm[:, j, :], po_, sj, ALU.mult))
                S.to("vector", lambda e, j=j: e.tensor_tensor(
                    yfm[:, j, :], yfm[:, j, :], tmp[:, :], ALU.add))
                S.to("vector", lambda e, po_=po_, cj=cj: e.tensor_tensor(
                    tmp[:, :], po_, cj, ALU.mult))
                S.to("vector", lambda e, j=j, pe_=pe_, sj=sj: e.tensor_tensor(
                    yfm[:, j + 4, :], pe_, sj, ALU.mult))
                rope_m[j] = S.to("vector", lambda e, j=j: e.tensor_tensor(
                    yfm[:, j + 4, :], tmp[:, :], yfm[:, j + 4, :], ALU.subtract),
                    inc=1)
            rope_done = rope_m[3]
            if wi == 0:
                ev2 = {}
                unit = 0
                dmaq = {}
                for oc in range(NCH):
                    for tg in range(NQG):
                        ps = pmm[unit % 4]
                        w_mm = [rope_done]
                        if unit >= 4:
                            w_mm.append(ev2[unit - 4])
                        for c in range(NCH):
                            S.to("tensor", lambda e, oc=oc, tg=tg, c=c, ps=ps: e.matmul(
                                ps[:, :], w2[:, c, oc * P:(oc + 1) * P],
                                yfm[:, c, tg * 512:(tg + 1) * 512],
                                start=(c == 0), stop=False),
                                waits=(w_mm if c == 0 else []))
                        mlast = S.to("tensor", lambda e, oc=oc, tg=tg, ps=ps: e.matmul(
                            ps[:, :], w2r[:, oc * P:(oc + 1) * P],
                            ones_bf[:, tg * 512:(tg + 1) * 512],
                            start=False, stop=True), inc=1)
                        w_ev = [mlast]
                        if oc >= 2 and tg == 0:
                            w_ev.append(dmaq[oc - 2])
                        ev2[unit] = S.to("vector", lambda e, oc=oc, tg=tg, ps=ps:
                            e.tensor_copy(stage[:, oc % 2, tg * 512:(tg + 1) * 512],
                                          ps[:, :]), waits=w_ev, inc=1)
                        unit += 1
                    dmaq[oc] = S.dma(
                        "sync",
                        dr["qhfm"].ap().rearrange("p (c t) -> p c t", c=NCH)[:, oc, :],
                        stage[:, oc % 2, :], waits=[ev2[unit - 1]], ch=f"q4{oc % 2}")
                prev_consumed = ev2[unit - 1]
            else:
                dname = "khtm" if wi == 1 else "vhtm"
                ev2 = {}
                dmakv = {}
                knmarks = {}
                kv_unit = 0
                for tt in range(TT):
                    par = tt % 2
                    for half in range(2):
                        ps = pmm[kv_unit % 4]
                        w_mm = [rope_done]
                        if kv_unit >= 4:
                            w_mm.append(ev2[kv_unit - 4])
                        if kv_unit // 2 >= 2 and half == 0:
                            w_mm.append(dmakv[kv_unit // 2 - 2])
                        elif kv_unit // 2 < 2 and half == 0:
                            w_mm += [mk for mk in prev_chain_guards[tt] if mk]
                        for c in range(NCH):
                            S.to("tensor", lambda e, tt=tt, half=half, c=c, ps=ps:
                                 e.matmul(ps[:, :], yfm[:, c, tt * P:(tt + 1) * P],
                                          w2[:, c, half * 512:(half + 1) * 512],
                                          start=(c == 0), stop=False),
                                 waits=(w_mm if c == 0 else []))
                        mlast = S.to("tensor", lambda e, tt=tt, half=half, ps=ps:
                                     e.matmul(ps[:, :], ones_bf[:, tt * P:(tt + 1) * P],
                                              w2r[:, half * 512:(half + 1) * 512],
                                              start=False, stop=True), inc=1)
                        w_ev = [mlast]
                        tvg = kv_unit // 2
                        if half == 0 and (tvg - 2) in knmarks:
                            w_ev.append(knmarks[tvg - 2])
                        ev2[kv_unit] = S.to("vector",
                            lambda e, par=par, half=half, ps=ps: e.tensor_copy(
                                kvstage[:, par, half * 512:(half + 1) * 512], ps[:, :]),
                            waits=w_ev, inc=1)
                        kv_unit += 1
                    dmakv[kv_unit // 2 - 1] = S.dma(
                        "sync", tile_view(dr[dname], tt), kvstage[:, par, :],
                        waits=[ev2[kv_unit - 1]], ch=f"kv4{par}")
                    if wi == 1:
                        mk1 = S.to("scalar", lambda e, par=par: e.activation(
                            khsq[:, :], kvstage[:, par, :], AF.Square),
                            waits=[ev2[kv_unit - 1]], inc=1)
                        mk2 = S.to("vector", lambda e: e.tensor_reduce(
                            knh2[:, :],
                            khsq.ap().rearrange("p (h f) -> p h f", h=16),
                            AX.X, ALU.add), waits=[mk1], inc=1)
                        knmarks[tt] = S.to("vector", lambda e: e.tensor_tensor(
                            knh[:, :], knh[:, :], knh2[:, :], ALU.max),
                            waits=[mk2], inc=1)
                prev_consumed = ev2[kv_unit - 1]
                prev_chain_guards = {0: [dmakv[TT - 2], knmarks.get(TT - 2)],
                                     1: [dmakv[TT - 1], knmarks.get(TT - 1)]}
                if wi == 1:
                    mtr = S.to("tensor", lambda e: e.transpose(
                        ptr32[:, :], knh[:, :], id32_sb[:, :]),
                        waits=[knmarks[TT - 1]], inc=1)
                    mrd = S.to("vector", lambda e: e.reduce_max(
                        knfin[:, 0:1], ptr32[0:16, :], axis=AX.X),
                        waits=[mtr], inc=1)
                    msq = S.to("scalar", lambda e: e.activation(
                        knfin[:, 0:1], knfin[:, 0:1], AF.Sqrt), waits=[mrd], inc=1)
                    S.dma("sync", dr["khn"].ap(), knfin[:, 0:1], waits=[msq])
        S.barrier()


def build_stage4():
    nc = bass.Bass("TRN2", target_bir_lowering=False)
    dr = {}
    dr["x1"] = nc.dram_tensor("x1", [TPC, D], F32, kind="ExternalInput")
    dr["cosf"] = nc.dram_tensor("cosf", [512, TPC], F32, kind="ExternalInput")
    dr["sinf"] = nc.dram_tensor("sinf", [512, TPC], F32, kind="ExternalInput")
    for nm in ("wqT", "wkT", "wvT"):
        dr[nm] = nc.dram_tensor(nm, [D, D], F32, kind="ExternalInput")
    for nm in ("wqiT", "wkiT", "wviT"):
        dr[nm] = nc.dram_tensor(nm, [D + 1, D], F32, kind="ExternalInput")
    dr["x1nfm"] = nc.dram_tensor("x1nfm", [P, NCH * TPC], BF16, kind="ExternalOutput")
    dr["qhfm"] = nc.dram_tensor("qhfm", [P, NCH * TPC], BF16, kind="ExternalOutput")
    dr["khtm"] = nc.dram_tensor("khtm", [TPC, D], BF16, kind="ExternalOutput")
    dr["vhtm"] = nc.dram_tensor("vhtm", [TPC, D], BF16, kind="ExternalOutput")
    dr["khn"] = nc.dram_tensor("khn", [16, 1], F32, kind="ExternalOutput")
    S = Sched(nc)
    emit_l1_front(nc, S, dr)
    return nc



def emit_l1_attn(nc, S, dr):
    mask_np = np.zeros((P, 4, 512), dtype=ml_dtypes.bfloat16)
    for dsv in range(4):
        for iv in range(P):
            mask_np[iv, dsv, dsv * P + iv:] = 1.0
    maskc = nc.inline_tensor(np.ascontiguousarray(mask_np.reshape(P, 4 * 512)),
                             "maskc5")
    NKT = SEQ // P
    from contextlib import ExitStack
    with ExitStack() as ctx:
        def sb(name, shape, dt):
            return ctx.enter_context(nc.sbuf_tensor(name, shape, dt))

        qhfm = sbm("qhfm5", [P, NCH, TPC], BF16)
        khp = sbm("khp5", [P, SEQ], BF16)
        vhs = sbm("vhs5", [P, NKT, 130], BF16)
        pts = sbm("pts5", [P, 2, 2, 512], BF16)
        ofm = sb("ofm5", [P, NCH, TPC], BF16)
        masksb = sbm("masksb5", [P, 4, 512], BF16)
        onesr = sb("onesr5", [1, P], BF16)
        ind = sb("ind5", [P, 2], BF16)
        qsq = sbm("qsq5", [P, TPC], BF16)
        btab = sb("btab5", [1, 8, NQG, NKT], F32)
        btabb = sb("btabb5", [1, 8, NQG, NKT], BF16)
        biast = sb("biast5", [P, NQG, NKT], F32)
        rows5 = sb("rows5", [1, 16], F32)
        kmaxh = sb("kmaxh5", [1, 16], F32)
        khn_in = sb("khn_in5", [1, 64], F32)
        kill_in = sb("kill_in5", [1, NQG, NKT], F32)
        lrowb = sb("lrowb5", [1, 512], BF16)
        lb_sb = sb("lb5", [64, 512], F32)
        ow = sb("ow5", [P, NCH, D], BF16)
        owr = sb("owr5", [1, D], BF16)
        x1_t = sb("x1t5", [P, 2, D], F32)
        r3_t = sb("r3t5", [P, 2, D], F32)
        ones2k = sb("ones2k5", [1, TPC], BF16)

        S.dma("sync", qhfm[:, :, :],
              dr["qhfm"].ap().rearrange("p (c t) -> p c t", c=NCH))
        S.dma("sync", masksb[:, :, :],
              maskc.ap().rearrange("p (d f) -> p d f", d=4))
        if agnh is not None:
            S.to("sync", lambda e: e.wait_ge(agnh[0], agnh[1]))
        S.dma("sync", khn_in[:, :], dr["khn4"].ap().rearrange("r c -> (r c)").unsqueeze(0))
        S.dma("sync", kill_in[:, :, :], dr["killb"].ap().unsqueeze(0))
        S.dma("gpsimd", ow[:, :, :],
              dr["owT"].ap()[0:D, :].rearrange("(c p) n -> p c n", p=P))
        S.dma("gpsimd", owr[:, :], dr["owT"].ap()[D:D + 1, :])
        S.to("vector", lambda e: e.memset(onesr[:, :], 1.0))
        S.to("vector", lambda e: e.memset(ones2k[:, :], 1.0))
        S.to("vector", lambda e: e.memset(ind[:, :], 0.0))
        S.to("vector", lambda e: e.memset(ind[0:64, 0:1], 1.0))
        S.to("vector", lambda e: e.memset(ind[64:128, 1:2], 1.0))
        S.barrier()
        rotk = emit_rotate(nc, S, dr["khfull"], dr["khrot"], [agkh])
        rotv = emit_rotate(nc, S, dr["vhfull"], dr["vhrot"], [agvh])

        S.to("vector", lambda e: e.tensor_max(kmaxh[:, :], khn_in[:, 0:16],
                                              khn_in[:, 16:32]))
        S.to("vector", lambda e: e.tensor_max(kmaxh[:, :], kmaxh[:, :],
                                              khn_in[:, 32:48]))
        S.to("vector", lambda e: e.tensor_max(kmaxh[:, :], kmaxh[:, :],
                                              khn_in[:, 48:64]))

        inner = ExitStack()
        pn = inner.enter_context(nc.psum_tensor("pn5", [1, 512], F32))
        sq_m = {}
        for hp in range(NCH):
            msq = S.to("scalar", lambda e, hp=hp: e.activation(
                qsq, qhfm[:, hp, :], AF.Square), inc=1)
            for qg in range(NQG):
                vmax = rows5[0:1, 0:1]
                vsq = rows5[0:1, 1:2]
                vpair = rows5[0:1, 2:3]
                for h2 in range(2):
                    w_mm = [msq]
                    if sq_m:
                        w_mm.append(last_rd)
                    mn = S.to("tensor", lambda e, hp=hp, qg=qg, h2=h2: e.matmul(
                        pn[:, :], ind[:, h2:h2 + 1],
                        qsq[:, qg * 512:(qg + 1) * 512],
                        start=True, stop=True), waits=w_mm, inc=1)
                    last_rd = S.to("vector", lambda e: e.reduce_max(
                        vmax, pn[0:1, :], axis=AX.X), waits=[mn], inc=1)
                    sq_m[(hp, qg, h2)] = last_rd
                    msqr = S.to("scalar", lambda e: e.activation(
                        vsq, vmax, AF.Sqrt), waits=[last_rd], inc=1)
                    if h2 == 0:
                        S.to("vector", lambda e, hp=hp: e.tensor_scalar(
                            vpair, vsq, kmaxh[0:1, 2 * hp:2 * hp + 1], None,
                            ALU.mult), waits=[msqr])
                    else:
                        S.to("vector", lambda e, hp=hp: e.tensor_scalar(
                            vsq, vsq, kmaxh[0:1, 2 * hp + 1:2 * hp + 2], None,
                            ALU.mult), waits=[msqr])
                        S.to("vector", lambda e: e.tensor_max(vpair, vpair, vsq))
                mng = S.to("vector", lambda e: e.tensor_scalar(
                    vpair, vpair, -1.002 / 8.0, None, ALU.mult), inc=1)
                S.to("vector", lambda e, hp=hp, qg=qg: e.tensor_scalar(
                    btab[0:1, hp, qg, :], kill_in[0:1, qg, :], vpair, None,
                    ALU.add), waits=[mng])
        mcv = S.to("vector", lambda e: e.tensor_copy(btabb[:, :, :, :],
                                                     btab[:, :, :, :]), inc=1)
        S.barrier()
        inner.close()

        inner2 = ExitStack()
        pst = [inner2.enter_context(nc.psum_tensor(f"pst5{i}", [P, 2, 512], F32))
               for i in range(2)]
        pz = [inner2.enter_context(nc.psum_tensor(f"pz5{i}", [65, 512], F32))
              for i in range(2)]
        pb = inner2.enter_context(nc.psum_tensor("pb5", [64, 512], F32))
        pbias = inner2.enter_context(nc.psum_tensor("pbias5", [P, 256], F32))

        ex_marks = {}
        pv_marks = {}
        o_marks = {}
        lb_m = None
        last_hp_pe = None
        for hp in range(NCH):
            w_bb = [mcv]
            if lb_m is not None:
                w_bb.append(bias_rd)
            mbb = S.to("tensor", lambda e, hp=hp: e.matmul(
                pbias[:, :], onesr[:, :], btabb[0:1, hp, :, :], start=True,
                stop=True), waits=w_bb, inc=1)
            bias_ev = S.to("vector", lambda e: e.tensor_copy(
                biast.ap().rearrange("p q k -> p (q k)"), pbias[:, :]),
                waits=[mbb], inc=1)
            bias_rd = bias_ev
            w_kh = [last_hp_pe] if last_hp_pe else []
            mkh = S.dma("sync", None, None, ch="khp", waits=w_kh,
                        fn=lambda e, hp=hp: e.dma_start(
                            khp[:, :], dr["khrot"].ap()[:, hp * P:(hp + 1) * P],
                            transpose=True))
            mkb = mkh
            w_vh = [last_hp_pe] if last_hp_pe else []
            mv1 = S.dma("gpsimd", vhs[:, :, 0:64],
                        dr["vhrot"].ap()[:, hp * P:hp * P + 64]
                        .rearrange("(s p) d -> p s d", p=P), ch="vh5a", waits=w_vh)
            mv2 = S.dma("gpsimd", vhs[:, :, 65:129],
                        dr["vhrot"].ap()[:, hp * P + 64:(hp + 1) * P]
                        .rearrange("(s p) d -> p s d", p=P), ch="vh5b", waits=w_vh)
            mo1 = S.to("vector", lambda e: e.memset(vhs[:, :, 64:65], 1.0),
                       waits=[mv1, mv2], inc=1)
            vh_ready = [mv1, mv2, mo1,
                        S.to("vector", lambda e: e.memset(vhs[:, :, 129:130], 1.0),
                             inc=1)]
            for qg in range(NQG):
                for kt in range(NKT):
                    ps = pst[kt % 2]
                    slot = kt % 2
                    w_mm = []
                    if qg == 0 and kt == 0:
                        w_mm += [mkh, mkb, bias_ev]
                    if (hp, qg, kt - 2) in ex_marks:
                        w_mm.append(ex_marks[(hp, qg, kt - 2)])
                    elif (hp, qg - 1, kt + NKT - 2) in ex_marks:
                        w_mm.append(ex_marks[(hp, qg - 1, kt + NKT - 2)])
                    elif (hp - 1, NQG - 1, kt + NKT - 2) in ex_marks:
                        w_mm.append(ex_marks[(hp - 1, NQG - 1, kt + NKT - 2)])
                    S.to("tensor", lambda e, hp=hp, qg=qg, kt=kt, ps=ps: e.matmul(
                        ps[:, 0, :], khp[0:64, kt * P:(kt + 1) * P],
                        qhfm[0:64, hp, qg * 512:(qg + 1) * 512],
                        start=True, stop=True), waits=w_mm)
                    msb = S.to("tensor", lambda e, hp=hp, qg=qg, kt=kt, ps=ps: e.matmul(
                        ps[:, 1, :], khp[64:128, kt * P:(kt + 1) * P],
                        qhfm[64:128, hp, qg * 512:(qg + 1) * 512],
                        start=True, stop=True), inc=1)
                    w_ex = [msb]
                    prev_pt = pv_marks.get((hp, qg, kt - 2))
                    if prev_pt is None and kt < 2:
                        prev_pt = pv_marks.get((hp, qg - 1, kt + NKT - 2)) if qg > 0                             else pv_marks.get((hp - 1, NQG - 1, kt + NKT - 2))
                    if prev_pt is not None:
                        w_ex.append(prev_pt)
                    mex = S.to("scalar", lambda e, qg=qg, kt=kt, ps=ps, slot=slot:
                               e.activation(pts[:, slot, :, :], ps[:, :, :], AF.Exp,
                                            scale=1.0 / 8.0,
                                            bias=biast[:, qg, kt:kt + 1]),
                               waits=w_ex, inc=1)
                    ex_marks[(hp, qg, kt)] = mex
                    pt_ready = mex
                    if qg * 4 <= kt < qg * 4 + 4:
                        dsv = kt - qg * 4
                        pt_ready = S.to("vector", lambda e, dsv=dsv, slot=slot:
                            e.tensor_tensor(pts[:, slot, :, :], pts[:, slot, :, :],
                                            masksb[:, :, dsv, :], ALU.mult),
                            waits=[mex], inc=1)
                        ex_marks[(hp, qg, kt)] = pt_ready
                    w_pv = [pt_ready] + (vh_ready if (kt == 0 and qg == 0) else [])
                    if kt == 0:
                        prev_o = o_marks.get((hp, qg - 1)) if qg > 0 else (
                            o_marks.get((hp - 1, NQG - 1)))
                        if prev_o:
                            w_pv.append(prev_o)
                    for h2 in range(2):
                        mpv = S.to("tensor", lambda e, h2=h2, kt=kt, slot=slot: e.matmul(
                            pz[h2][:, :], vhs[:, kt, 65 * h2:65 * (h2 + 1)],
                            pts[:, slot, h2, :], start=(kt == 0), stop=(kt == NKT - 1)),
                            waits=(w_pv if h2 == 0 else []),
                            inc=(1 if h2 == 1 else 0))
                    pv_marks[(hp, qg, kt)] = mpv
                zdone = pv_marks[(hp, qg, NKT - 1)]
                o_sub = []
                for h2 in range(2):
                    w_rc = [zdone]
                    if lb_m is not None:
                        w_rc.append(lb_m)
                    mrc = S.to("vector", lambda e, h2=h2: e.reciprocal(
                        lb_sb[0:1, 0:512], pz[h2][64:65, :]), waits=w_rc, inc=1)
                    mcv2 = S.to("vector", lambda e: e.tensor_copy(
                        lrowb[:, :], lb_sb[0:1, 0:512]), inc=1)
                    mlb = S.to("tensor", lambda e: e.matmul(
                        pb[:, :], onesr[:, 0:64], lrowb[:, :], start=True,
                        stop=True), waits=[mcv2], inc=1)
                    mlbe = S.to("vector", lambda e: e.tensor_copy(
                        lb_sb[:, :], pb[:, :]), waits=[mlb], inc=1)
                    lb_m = mlbe
                    mo = S.to("vector", lambda e, h2=h2, hp=hp, qg=qg: e.tensor_tensor(
                        ofm[64 * h2:64 * (h2 + 1), hp, qg * 512:(qg + 1) * 512],
                        pz[h2][0:64, :], lb_sb[0:64, :], ALU.mult),
                        waits=[mlbe], inc=1)
                    o_sub.append(mo)
                o_marks[(hp, qg)] = o_sub[-1]
            last_hp_pe = pv_marks[(hp, NQG - 1, NKT - 1)]
        S.barrier()
        inner2.close()

        po = [ctx.enter_context(nc.psum_tensor(f"po5{i}", [P, 512], F32))
              for i in range(4)]
        dmr = {}
        ev_m = {}
        unit = 0
        for tt in range(TT):
            par = tt % 2
            w_x1 = [dmr[tt - 2]] if (tt - 2) in dmr else []
            mx1 = S.dma("sync", x1_t[:, par, :], tile_view(dr["x1"], tt),
                        ch=f"x15{par}", waits=w_x1)
            for half in range(2):
                ps = po[unit % 4]
                w_mm = []
                if unit >= 4:
                    w_mm.append(ev_m[unit - 4])
                for c in range(NCH):
                    S.to("tensor", lambda e, c=c, tt=tt, half=half, ps=ps: e.matmul(
                        ps[:, :], ofm[:, c, tt * P:(tt + 1) * P],
                        ow[:, c, half * 512:(half + 1) * 512],
                        start=(c == 0), stop=False),
                        waits=(w_mm if c == 0 else []))
                mlast = S.to("tensor", lambda e, tt=tt, half=half, ps=ps: e.matmul(
                    ps[:, :], ones2k[:, tt * P:(tt + 1) * P],
                    owr[:, half * 512:(half + 1) * 512],
                    start=False, stop=True), inc=1)
                ev_m[unit] = S.to("vector", lambda e, par=par, half=half, ps=ps:
                    e.scalar_tensor_tensor(
                        r3_t[:, par, half * 512:(half + 1) * 512], ps[:, :], 1.0,
                        x1_t[:, par, half * 512:(half + 1) * 512],
                        ALU.mult, ALU.add), waits=[mlast, mx1], inc=1)
                unit += 1
            dmr[tt] = S.dma("sync", tile_view(dr["r3"], tt), r3_t[:, par, :],
                            ch=f"r35{par}", waits=[ev_m[unit - 1]])
        S.barrier()


def build_stage5():
    nc = bass.Bass("TRN2", target_bir_lowering=False)
    dr = {}
    dr["qhfm"] = nc.dram_tensor("qhfm", [P, NCH * TPC], BF16, kind="ExternalInput")
    dr["khrot"] = nc.dram_tensor("khrot", [SEQ, D], BF16, kind="ExternalInput")
    dr["vhrot"] = nc.dram_tensor("vhrot", [SEQ, D], BF16, kind="ExternalInput")
    dr["x1"] = nc.dram_tensor("x1", [TPC, D], F32, kind="ExternalInput")
    dr["khn4"] = nc.dram_tensor("khn4", [4, 16], F32, kind="ExternalInput")
    dr["killb"] = nc.dram_tensor("killb", [NQG, SEQ // P], F32, kind="ExternalInput")
    dr["owT"] = nc.dram_tensor("owT", [D + 1, D], F32, kind="ExternalInput")
    dr["r3"] = nc.dram_tensor("r3", [TPC, D], F32, kind="ExternalOutput")
    S = Sched(nc)
    emit_l1_attn(nc, S, dr)
    return nc



def emit_l1_ffn(nc, S, dr):
    id32 = nc.inline_tensor(np.eye(P, dtype=np.float32), "id326")
    NE = 32
    from contextlib import ExitStack
    with ExitStack() as ctx:
        def sb(name, shape, dt):
            return ctx.enter_context(nc.sbuf_tensor(name, shape, dt))

        xnfm = sbm("xnfm6", [P, NCH, TPC], BF16)
        outacc = sb("outacc6", [P, TT, D], F32)
        selsb = sb("sel6", [P, NCH, NE], BF16)
        gt_sb = sb("gt6", [NE, TPC], F32)
        growb = sbm("growb6", [1, 4, TPC], BF16)
        onesr = sb("onesr6", [1, P], BF16)
        lg = sb("lg6", [P, NE], F32)
        vv = sb("vv6", [P, NE], F32)
        msk = sb("msk6", [P, NE], F32)
        sgm = sb("sgm6", [P, NE], F32)
        st6 = sb("st6", [P, 2], F32)
        id32_sb = sb("id326_sb", [P, P], F32)
        gb_sb = sb("gb6", [P, 512], F32)
        srelu = sb("srelu6", [P, 512], F32)
        inner_w = ExitStack()
        kb = inner_w.enter_context(nc.sbuf_tensor("kb6", [P, 4, NCH, 512], BF16))
        vb = inner_w.enter_context(nc.sbuf_tensor("vb6", [P, 4, 4, D], BF16))
        sg = inner_w.enter_context(nc.sbuf_tensor("sg6", [P, 4, 4, 512], BF16))

        S.dma("sync", xnfm[:, :, :],
              dr["x1nfm"].ap().rearrange("p (c t) -> p c t", c=NCH))
        S.dma("gpsimd", selsb[:, :, :],
              dr["selT"].ap().rearrange("(c p) n -> p c n", p=P))
        S.dma("sync", id32_sb[:, :], id32[:, :])
        S.to("vector", lambda e: e.memset(onesr[:, :], 1.0))
        S.barrier()

        inner = ExitStack()
        pg = inner.enter_context(nc.psum_tensor("pg6", [P, NE], F32))
        pgt = inner.enter_context(nc.psum_tensor("pgt6", [NE, P], F32))
        m4 = st6[:, 0:1]
        prev_ev = None
        prev_tr = None
        mg_prev = None
        for tt in range(TT):
            w_mm = [prev_ev] if prev_ev else []
            for c in range(NCH):
                S.to("tensor", lambda e, c=c, tt=tt: e.matmul(
                    pg[:, :], xnfm[:, c, tt * P:(tt + 1) * P], selsb[:, c, :],
                    start=(c == 0), stop=(c == NCH - 1)),
                    waits=(w_mm if c == 0 else []), inc=(1 if c == NCH - 1 else 0))
            mgl = (S.sem["tensor"], S.count["tensor"])
            w_lg = [mgl]
            if mg_prev is not None:
                w_lg.append(msig_prev)
            prev_ev = S.to("vector", lambda e: e.tensor_copy(lg[:, :], pg[:, :]),
                           waits=w_lg, inc=1)
            S.to("vector", lambda e: e.tensor_copy(vv[:, :], lg[:, :]))
            for r in range(4):
                S.to("vector", lambda e: e.reduce_max(m4, vv[:, :], axis=AX.X))
                if r < 3:
                    S.to("vector", lambda e: e.tensor_scalar(
                        msk[:, :], vv[:, :], m4, None, ALU.is_ge))
                    S.to("vector", lambda e: e.scalar_tensor_tensor(
                        vv[:, :], msk[:, :], -1e30, vv[:, :], ALU.mult, ALU.add))
            w_sig = [prev_ev] + ([mg_prev] if mg_prev is not None else [])
            msig = S.to("scalar", lambda e: e.activation(sgm[:, :], lg[:, :],
                                                         AF.Sigmoid),
                        waits=w_sig, inc=1)
            msig_prev = msig
            S.to("vector", lambda e: e.tensor_scalar(
                msk[:, :], lg[:, :], m4, None, ALU.is_ge), waits=[msig])
            mg = S.to("vector", lambda e: e.tensor_tensor(
                msk[:, :], msk[:, :], sgm[:, :], ALU.mult), inc=1)
            mg_prev = mg
            w_tr = [mg]
            if prev_tr is not None:
                w_tr.append(prev_tr)
            mtr = S.to("tensor", lambda e: e.transpose(
                pgt[:, :], msk[:, :], id32_sb[:, :]), waits=w_tr, inc=1)
            prev_tr = S.to("vector", lambda e, tt=tt: e.tensor_copy(
                gt_sb[:, tt * P:(tt + 1) * P], pgt[0:NE, :]), waits=[mtr], inc=1)
        S.barrier()
        inner.close()

        ps6 = [ctx.enter_context(nc.psum_tensor(f"ps6{i}", [P, 512], F32))
               for i in range(2)]
        py6 = [ctx.enter_context(nc.psum_tensor(f"py6{i}", [P, 512], F32))
               for i in range(2)]
        pgb = ctx.enter_context(nc.psum_tensor("pgb6", [P, 512], F32))
        for tt in range(TT):
            S.to("vector", lambda e, tt=tt: e.memset(outacc[:, tt, :], 0.0))
        S.barrier()

        last_y = {}
        sg_marks = {}
        gbe_prev = None
        for ebk in range(8):
            w_lw = [last_y[(ebk - 1, 3, 3, 1)]] if (ebk - 1, 3, 3, 1) in last_y else []
            lws = []
            for ei in range(4):
                eg = ebk * 4 + ei
                lws.append(S.dma("gpsimd", kb[:, ei, :, :],
                                 dr[f"keys_{eg}"].ap().rearrange(
                                     "(c p) n -> p c n", p=P),
                                 ch=f"kb{ei}", waits=w_lw))
                lws.append(S.dma("gpsimd", vb[:, ei, :, :],
                                 dr[f"values_{eg}"].ap().rearrange(
                                     "(c p) n -> p c n", p=P),
                                 ch=f"vb{ei}", waits=w_lw))
            grow_m = {}
            for ei in range(4):
                eg = ebk * 4 + ei
                w_gw = list(w_lw)
                grow_m[ei] = S.dma("gpsimd", growb[:, ei, :],
                                   gt_sb[eg:eg + 1, :], ch=f"gw{ei}",
                                   waits=w_gw)
            for tg in range(NQG):
                for ei in range(4):
                    w_gb = [grow_m[ei]]
                    if gbe_prev is not None:
                        w_gb.append(gbe_prev)
                    mgb = S.to("tensor", lambda e, ei=ei, tg=tg: e.matmul(
                        pgb[:, :], onesr[:, :],
                        growb[:, ei, tg * 512:(tg + 1) * 512],
                        start=True, stop=True), waits=w_gb, inc=1)
                    gbe = S.to("vector", lambda e: e.tensor_copy(gb_sb[:, :],
                                                                pgb[:, :]),
                               waits=[mgb], inc=1)
                    gbe_prev = gbe
                    for fc in range(4):
                        un = ei * 4 + fc
                        ps = ps6[un % 2]
                        w_mm = list(lws) if (tg == 0 and un == 0) else []
                        key_prev = (ebk, tg, un - 2)
                        if un < 2:
                            key_prev = (ebk, tg - 1, un + 14) if tg > 0 else (
                                (ebk - 1, NQG - 1, un + 14))
                        if key_prev in sg_marks:
                            w_mm.append(sg_marks[key_prev])
                        for c in range(NCH):
                            S.to("tensor", lambda e, c=c, ei=ei, fc=fc, tg=tg, ps=ps:
                                 e.matmul(ps[:, :], kb[:, ei, c, fc * P:(fc + 1) * P],
                                          xnfm[:, c, tg * 512:(tg + 1) * 512],
                                          start=(c == 0), stop=(c == NCH - 1)),
                                 waits=(w_mm if c == 0 else []),
                                 inc=(1 if c == NCH - 1 else 0))
                        msf = (S.sem["tensor"], S.count["tensor"])
                        w_re = [msf]
                        if key_prev in sg_marks:
                            w_re.append(sg_marks[key_prev])
                        prev_un = (ebk, tg, un - 1)
                        if un == 0:
                            prev_un = (ebk, tg - 1, 15) if tg > 0 else (ebk - 1, NQG - 1, 15)
                        if prev_un in sg_marks:
                            w_re.append(sg_marks[prev_un])
                        mre = S.to("scalar", lambda e, ps=ps: e.activation(
                            srelu[:, :], ps[:, :], AF.Relu), waits=w_re, inc=1)
                        sg_marks[(ebk, tg, un)] = S.to(
                            "vector", lambda e, ei=ei, fc=fc: e.tensor_tensor(
                                sg[:, ei, fc, :], srelu[:, :], gb_sb[:, :],
                                ALU.mult), waits=[mre, gbe], inc=1)
                sgall = sg_marks[(ebk, tg, 15)]
                for tq in range(4):
                    tt = tg * 4 + tq
                    for half in range(2):
                        ps = py6[half]
                        w_mm = [sgall]
                        if (ebk, tq - 1, half) in last_y:
                            w_mm.append(last_y[(ebk, tq - 1, half)])
                        elif tq == 0 and (ebk, 3, half, "prev") in last_y:
                            w_mm.append(last_y[(ebk, 3, half, "prev")])
                        first = True
                        for ei in range(4):
                            for fc in range(4):
                                S.to("tensor", lambda e, ei=ei, fc=fc, tq=tq,
                                     half=half, ps=ps, first=first: e.matmul(
                                    ps[:, :], sg[:, ei, fc, tq * P:(tq + 1) * P],
                                    vb[:, ei, fc, half * 512:(half + 1) * 512],
                                    start=first,
                                    stop=(ei == 3 and fc == 3)),
                                    waits=(w_mm if first else []),
                                    inc=(1 if (ei == 3 and fc == 3) else 0))
                                first = False
                        my = (S.sem["tensor"], S.count["tensor"])
                        last_y[(ebk, tq, half)] = S.to(
                            "vector", lambda e, tt=tt, half=half, ps=ps:
                            e.scalar_tensor_tensor(
                                outacc[:, tt, half * 512:(half + 1) * 512],
                                ps[:, :], 1.0,
                                outacc[:, tt, half * 512:(half + 1) * 512],
                                ALU.mult, ALU.add), waits=[my], inc=1)
                for half in range(2):
                    last_y[(ebk, 3, half, "prev")] = last_y[(ebk, 3, half)]
                if tg == NQG - 1:
                    last_y[(ebk, 3, 3, 1)] = last_y[(ebk, 3, 1)]
        S.barrier()
        inner_w.close()
        r3_t = ctx.enter_context(nc.sbuf_tensor("r3t6", [P, 2, D], F32))

        dmf = {}
        for tt in range(TT):
            par = tt % 2
            w_r3 = [dmf[tt - 2]] if (tt - 2) in dmf else []
            mr3 = S.dma("sync", r3_t[:, par, :], tile_view(dr["r3"], tt),
                        ch=f"r36{par}", waits=w_r3)
            ma = S.to("vector", lambda e, tt=tt, par=par: e.tensor_add(
                r3_t[:, par, :], r3_t[:, par, :], outacc[:, tt, :]),
                waits=[mr3], inc=1)
            dmf[tt] = S.dma("sync", tile_view(dr["out"], tt), r3_t[:, par, :],
                            ch=f"o6{par}", waits=[ma])
        S.barrier()


def build_stage6():
    nc = bass.Bass("TRN2", target_bir_lowering=False)
    dr = {}
    dr["x1nfm"] = nc.dram_tensor("x1nfm", [P, NCH * TPC], BF16, kind="ExternalInput")
    dr["r3"] = nc.dram_tensor("r3", [TPC, D], F32, kind="ExternalInput")
    dr["selT"] = nc.dram_tensor("selT", [D, 32], F32, kind="ExternalInput")
    for eg in range(32):
        dr[f"keys_{eg}"] = nc.dram_tensor(f"keys_{eg}", [D, 512], F32,
                                          kind="ExternalInput")
        dr[f"values_{eg}"] = nc.dram_tensor(f"values_{eg}", [512, D], F32,
                                            kind="ExternalInput")
    dr["out"] = nc.dram_tensor("out", [TPC, D], F32, kind="ExternalOutput")
    S = Sched(nc)
    emit_l1_ffn(nc, S, dr)
    return nc


def _ext(w, b):
    return np.ascontiguousarray(
        np.concatenate([w.T, b[None, :]], axis=0), dtype=np.float32)


def host_prep_stage1(inputs):
    x = np.asarray(inputs["L0_attn_x"]).reshape(NCORES * TPC, D)
    wq = _ext(np.asarray(inputs["L0_attn_Wq"]), np.asarray(inputs["L0_attn_bq"]))
    wk = _ext(np.asarray(inputs["L0_attn_Wk"]), np.asarray(inputs["L0_attn_bk"]))
    wv = _ext(np.asarray(inputs["L0_attn_Wv"]), np.asarray(inputs["L0_attn_bv"]))
    maps = []
    for c in range(NCORES):
        maps.append(dict(x=np.ascontiguousarray(x[c * TPC:(c + 1) * TPC]),
                         wqT=wq, wkT=wk, wvT=wv))
    return maps


_ROPE_PERM = np.concatenate([np.arange(0, D, 2), np.arange(1, D, 2)])
_GLU_PERM = np.concatenate(
    [np.concatenate([np.arange(256 * j, 256 * (j + 1), 2),
                     np.arange(256 * j + 1, 256 * (j + 1), 2)])
     for j in range(8)])


def _rope_tables():
    i = np.arange(512, dtype=np.float64)
    theta = 10000.0 ** (-2.0 * (i - 1.0) / float(D))
    ang = np.arange(SEQ, dtype=np.float64)[:, None] * theta[None, :]
    return np.cos(ang).astype(np.float32), np.sin(ang).astype(np.float32)



def emit_l1_merged(nc, S, dr, agkh=None, agvh=None, agnh=None):
    mask_np = np.zeros((P, 4, 512), dtype=ml_dtypes.bfloat16)
    for dsv in range(4):
        for iv in range(P):
            mask_np[iv, dsv, dsv * P + iv:] = 1.0
    maskc = nc.inline_tensor(np.ascontiguousarray(mask_np.reshape(P, 4 * 512)),
                             "maskc5")
    NKT = SEQ // P
    NE = 32
    EB = 2
    NBLK = NE // EB
    from contextlib import ExitStack
    with ExitStack() as ctx:
        def sb(name, shape, dt):
            return ctx.enter_context(nc.sbuf_tensor(name, shape, dt))

        def pt(name, shape, dt):
            return ctx.enter_context(nc.psum_tensor(name, shape, dt))

        onesr = sb("onesr5", [1, P], BF16)
        ind = sb("ind5", [P, 2], BF16)
        btabb = sb("btabb5", [1, 8, NQG, NKT], BF16)
        biast = sb("biast5", [P, 2, NQG, NKT], F32)
        rows5 = sb("rows5", [1, 32], F32)
        kmaxh = sb("kmaxh5", [1, 16], F32)
        khn_in = sb("khn_in5", [1, 64], F32)
        kill_in = sb("kill_in5", [1, NQG, NKT], F32)
        lb_sb = sb("lb5", [64, 512], F32)
        selsb = sb("sel6", [P, NCH, NE], BF16)
        gt_sb = sb("gt6", [NE, TPC], BF16)
        lgs = [sb(f"lg6{i}", [P, NE], F32) for i in range(4)]
        vvs = [sb(f"vv6{i}", [P, NE], F32) for i in range(4)]
        msks = [sb(f"msk6{i}", [P, NE], F32) for i in range(4)]
        sgms = [sb(f"sgm6{i}", [P, NE], F32) for i in range(4)]
        st6 = sb("st6", [P, 4], F32)
        id32_sb = sb("id326_sb", [P, P], F32)
        gb_sb = sb("gb6", [P, 512], F32)
        srelu = sb("srelu6", [P, 512], BF16)
        yst = sb("yst6", [P, 4, 512], F32)

        inner_main = ExitStack()

        def sbm(name, shape, dt):
            return inner_main.enter_context(nc.sbuf_tensor(name, shape, dt))

        def ptm(name, shape, dt):
            return inner_main.enter_context(nc.psum_tensor(name, shape, dt))

        qhp2 = sbm("qhp5", [P, 2, TPC], BF16)
        khp = sbm("khp5", [P, SEQ], BF16)
        vhs = sbm("vhs5", [P, NKT, 130], BF16)
        pts = sbm("pts5", [P, 2, 2, 512], BF16)
        ofm_hp = sbm("ofmhp5", [P, TPC], BF16)
        masksb = sbm("masksb5", [P, 4, 512], BF16)
        xnfm = sbm("xnfm6", [P, NCH, TPC], BF16)
        kb = sbm("kb6", [P, 2, EB, NCH, 512], BF16)
        vb = sbm("vb6", [P, 2, EB, 4, D], BF16)
        sg = sbm("sg6", [P, 2, EB, 4, 512], BF16)
        gbb = sbm("gbb6", [P, EB, NQG, 512], BF16)
        qsq = sg.ap().rearrange("p a b c d -> p (a b c d)")[:, 0:TPC]

        id32 = nc.inline_tensor(np.eye(P, dtype=np.float32), "id326")

        pst = [ptm(f"pst5{i}", [P, 2, 512], F32) for i in range(2)]
        pz = [ptm(f"pz5{i}", [P, 512], F32) for i in range(2)]
        pf = [ptm(f"pf6{i}", [P, 512], F32) for i in range(2)]

        S.dma("sync", masksb[:, :, :],
              maskc.ap().rearrange("p (d f) -> p d f", d=4))
        if agnh is not None:
            S.to("sync", lambda e: e.wait_ge(agnh[0], agnh[1]))
        S.dma("sync", khn_in[:, :], dr["khn4"].ap().rearrange("r c -> (r c)").unsqueeze(0))
        S.dma("sync", kill_in[:, :, :], dr["killb"].ap().unsqueeze(0))
        S.dma("sync", xnfm[:, :, :],
              dr["x1nfm"].ap().rearrange("p (c t) -> p c t", c=NCH))
        S.dma("gpsimd", selsb[:, :, :],
              dr["selT"].ap().rearrange("(c p) n -> p c n", p=P))
        S.dma("sync", id32_sb[:, :], id32[:, :])
        S.dma("gpsimd", dr["rff"].ap(), dr["x1"].ap(), ch="seed", fence=False)
        S.to("vector", lambda e: e.memset(onesr[:, :], 1.0))
        S.to("vector", lambda e: e.memset(ind[:, :], 0.0))
        S.to("vector", lambda e: e.memset(ind[0:64, 0:1], 1.0))
        S.to("vector", lambda e: e.memset(ind[64:128, 1:2], 1.0))
        S.barrier()
        rotk = emit_rotate(nc, S, dr["khfull"], dr["khrot"], [agkh])
        rotv = emit_rotate(nc, S, dr["vhfull"], dr["vhrot"], [agvh])

        qld = {}
        qch = [None]

        def qload(h, waits=()):
            w = list(waits)
            if qch[0] is not None:
                w.append(qch[0])
            mk = S.dma("sync", qhp2[:, h % 2, :],
                       dr["qhfm"].ap()[:, h * TPC:(h + 1) * TPC],
                       ch="w1r", waits=w)
            qch[0] = mk
            qld[h] = mk
            return mk

        S.to("vector", lambda e: e.tensor_max(kmaxh[:, :], khn_in[:, 0:16],
                                              khn_in[:, 16:32]))
        S.to("vector", lambda e: e.tensor_max(kmaxh[:, :], kmaxh[:, :],
                                              khn_in[:, 32:48]))
        S.to("vector", lambda e: e.tensor_max(kmaxh[:, :], kmaxh[:, :],
                                              khn_in[:, 48:64]))
        sgbank = [pf[0][:, 0:NE], pz[0][:, 0:NE], pz[1][:, 0:NE],
                  pst[0][:, 0, 0:NE]]
        tgbank = [pf[1][0:NE, 0:P], pst[0][0:NE, 1, 0:P],
                  pst[1][0:NE, 0, 0:P], pst[1][0:NE, 1, 0:P]]
        prev_ev = [None] * 4
        prev_tr = [None] * 4
        mg_prev = [None] * 4
        msig_prev = [None] * 4
        for g in range(TT // 4):
            tts = [g * 4 + i for i in range(4)]
            mgl = {}
            for pi, tt in enumerate(tts):
                w_mm = [prev_ev[pi]] if prev_ev[pi] else []
                for c in range(NCH):
                    S.to("tensor", lambda e, c=c, tt=tt, pi=pi: e.matmul(
                        sgbank[pi],
                        xnfm[:, c, tt * P:(tt + 1) * P], selsb[:, c, :],
                        start=(c == 0), stop=(c == NCH - 1)),
                        waits=(w_mm if c == 0 else []),
                        inc=(1 if c == NCH - 1 else 0))
                mgl[pi] = (S.sem["tensor"], S.count["tensor"])
            for pi in range(4):
                w_lg = [mgl[pi]]
                if msig_prev[pi] is not None:
                    w_lg.append(msig_prev[pi])
                prev_ev[pi] = S.to("vector", lambda e, pi=pi: e.tensor_copy(
                    lgs[pi][:, :], sgbank[pi]),
                    waits=w_lg, inc=1)
            for pi in range(4):
                m4p = st6[:, pi:pi + 1]
                S.to("vector", lambda e, pi=pi: e.tensor_copy(
                    vvs[pi][:, :], lgs[pi][:, :]))
                for r in range(4):
                    S.to("vector", lambda e, pi=pi, m4p=m4p: e.reduce_max(
                        m4p, vvs[pi][:, :], axis=AX.X))
                    if r < 3:
                        S.to("vector", lambda e, pi=pi, m4p=m4p: e.tensor_scalar(
                            msks[pi][:, :], vvs[pi][:, :], m4p, None, ALU.is_ge))
                        S.to("vector", lambda e, pi=pi: e.scalar_tensor_tensor(
                            vvs[pi][:, :], msks[pi][:, :], -1e30, vvs[pi][:, :],
                            ALU.mult, ALU.add))
            for pi in range(4):
                w_sig = [prev_ev[pi]] + ([mg_prev[pi]] if mg_prev[pi] else [])
                msig_prev[pi] = S.to("scalar", lambda e, pi=pi: e.activation(
                    sgms[pi][:, :], lgs[pi][:, :], AF.Sigmoid),
                    waits=w_sig, inc=1)
            mgs = {}
            for pi in range(4):
                m4p = st6[:, pi:pi + 1]
                S.to("vector", lambda e, pi=pi, m4p=m4p: e.tensor_scalar(
                    msks[pi][:, :], lgs[pi][:, :], m4p, None, ALU.is_ge),
                    waits=[msig_prev[pi]])
                mgs[pi] = S.to("vector", lambda e, pi=pi: e.tensor_tensor(
                    msks[pi][:, :], msks[pi][:, :], sgms[pi][:, :], ALU.mult),
                    inc=1)
            mg_prev = [mgs[pi] for pi in range(4)]
            mtrs = {}
            for pi, tt in enumerate(tts):
                w_tr = [mgs[pi]]
                if prev_tr[pi] is not None:
                    w_tr.append(prev_tr[pi])
                mtrs[pi] = S.to("tensor", lambda e, pi=pi: e.transpose(
                    tgbank[pi], msks[pi][:, :],
                    id32_sb[:, :]), waits=w_tr, inc=1)
            for pi, tt in enumerate(tts):
                prev_tr[pi] = S.to("vector", lambda e, tt=tt, pi=pi: e.tensor_copy(
                    gt_sb[:, tt * P:(tt + 1) * P],
                    tgbank[pi]), waits=[mtrs[pi]], inc=1)
                S.dma("gpsimd", dr["gtd"].ap()[:, tt * P:(tt + 1) * P],
                      gt_sb[:, tt * P:(tt + 1) * P], ch=f"gtd{tt % 2}",
                      waits=[prev_tr[pi]])
        sq_m = {}
        last_rd = None
        mbq_prev = None
        qload(0)
        qload(1)
        for hp in range(NCH):
            msq = S.to("scalar", lambda e, hp=hp: e.activation(
                qsq, qhp2[:, hp % 2, :], AF.Square), waits=[qld[hp]], inc=1)
            if hp + 2 < NCH:
                qload(hp + 2, [msq])
            for qg in range(NQG):
                vmax = rows5[0:1, 0:1]
                vsq = rows5[0:1, 1:2]
                vpair = rows5[0:1, 2:3]
                for h2 in range(2):
                    w_mm = [msq]
                    if last_rd is not None:
                        w_mm.append(last_rd)
                    elif hp == 0 and qg == 0:
                        w_mm.append(prev_ev[1])
                    mn = S.to("tensor", lambda e, qg=qg, h2=h2: e.matmul(
                        pz[0][0:1, :], ind[:, h2:h2 + 1],
                        qsq[:, qg * 512:(qg + 1) * 512],
                        start=True, stop=True), waits=w_mm, inc=1)
                    last_rd = S.to("vector", lambda e: e.reduce_max(
                        vmax, pz[0][0:1, :], axis=AX.X), waits=[mn], inc=1)
                    msqr = S.to("scalar", lambda e: e.activation(
                        vsq, vmax, AF.Sqrt), waits=[last_rd], inc=1)
                    if h2 == 0:
                        S.to("vector", lambda e, hp=hp: e.tensor_scalar(
                            vpair, vsq, kmaxh[0:1, 2 * hp:2 * hp + 1], None,
                            ALU.mult), waits=[msqr])
                    else:
                        S.to("vector", lambda e, hp=hp: e.tensor_scalar(
                            vsq, vsq, kmaxh[0:1, 2 * hp + 1:2 * hp + 2], None,
                            ALU.mult), waits=[msqr])
                        S.to("vector", lambda e: e.tensor_max(vpair, vpair, vsq))
                mng = S.to("vector", lambda e: e.tensor_scalar(
                    vpair, vpair, -1.002 / 8.0, None, ALU.mult), inc=1)
                mbq_prev = S.to("vector", lambda e, hp=hp, qg=qg: e.tensor_scalar(
                    btabb[0:1, hp, qg, :], kill_in[0:1, qg, :], vpair, None,
                    ALU.add), waits=[mng])
        mcv = mbq_prev
        S.dma("gpsimd", dr["btd"].ap(),
              btabb[0:1, :, :, :].rearrange("p a q k -> p (a q k)"),
              ch="ofmd", waits=[mcv])
        S.barrier()

        shared_pf0 = [None]

        class SharedBankGuard:
            def __init__(self, cell):
                self.cell = cell
                self.g1 = None
            def __getitem__(self, k):
                return self.cell[0] if k == 0 else self.g1
            def __setitem__(self, k, v):
                if k == 0:
                    self.cell[0] = v
                else:
                    self.g1 = v

        def gen_attn():
            ex_marks = {}
            s_last = {}
            pv_marks = {}
            o_marks = {}
            lbe_m = None
            last_hp_pe = None
            odma = {}
            bdma = {}

            def bias_bcast(hp, waits):
                return S.dma("gpsimd", None, None, ch=f"gtd{hp % 2}",
                             waits=waits,
                             fn=lambda e, hp=hp: e.dma_start(
                                 biast[:, hp % 2, :, :]
                                 .rearrange("p q k -> p (q k)"),
                                 bass.AP(dr["btd"], hp * NQG * NKT,
                                         [[0, P], [1, NQG * NKT]])))

            for hp in range(NCH):
                if hp == 0:
                    bdma[0] = bias_bcast(0, [])
                    bdma[1] = bias_bcast(1, [])
                elif hp + 1 < NCH:
                    bdma[hp + 1] = bias_bcast(
                        hp + 1, [ex_marks[(hp - 1, NQG - 1, NKT - 1)]])
                w_kh = ([last_hp_pe] if last_hp_pe else []) + list(rotk)
                mkh = S.dma("sync", None, None, ch="khp", waits=w_kh,
                            fn=lambda e, hp=hp: e.dma_start(
                                khp[:, :], dr["khrot"].ap()[:, hp * P:(hp + 1) * P],
                                transpose=True))
                if hp == 0:
                    qload(0)
                    qload(1)
                elif hp + 1 < NCH:
                    qload(hp + 1, [s_last[hp - 1]])
                w_vh = ([last_hp_pe] if last_hp_pe else []) + list(rotv)
                mv1 = S.dma("gpsimd", vhs[:, :, 0:64],
                            dr["vhrot"].ap()[:, hp * P:hp * P + 64]
                            .rearrange("(s p) d -> p s d", p=P), ch="vh5a",
                            waits=w_vh)
                mv2 = S.dma("gpsimd", vhs[:, :, 65:129],
                            dr["vhrot"].ap()[:, hp * P + 64:(hp + 1) * P]
                            .rearrange("(s p) d -> p s d", p=P), ch="vh5b",
                            waits=w_vh)
                mo1 = S.to("vector", lambda e: e.memset(vhs[:, :, 64:65], 1.0),
                           waits=[mv1, mv2], inc=1)
                vh_ready = [mv1, mv2, mo1,
                            S.to("vector", lambda e: e.memset(vhs[:, :, 129:130], 1.0),
                                 inc=1)]
                yield
                for qg in range(NQG):
                    pt_ready = {}
                    for it in range(NKT + 2):
                        kt = it
                        if kt < NKT:
                            ps = pst[kt % 2]
                            slot = kt % 2
                            w_mm = []
                            if qg == 0 and kt == 0:
                                w_mm += [mkh]
                            if (hp, qg, kt - 2) in ex_marks:
                                w_mm.append(ex_marks[(hp, qg, kt - 2)])
                            elif (hp, qg - 1, kt + NKT - 2) in ex_marks:
                                w_mm.append(ex_marks[(hp, qg - 1, kt + NKT - 2)])
                            elif (hp - 1, NQG - 1, kt + NKT - 2) in ex_marks:
                                w_mm.append(ex_marks[(hp - 1, NQG - 1,
                                                      kt + NKT - 2)])
                            if qg == 0 and kt == 0:
                                w_mm.append(qld[hp])
                            S.to("tensor", lambda e, hp=hp, qg=qg, kt=kt, ps=ps:
                                 e.matmul(
                                ps[:, 0, :], khp[0:64, kt * P:(kt + 1) * P],
                                qhp2[0:64, hp % 2, qg * 512:(qg + 1) * 512],
                                start=True, stop=True), waits=w_mm)
                            msb = S.to("tensor", lambda e, hp=hp, qg=qg, kt=kt,
                                       ps=ps: e.matmul(
                                ps[:, 1, :], khp[64:128, kt * P:(kt + 1) * P],
                                qhp2[64:128, hp % 2, qg * 512:(qg + 1) * 512],
                                start=True, stop=True), inc=1)
                            if qg == NQG - 1 and kt == NKT - 1:
                                s_last[hp] = msb
                        ktd = it - 2
                        if ktd >= 0:
                            slotd = ktd % 2
                            w_pv = [pt_ready[ktd]]
                            if ktd == 0:
                                if qg == 0:
                                    w_pv += vh_ready
                                prev_o = o_marks.get((hp, qg - 1)) if qg > 0 else (
                                    o_marks.get((hp - 1, NQG - 1)))
                                if prev_o:
                                    w_pv.append(prev_o)
                            for h2 in range(2):
                                mpv = S.to("tensor", lambda e, h2=h2, ktd=ktd,
                                           slotd=slotd: e.matmul(
                                    pz[h2][0:65, :],
                                    vhs[:, ktd, 65 * h2:65 * (h2 + 1)],
                                    pts[:, slotd, h2, :], start=(ktd == 0),
                                    stop=(ktd == NKT - 1)),
                                    waits=(w_pv if h2 == 0 else []),
                                    inc=(1 if h2 == 1 else 0))
                            pv_marks[(hp, qg, ktd)] = mpv
                        if kt < NKT:
                            w_ex = [msb]
                            if qg == 0 and kt == 0:
                                w_ex.append(bdma[hp])
                            prev_pt = pv_marks.get((hp, qg, kt - 2))
                            if prev_pt is None and kt < 2:
                                prev_pt = pv_marks.get((hp, qg - 1, kt + NKT - 2)) \
                                    if qg > 0 else pv_marks.get((hp - 1, NQG - 1,
                                                                 kt + NKT - 2))
                            if prev_pt is not None:
                                w_ex.append(prev_pt)
                            mex = S.to("scalar", lambda e, qg=qg, kt=kt, ps=ps,
                                       slot=slot, hp=hp:
                                       e.activation(pts[:, slot, :, :], ps[:, :, :],
                                                    AF.Exp, scale=1.0 / 8.0,
                                                    bias=biast[:, hp % 2, qg,
                                                               kt:kt + 1]),
                                       waits=w_ex, inc=1)
                            ex_marks[(hp, qg, kt)] = mex
                            ptr_m = mex
                            if qg * 4 <= kt < qg * 4 + 4:
                                dsv = kt - qg * 4
                                S.to("vector", lambda e, dsv=dsv, slot=slot:
                                    e.tensor_tensor(pts[:, slot, 0, :],
                                                    pts[:, slot, 0, :],
                                                    masksb[:, dsv, :], ALU.mult),
                                    waits=[mex])
                                ptr_m = S.to("vector", lambda e, dsv=dsv, slot=slot:
                                    e.tensor_tensor(pts[:, slot, 1, :],
                                                    pts[:, slot, 1, :],
                                                    masksb[:, dsv, :], ALU.mult),
                                    inc=1)
                                ex_marks[(hp, qg, kt)] = ptr_m
                            pt_ready[kt] = ptr_m
                        if it % 4 == 3:
                            yield
                    zdone = pv_marks[(hp, qg, NKT - 1)]
                    for h2 in range(2):
                        w_rc = [zdone]
                        if lbe_m is not None:
                            w_rc.append(lbe_m)
                        mrc = S.to("vector", lambda e, h2=h2: e.reciprocal(
                            lb_sb[0:1, 0:512], pz[h2][64:65, :]), waits=w_rc, inc=1)
                        md1 = S.dma("gpsimd", dr["lbd"].ap(), lb_sb[0:1, 0:512],
                                    ch="lbd1", waits=[mrc])
                        mlbe = S.dma("gpsimd", None, None, ch="lbd2", waits=[md1],
                                     fn=lambda e: e.dma_start(
                                         lb_sb[:, :],
                                         bass.AP(dr["lbd"], 0, [[0, 64], [1, 512]])))
                        lbe_m = mlbe
                        w_o = [mlbe]
                        if qg == 0 and (hp - 1) in odma:
                            w_o.append(odma[hp - 1])
                        mo = S.to("vector", lambda e, h2=h2, qg=qg: e.tensor_tensor(
                            ofm_hp[64 * h2:64 * (h2 + 1), qg * 512:(qg + 1) * 512],
                            pz[h2][0:64, :], lb_sb[0:64, :], ALU.mult),
                            waits=w_o, inc=1)
                        o_marks[(hp, qg)] = mo
                    yield
                odma[hp] = S.dma("sync", None, None, ch="ofmd",
                                 fn=lambda e, hp=hp: e.dma_start(
                                     dr["ofm"].ap().rearrange(
                                         "p (c t) -> p c t", c=NCH)[:, hp, :],
                                     ofm_hp[:, :]),
                                 waits=[o_marks[(hp, NQG - 1)]])
                last_hp_pe = pv_marks[(hp, NQG - 1, NKT - 1)]
                o_marks[(hp, NQG - 1)] = (odma[hp][0], odma[hp][1])

        def gen_ffn():
            sg_marks = {}
            relu_prev = None
            bank_guard = SharedBankGuard(shared_pf0)
            ydma_prev = {}
            ch_prev = {}
            last_y = None
            blk_pe_done = {}
            y_pe_done = {}
            gw_war = {}
            lws_blk = {}
            ch_last = {}

            def load_blk(b):
                sl = b % 2
                ms = []
                for ei in range(EB):
                    eg = b * EB + ei
                    for nm, buf, src in (("kb", kb, "keys"),
                                         ("vb", vb, "values")):
                        chn = f"{nm}{ei}"
                        w = []
                        if chn in ch_last:
                            w.append(ch_last[chn])
                        if (b - 2) in blk_pe_done:
                            w.append(blk_pe_done[b - 2])
                        mk = S.dma("gpsimd", buf[:, sl, ei, :, :],
                                   dr[f"{src}_{eg}"].ap().rearrange(
                                       "(c p) n -> p c n", p=P),
                                   ch=chn, waits=w)
                        ch_last[chn] = mk
                        ms.append(mk)
                lws_blk[b] = ms

            for ebk in range(NBLK):
                sl = ebk % 2
                if ebk == 0:
                    load_blk(0)
                lws = lws_blk[ebk]
                yield
                gb_done = []
                for ei in range(EB):
                    eg = ebk * EB + ei
                    w_gw = [gw_war[ei]] if ei in gw_war else []
                    mk = S.dma("gpsimd", None, None, ch=f"gw{ei}", waits=w_gw,
                               fn=lambda e, ei=ei, eg=eg: e.dma_start(
                                   gbb[:, ei, :, :].rearrange("p a b -> p (a b)"),
                                   bass.AP(dr["gtd"], eg * TPC, [[0, P], [1, TPC]])))
                    gb_done.append(mk)
                    gw_war[ei] = mk
                yield
                for tg in range(NQG + 1):
                    if tg == 1 and ebk + 1 < NBLK:
                        load_blk(ebk + 1)
                    if tg < NQG:
                        par = tg % 2
                        for ei in range(EB):
                            for fc in range(4):
                                un = ei * 4 + fc
                                bk = (un + 1) % 2
                                ps = pf[bk]
                                w_mm = []
                                if bank_guard[bk] is not None:
                                    w_mm.append(bank_guard[bk])
                                if un == 0:
                                    if tg == 0:
                                        w_mm += lws
                                    yw = y_pe_done.get(tg - 2)
                                    if yw is not None:
                                        w_mm.append(yw)
                                for c in range(NCH):
                                    S.to("tensor", lambda e, c=c, ei=ei, fc=fc,
                                         tg=tg, ps=ps, sl=sl: e.matmul(
                                        ps[:, :],
                                        kb[:, sl, ei, c, fc * P:(fc + 1) * P],
                                        xnfm[:, c, tg * 512:(tg + 1) * 512],
                                        start=(c == 0), stop=(c == NCH - 1)),
                                        waits=(w_mm if c == 0 else []),
                                        inc=(1 if c == NCH - 1 else 0))
                                msf = (S.sem["tensor"], S.count["tensor"])
                                w_re = [msf]
                                if relu_prev is not None:
                                    w_re.append(relu_prev)
                                mre = S.to("scalar", lambda e, ps=ps: e.activation(
                                    srelu[:, :], ps[:, :], AF.Relu),
                                    waits=w_re, inc=1)
                                w_sg = [mre]
                                if tg == 0 and fc == 0:
                                    w_sg.append(gb_done[ei])
                                sg_marks[(tg, un)] = S.to(
                                    "vector", lambda e, ei=ei, fc=fc, par=par, tg=tg:
                                    e.tensor_tensor(
                                        sg[:, par, ei, fc, :], srelu[:, :],
                                        gbb[:, ei, tg, :], ALU.mult),
                                    waits=w_sg, inc=1)
                                relu_prev = sg_marks[(tg, un)]
                                bank_guard[bk] = sg_marks[(tg, un)]
                                if tg == NQG - 1:
                                    gw_war[ei] = sg_marks[(tg, un)]
                                yield
                    tgd = tg - 1
                    if tgd >= 0:
                        pard = tgd % 2
                        sgall = sg_marks[(tgd, 4 * EB - 1)]
                        for tq in range(4):
                            tt = tgd * 4 + tq
                            for half in range(2):
                                ps = pf[half]
                                slot = (tq % 2) * 2 + half
                                w_mm = [sgall]
                                if bank_guard[half] is not None:
                                    w_mm.append(bank_guard[half])
                                first = True
                                for ei in range(EB):
                                    for fc in range(4):
                                        S.to("tensor", lambda e, ei=ei, fc=fc, tq=tq,
                                             half=half, ps=ps, first=first,
                                             pard=pard, sl=sl: e.matmul(
                                            ps[:, :],
                                            sg[:, pard, ei, fc,
                                               tq * P:(tq + 1) * P],
                                            vb[:, sl, ei, fc,
                                               half * 512:(half + 1) * 512],
                                            start=first,
                                            stop=(ei == EB - 1 and fc == 3)),
                                            waits=(w_mm if first else []),
                                            inc=(1 if (ei == EB - 1 and fc == 3)
                                                 else 0))
                                        first = False
                                my = (S.sem["tensor"], S.count["tensor"])
                                w_ev = [my]
                                yp = ydma_prev.get(slot)
                                if yp is not None:
                                    w_ev.append(yp)
                                mev = S.to("vector", lambda e, slot=slot, ps=ps:
                                           e.tensor_copy(yst[:, slot, :], ps[:, :]),
                                           waits=w_ev, inc=1)
                                bank_guard[half] = mev
                                w_acc = [mev]
                                pc = ch_prev.get((tq, half))
                                if pc is not None:
                                    w_acc.append(pc)
                                else:
                                    w_acc.append(seed_mark)
                                mdma = S.dma(
                                    "gpsimd", None, None, ch=f"acc{tq}{half}",
                                    fn=lambda e, tt=tt, half=half, slot=slot:
                                    e.dma_start(
                                        tile_view(dr["rff"], tt)[:,
                                            half * 512:(half + 1) * 512],
                                        yst[:, slot, :], accum_op=ALU.add),
                                    waits=w_acc)
                                ch_prev[(tq, half)] = mdma
                                ydma_prev[slot] = mdma
                            if tq == 3:
                                last_y = ydma_prev[3]
                                y_pe_done[tgd] = my
                                if tgd == NQG - 1:
                                    blk_pe_done[ebk] = my
                    yield

        seed_mark = S.count.get("seed") and (S.sem["seed"], S.count["seed"])
        ga = gen_attn()
        gf = gen_ffn()
        a_live, f_live = True, True
        while a_live or f_live:
            if a_live:
                try:
                    next(ga)
                except StopIteration:
                    a_live = False
            if f_live:
                for _ in range(1):
                    try:
                        next(gf)
                    except StopIteration:
                        f_live = False
                        break
        S.barrier()

        inner_main.close()
        inner = ExitStack()
        ofm = inner.enter_context(nc.sbuf_tensor("ofm5e", [P, NCH, TPC], BF16))
        ow = inner.enter_context(nc.sbuf_tensor("ow5", [P, NCH, D], BF16))
        owr = inner.enter_context(nc.sbuf_tensor("owr5", [1, D], BF16))
        x1_t = inner.enter_context(nc.sbuf_tensor("x1t5", [P, 2, D], F32))
        r3_t = inner.enter_context(nc.sbuf_tensor("r3t5", [P, 2, D], F32))
        ones2k = inner.enter_context(nc.sbuf_tensor("ones2k5", [1, TPC], BF16))
        po = [pt(f"po5{i}", [P, 512], F32) for i in range(4)]
        S.dma("sync", ofm[:, :, :],
              dr["ofm"].ap().rearrange("p (c t) -> p c t", c=NCH))
        S.dma("gpsimd", ow[:, :, :],
              dr["owT"].ap()[0:D, :].rearrange("(c p) n -> p c n", p=P))
        S.dma("gpsimd", owr[:, :], dr["owT"].ap()[D:D + 1, :])
        S.to("vector", lambda e: e.memset(ones2k[:, :], 1.0))
        S.barrier()
        dmr = {}
        ev_m = {}
        unit = 0
        for tt in range(TT):
            par = tt % 2
            w_x1 = [dmr[tt - 2]] if (tt - 2) in dmr else []
            mx1 = S.dma("sync", x1_t[:, par, :], tile_view(dr["rff"], tt),
                        ch=f"x15{par}", waits=w_x1)
            for half in range(2):
                ps = po[unit % 4]
                w_mm = []
                if unit >= 4:
                    w_mm.append(ev_m[unit - 4])
                for c in range(NCH):
                    S.to("tensor", lambda e, c=c, tt=tt, half=half, ps=ps: e.matmul(
                        ps[:, :], ofm[:, c, tt * P:(tt + 1) * P],
                        ow[:, c, half * 512:(half + 1) * 512],
                        start=(c == 0), stop=False),
                        waits=(w_mm if c == 0 else []))
                mlast = S.to("tensor", lambda e, tt=tt, half=half, ps=ps: e.matmul(
                    ps[:, :], ones2k[:, tt * P:(tt + 1) * P],
                    owr[:, half * 512:(half + 1) * 512],
                    start=False, stop=True), inc=1)
                ev_m[unit] = S.to("vector", lambda e, par=par, half=half, ps=ps:
                    e.scalar_tensor_tensor(
                        r3_t[:, par, half * 512:(half + 1) * 512], ps[:, :], 1.0,
                        x1_t[:, par, half * 512:(half + 1) * 512],
                        ALU.mult, ALU.add), waits=[mlast, mx1], inc=1)
                unit += 1
            dmr[tt] = S.dma("sync", tile_view(dr["r3"], tt), r3_t[:, par, :],
                            ch=f"r35{par}", waits=[ev_m[unit - 1]])
        S.barrier()
        inner.close()


RG4 = [[0, 1, 2, 3], [4, 5, 6, 7]]


def emit_ag(nc, S, in_t, out_t, waits):
    return S.to("gpsimd", lambda e: e.collective_compute(
        "AllGather", ALU.bypass, replica_groups=RG4,
        ins=[in_t.ap().opt()], outs=[out_t.ap().opt()]), waits=waits, inc=1,
        serialize=False, ch=f"ag_{out_t.name}", fence=False)


_ROT_STARTS = {}


def emit_rotate(nc, S, src, dst, waits):
    marks = []
    for j in range(4):
        def fn(e, j=j):
            if j not in _ROT_STARTS:
                pid = e.partition_id()
                _ROT_STARTS[j] = (((pid % 4) + j) % 4) * TPC
            return e.dma_start(dst.ap()[j * TPC:(j + 1) * TPC, :],
                               src.ap()[bass.ds(_ROT_STARTS[j], TPC), :])
        marks.append(S.dma("gpsimd", None, None, ch=f"rot_{dst.name}_{j}",
                           fn=fn, waits=waits, fence=False))
    return marks


def build_fused():
    _ROT_STARTS.clear()
    nc = bass.Bass("TRN2", target_bir_lowering=False)
    S = Sched(nc)

    def ext(name, shape, dt=F32):
        return nc.dram_tensor(name, shape, dt, kind="ExternalInput")

    def intr(name, shape, dt=F32):
        return nc.dram_tensor(name, shape, dt, kind="Internal")

    x_d = ext("x", [TPC, D])
    wq0, wk0, wv0 = (ext(n, [D + 1, D]) for n in ("wqT", "wkT", "wvT"))
    normw = ext("normw", [D])
    gateT = ext("gateT", [D + 1, 8])
    w1T = [ext(f"w1T_{e}", [D + 1, 2048]) for e in range(8)]
    w2T = [ext(f"w2T_{e}", [D + 1, D]) for e in range(8)]
    cosf = ext("cosf", [512, TPC])
    sinf = ext("sinf", [512, TPC])
    wq1, wk1, wv1 = (ext("l1" + n, [D, D]) for n in ("wqT", "wkT", "wvT"))
    wqi, wki, wvi = (ext("l1" + n, [D + 1, D]) for n in ("wqiT", "wkiT", "wviT"))
    owT = ext("owT", [D + 1, D])
    killb = ext("killb", [NQG, SEQ // P])
    selT = ext("selT", [D, 32])
    keys = [ext(f"keys_{e}", [D, 512]) for e in range(32)]
    values = [ext(f"values_{e}", [512, D]) for e in range(32)]
    out_d = nc.dram_tensor("out", [TPC, D], F32, kind="ExternalOutput")

    xn_d = intr("xn", [TPC, D])
    r1_d = intr("r1", [TPC, D])
    qfm_d = intr("qfm", [P, NCH * TPC], BF16)
    xnfm_d = intr("xnfm", [P, NCH * TPC], BF16)
    ktm_d = intr("ktm", [TPC, D], BF16)
    vtm_d = intr("vtm", [TPC, D], BF16)
    knmax_d = intr("knmax", [1, 1])
    kfull_d = intr("kfull", [SEQ, D], BF16)
    vfull_d = intr("vfull", [SEQ, D], BF16)
    knm4_d = intr("knm4", [4, 1])
    rf_d = intr("rf", [TPC, D])
    x1_d = intr("x1", [TPC, D])
    x1nfm_d = intr("x1nfm", [P, NCH * TPC], BF16)
    qhfm_d = intr("qhfm", [P, NCH * TPC], BF16)
    khtm_d = intr("khtm", [TPC, D], BF16)
    vhtm_d = intr("vhtm", [TPC, D], BF16)
    khn_d = intr("khn", [16, 1])
    khfull_d = intr("khfull", [SEQ, D], BF16)
    vhfull_d = intr("vhfull", [SEQ, D], BF16)
    khn4_d = intr("khn4", [64, 1])
    khrot_d = intr("khrot", [SEQ, D], BF16)
    vhrot_d = intr("vhrot", [SEQ, D], BF16)
    rff_d = intr("rff", [TPC, D])

    emit_l0_front(nc, S, dict(x=x_d, wqT=wq0, wkT=wk0, wvT=wv0, xn=xn_d,
                              r1=r1_d, qfm=qfm_d, xnfm=xnfm_d, ktm=ktm_d,
                              vtm=vtm_d, knmax=knmax_d))
    agk = emit_ag(nc, S, ktm_d, kfull_d, [])
    agv = emit_ag(nc, S, vtm_d, vfull_d, [])
    agn = emit_ag(nc, S, knmax_d, knm4_d, [])
    dr3 = dict(xnfm=xnfm_d, r2=r1_d, normw=normw, gateT=gateT, x1=rf_d)
    for e in range(8):
        dr3[f"w1T_{e}"] = w1T[e]
        dr3[f"w2T_{e}"] = w2T[e]
    emit_l0_ffn(nc, S, dr3)
    S.to("tensor", lambda e: e.wait_ge(agk[0], agk[1]))
    S.to("tensor", lambda e: e.wait_ge(agv[0], agv[1]))
    S.to("sync", lambda e: e.wait_ge(agk[0], agk[1]))
    S.to("sync", lambda e: e.wait_ge(agn[0], agn[1]))
    S.to("gpsimd", lambda e: e.wait_ge(agv[0], agv[1]), serialize=False)
    emit_l0_attn(nc, S, dict(qfm=qfm_d, kfull=kfull_d, vfull=vfull_d, r1=rf_d,
                             knmax4=knm4_d, r2=x1_d))
    emit_l1_front(nc, S, dict(x1=x1_d, cosf=cosf, sinf=sinf, wqT=wq1, wkT=wk1,
                              wvT=wv1, wqiT=wqi, wkiT=wki, wviT=wvi,
                              x1nfm=x1nfm_d, qhfm=qhfm_d, khtm=khtm_d,
                              vhtm=vhtm_d, khn=khn_d))
    agkh = emit_ag(nc, S, khtm_d, khfull_d, [])
    agvh = emit_ag(nc, S, vhtm_d, vhfull_d, [])
    agnh = emit_ag(nc, S, khn_d, khn4_d, [])
    ofm_d = intr("ofm", [P, NCH * TPC], BF16)
    gtd_d = intr("gtd", [32, TPC], BF16)
    lbd_d = intr("lbd", [1, 512], F32)
    btd_d = intr("btd", [1, 8 * 4 * 64], BF16)
    drm = dict(qhfm=qhfm_d, khrot=khrot_d, vhrot=vhrot_d, khn4=khn4_d,
               khfull=khfull_d, vhfull=vhfull_d, gtd=gtd_d, lbd=lbd_d, btd=btd_d,
               killb=killb, x1nfm=x1nfm_d, selT=selT, x1=x1_d, rff=rff_d,
               ofm=ofm_d, owT=owT, r3=out_d)
    for e in range(32):
        drm[f"keys_{e}"] = keys[e]
        drm[f"values_{e}"] = values[e]
    emit_l1_merged(nc, S, drm, agkh=agkh, agvh=agvh, agnh=agnh)
    return nc


_BUILD_CACHE = {}


def _get(builder, name):
    if name not in _BUILD_CACHE:
        _BUILD_CACHE[name] = builder()
    return _BUILD_CACHE[name]


def host_prep_fused(inputs):
    x = np.asarray(inputs["L0_attn_x"]).reshape(NCORES * TPC, D)
    wq0 = _ext(np.asarray(inputs["L0_attn_Wq"]), np.asarray(inputs["L0_attn_bq"]))
    wk0 = _ext(np.asarray(inputs["L0_attn_Wk"]), np.asarray(inputs["L0_attn_bk"]))
    wv0 = _ext(np.asarray(inputs["L0_attn_Wv"]), np.asarray(inputs["L0_attn_bv"]))
    gateT = _ext(np.asarray(inputs["L0_ffn_gate_w"]),
                 np.asarray(inputs["L0_ffn_gate_b"]))
    w1T = [_ext(np.asarray(inputs["L0_ffn_mlp1_w"])[e][_GLU_PERM],
                np.asarray(inputs["L0_ffn_mlp1_b"])[e][_GLU_PERM])
           for e in range(8)]
    w2T = [_ext(np.asarray(inputs["L0_ffn_mlp2_w"])[e],
                np.asarray(inputs["L0_ffn_mlp2_b"])[e]) for e in range(8)]
    normw = np.ascontiguousarray(np.asarray(inputs["L0_ffn_norm_w"],
                                            dtype=np.float32))
    cos, sin = _rope_tables()
    wq = np.asarray(inputs["L1_attn_wq"]); wk = np.asarray(inputs["L1_attn_wk"])
    wv = np.asarray(inputs["L1_attn_wv"])
    ipw = np.asarray(inputs["L1_attn_in_proj_w"])
    ipb = np.asarray(inputs["L1_attn_in_proj_b"])
    wqi, wki, wvi = np.split(ipw, 3, axis=0)
    bqi, bki, bvi = np.split(ipb, 3, axis=0)

    def w1prep(w):
        return np.ascontiguousarray(w.T[:, _ROPE_PERM], dtype=np.float32)

    def w2prep(w, b):
        return np.ascontiguousarray(
            np.concatenate([w.T[_ROPE_PERM, :], b[None, :]], 0), dtype=np.float32)

    owT = _ext(np.asarray(inputs["L1_attn_out_w"]),
               np.asarray(inputs["L1_attn_out_b"]))
    selT = np.ascontiguousarray(np.asarray(inputs["L1_ffn_expert_sel"]).T,
                                dtype=np.float32)
    keys = np.asarray(inputs["L1_ffn_keys"])
    values = np.asarray(inputs["L1_ffn_values"])

    maps = []
    for c in range(NCORES):
        qbase = (c % 4) * TPC
        rows = np.arange(qbase, qbase + TPC)
        kill = np.zeros((NQG, SEQ // P), dtype=np.float32)
        lim = (SEQ - qbase) // P
        for qg in range(NQG):
            for kt in range(SEQ // P):
                if qg * 4 + 4 <= kt < lim:
                    kill[qg, kt] = -1e30
        m = dict(x=np.ascontiguousarray(x[c * TPC:(c + 1) * TPC]),
                 wqT=wq0, wkT=wk0, wvT=wv0, gateT=gateT, normw=normw,
                 cosf=np.ascontiguousarray(cos[rows].T),
                 sinf=np.ascontiguousarray(sin[rows].T),
                 l1wqT=w1prep(wq), l1wkT=w1prep(wk), l1wvT=w1prep(wv),
                 l1wqiT=w2prep(wqi, bqi), l1wkiT=w2prep(wki, bki),
                 l1wviT=w2prep(wvi, bvi), owT=owT, killb=kill, selT=selT)
        for e in range(8):
            m[f"w1T_{e}"] = w1T[e]
            m[f"w2T_{e}"] = w2T[e]
        for e in range(32):
            m[f"keys_{e}"] = np.ascontiguousarray(keys[e])
            m[f"values_{e}"] = np.ascontiguousarray(values[e])
        maps.append(m)
    return maps


def kernel(**inputs):
    cores = list(range(NCORES))
    nc = _get(build_fused, "fused")
    maps = host_prep_fused(inputs)
    res = run_bass_kernel_spmd(nc, maps, core_ids=cores).results
    out = np.concatenate([res[c]["out"] for c in cores], axis=0)
    return out.reshape(2, SEQ, D)


def _kernel_multilaunch(**inputs):
    cores = list(range(NCORES))

    nc1 = _get(build_stage1, "s1")
    maps1 = host_prep_stage1(inputs)
    r1o = run_bass_kernel_spmd(nc1, maps1, core_ids=cores).results

    kfull = [np.concatenate([r1o[b * 4 + j]["ktm"] for j in range(4)], axis=0)
             for b in range(2)]
    vfull = [np.concatenate([r1o[b * 4 + j]["vtm"] for j in range(4)], axis=0)
             for b in range(2)]
    knm4 = [np.stack([r1o[b * 4 + j]["knmax"][0] for j in range(4)], axis=0)
            for b in range(2)]

    nc2 = _get(build_stage2, "s2")
    maps2 = []
    for c in cores:
        b = c // 4
        maps2.append(dict(qfm=r1o[c]["qfm"], kfull=kfull[b], vfull=vfull[b],
                          r1=r1o[c]["r1"], knmax4=knm4[b]))
    r2o = run_bass_kernel_spmd(nc2, maps2, core_ids=cores).results

    nc3 = _get(build_stage3, "s3")
    gateT = _ext(np.asarray(inputs["L0_ffn_gate_w"]),
                 np.asarray(inputs["L0_ffn_gate_b"]))
    w1T = [_ext(np.asarray(inputs["L0_ffn_mlp1_w"])[e][_GLU_PERM],
                np.asarray(inputs["L0_ffn_mlp1_b"])[e][_GLU_PERM])
           for e in range(8)]
    w2T = [_ext(np.asarray(inputs["L0_ffn_mlp2_w"])[e],
                np.asarray(inputs["L0_ffn_mlp2_b"])[e]) for e in range(8)]
    normw = np.ascontiguousarray(np.asarray(inputs["L0_ffn_norm_w"],
                                            dtype=np.float32))
    maps3 = []
    for c in cores:
        m = dict(xnfm=r1o[c]["xnfm"], r2=r2o[c]["r2"], normw=normw, gateT=gateT)
        for e in range(8):
            m[f"w1T_{e}"] = w1T[e]
            m[f"w2T_{e}"] = w2T[e]
        maps3.append(m)
    r3o = run_bass_kernel_spmd(nc3, maps3, core_ids=cores).results

    nc4 = _get(build_stage4, "s4")
    cos, sin = _rope_tables()
    wq = np.asarray(inputs["L1_attn_wq"]); wk = np.asarray(inputs["L1_attn_wk"])
    wv = np.asarray(inputs["L1_attn_wv"])
    ipw = np.asarray(inputs["L1_attn_in_proj_w"])
    ipb = np.asarray(inputs["L1_attn_in_proj_b"])
    wqi, wki, wvi = np.split(ipw, 3, axis=0)
    bqi, bki, bvi = np.split(ipb, 3, axis=0)

    def w1prep(w):
        return np.ascontiguousarray(w.T[:, _ROPE_PERM], dtype=np.float32)

    def w2prep(w, b):
        return np.ascontiguousarray(
            np.concatenate([w.T[_ROPE_PERM, :], b[None, :]], 0), dtype=np.float32)

    maps4 = []
    for c in cores:
        rows = np.arange((c % 4) * TPC, (c % 4) * TPC + TPC)
        maps4.append(dict(
            x1=r3o[c]["x1"],
            cosf=np.ascontiguousarray(cos[rows].T),
            sinf=np.ascontiguousarray(sin[rows].T),
            wqT=w1prep(wq), wkT=w1prep(wk), wvT=w1prep(wv),
            wqiT=w2prep(wqi, bqi), wkiT=w2prep(wki, bki),
            wviT=w2prep(wvi, bvi)))
    r4o = run_bass_kernel_spmd(nc4, maps4, core_ids=cores).results

    nc5 = _get(build_stage5, "s5")
    khfull = [np.concatenate([r4o[b * 4 + j]["khtm"] for j in range(4)], axis=0)
              for b in range(2)]
    vhfull = [np.concatenate([r4o[b * 4 + j]["vhtm"] for j in range(4)], axis=0)
              for b in range(2)]
    khn4 = [np.stack([r4o[b * 4 + j]["khn"][:, 0] for j in range(4)], axis=0)
            for b in range(2)]
    owT = _ext(np.asarray(inputs["L1_attn_out_w"]),
               np.asarray(inputs["L1_attn_out_b"]))
    maps5 = []
    for c in cores:
        b = c // 4
        qbase = (c % 4) * TPC
        kill = np.zeros((NQG, SEQ // P), dtype=np.float32)
        lim = (SEQ - qbase) // P
        for qg in range(NQG):
            for kt in range(SEQ // P):
                if qg * 4 + 4 <= kt < lim:
                    kill[qg, kt] = -1e30
        maps5.append(dict(
            qhfm=r4o[c]["qhfm"],
            khrot=np.ascontiguousarray(np.roll(khfull[b], -qbase, axis=0)),
            vhrot=np.ascontiguousarray(np.roll(vhfull[b], -qbase, axis=0)),
            x1=r3o[c]["x1"], khn4=khn4[b], killb=kill, owT=owT))
    r5o = run_bass_kernel_spmd(nc5, maps5, core_ids=cores).results

    nc6 = _get(build_stage6, "s6")
    selT = np.ascontiguousarray(np.asarray(inputs["L1_ffn_expert_sel"]).T,
                                dtype=np.float32)
    keys = np.asarray(inputs["L1_ffn_keys"])
    values = np.asarray(inputs["L1_ffn_values"])
    maps6 = []
    for c in cores:
        m = dict(x1nfm=r4o[c]["x1nfm"], r3=r5o[c]["r3"], selT=selT)
        for eg in range(32):
            m[f"keys_{eg}"] = np.ascontiguousarray(keys[eg])
            m[f"values_{eg}"] = np.ascontiguousarray(values[eg])
        maps6.append(m)
    r6o = run_bass_kernel_spmd(nc6, maps6, core_ids=cores).results

    out = np.concatenate([r6o[c]["out"] for c in cores], axis=0)
    return out.reshape(2, SEQ, D)


# version 19 (final)
# speedup vs baseline: 1.0172x; 1.0172x over previous
import numpy as np
import ml_dtypes
import concourse.bass as bass
import concourse.mybir as mybir
from concourse.bass_utils import run_bass_kernel_spmd

F32 = mybir.dt.float32
BF16 = mybir.dt.bfloat16
AF = mybir.ActivationFunctionType
ALU = mybir.AluOpType
AX = mybir.AxisListType

P = 128
D = 1024
NCH = D // P
TPC = 2048
TT = TPC // P
NQG = TPC // 512
NCORES = 8
SEQ = 8192
EPS_LN = 1e-5
EPS_RMS = float(np.finfo(np.float32).eps)

ENGINES = ("sync", "gpsimd", "vector", "scalar", "tensor")


class Sched:

    def __init__(self, nc):
        self.nc = nc
        self.sem = {}
        self.count = {}
        self._fenced = {}
        self._owner = {}
        for e in ENGINES:
            self._key(e, e)
        self.streams = {e: [] for e in ENGINES}
        self._nofence = set()

    def _key(self, key, owner):
        if key not in self.sem:
            self.sem[key] = self.nc.alloc_semaphore(name=f"s_{key}")
            self.count[key] = 0
            self._fenced[key] = 0
            self._owner[key] = owner
        return key

    def to(self, ename, fn, waits=(), inc=0, serialize=None, ch=None,
           fence=True):
        if serialize is None:
            serialize = ename in ("vector", "scalar", "gpsimd")
        wl = list(waits)
        if serialize:
            if self.count[ename] > 0:
                wl.append((self.sem[ename], self.count[ename]))
            inc = max(inc, 1)
        key = self._key(ch, ename) if ch else ename
        if not fence:
            self._nofence.add(key)
        mark = None
        if inc:
            self.count[key] += inc
            mark = (self.sem[key], self.count[key])
        self.streams[ename].append((tuple(wl), fn, inc, self.sem[key]))
        return mark

    def dma(self, ename, out, in_, waits=(), ch=None, fn=None, fence=True):
        key = self._key(ch, ename) if ch else self._key(ename + ":d", ename)
        if not fence:
            self._nofence.add(key)
        if fn is None:
            fn = lambda e: e.dma_start(out, in_)
        self.count[key] += 16
        mark = (self.sem[key], self.count[key])
        self.streams[ename].append(((tuple(waits)), fn, 16, self.sem[key]))
        return mark

    def barrier(self):
        for key, owner in list(self._owner.items()):
            if key in self._nofence:
                continue
            if self.count[key] > self._fenced[key]:
                sem, val = self.sem[key], self.count[key]
                self.streams[owner].append(
                    ((), lambda eng, sem=sem, val=val: eng.wait_ge(sem, val) and None, 0, sem))
                self._fenced[key] = val
        nc = self.nc
        streams = self.streams
        self.streams = {e: [] for e in ENGINES}
        active = [e for e in ENGINES if streams[e]]
        if not active:
            return
        with nc.Block() as blk:
            for e in active:
                def make(ops):
                    def emit(eng):
                        for waits, fn, inc, sem in ops:
                            for (wsem, wval) in waits:
                                eng.wait_ge(wsem, wval)
                            instr = fn(eng)
                            if inc:
                                instr.then_inc(sem, inc, skip_validation=True)
                    return emit
                getattr(blk, e)(make(streams[e]))


def tile_view(dram, i, rows=P, cols=None):
    ap = dram.ap().rearrange("(n p) c -> n p c", p=rows)
    return ap[i]



def emit_l0_front(nc, S, dr):
    id32 = nc.inline_tensor(np.eye(P, dtype=np.float32), "id32")
    idbf = nc.inline_tensor(np.eye(P, dtype=ml_dtypes.bfloat16), "idbf")

    from contextlib import ExitStack
    with ExitStack() as ctx:
        def sb(name, shape, dt):
            return ctx.enter_context(nc.sbuf_tensor(name, shape, dt))

        def pt(name, shape, dt):
            return ctx.enter_context(nc.psum_tensor(name, shape, dt))

        x_sb = sb("x_sb", [P, TT, D], F32)
        xnfm = sb("xnfm_s1", [P, NCH, TPC], BF16)
        w3 = sb("w3", [P, 3, NCH, D], BF16)
        w3b = sb("w3b", [1, 3, D], BF16)
        ones_bf = sb("ones_bf", [1, TPC], BF16)
        idbf_sb = sb("idbf_sb", [P, P], BF16)
        id32_sb = sb("id32_sb", [P, P], F32)
        xn_t = sb("xn_t", [P, 2, D], F32)
        r1_t = sb("r1_t", [P, 2, D], F32)
        xnb = sb("xnb", [P, 2, D], BF16)
        xsq = sb("xsq", [P, D], F32)
        st = sb("st", [P, 8], F32)
        qstage = sb("qstage", [P, 2, TPC], BF16)
        kvstage = sb("kvstage", [P, 2, D], BF16)
        ksq = sb("ksq", [P, D], F32)
        knall = sb("knall", [P, TT], F32)
        knfin = sb("knfin", [1, 4], F32)
        ceps = sb("ceps", [P, 1], F32)
        pmm0 = pt("pmm0", [P, 512], F32)
        pmm1 = pt("pmm1", [P, 512], F32)
        pmm2 = pt("pmm2", [P, 512], F32)
        pmm3 = pt("pmm3", [P, 512], F32)
        ptr0 = pt("ptr0", [P, P], BF16)
        ptr1 = pt("ptr1", [P, P], BF16)
        ptr32 = pt("ptr32", [P, P], F32)
        pmm = [pmm0, pmm1, pmm2, pmm3]
        ptr = [ptr0, ptr1]

        S.dma("sync", x_sb[:, :, :], dr["x"].ap().rearrange("(n p) c -> p n c", p=P))
        for wi, nm in enumerate(("wqT", "wkT", "wvT")):
            wd = dr[nm]
            S.dma("gpsimd", w3[:, wi, :, :],
                  wd.ap()[0:D, :].rearrange("(c p) n -> p c n", p=P))
            S.dma("gpsimd", w3b[:, wi, :], wd.ap()[D:D + 1, :])
        S.dma("sync", idbf_sb[:, :], idbf[:, :])
        S.dma("sync", id32_sb[:, :], id32[:, :])
        S.to("vector", lambda e: e.memset(ones_bf[:, :], 1.0))
        S.to("vector", lambda e: e.memset(ceps[:, :], EPS_LN))
        S.barrier()

        dma_xn = {}
        dma_r1 = {}
        evac_marks = {}
        xn_marks = {}
        tpose_marks = {}
        s1 = st[:, 0:1]; s2 = st[:, 1:2]; m_ = st[:, 2:3]; ms = st[:, 3:4]
        var = st[:, 4:5]; std = st[:, 5:6]; rstd = st[:, 6:7]; nb = st[:, 7:8]
        for i in range(TT):
            par = i % 2
            x_t = x_sb[:, i, :]
            S.to("vector", lambda e, x_t=x_t: e.reduce_sum(s1, x_t, axis=AX.X))
            msq = S.to("scalar", lambda e, x_t=x_t: e.activation(
                xsq[:, :], x_t, AF.Square, accum_out=s2), inc=1)
            S.to("vector", lambda e: e.tensor_scalar(m_, s1, 1.0 / D, None, ALU.mult))
            S.to("vector", lambda e: e.tensor_scalar(ms, s2, 1.0 / D, None, ALU.mult),
                 waits=[msq])
            S.to("vector", lambda e: e.tensor_scalar(s1, m_, m_, None, ALU.mult))
            mvar = S.to("vector", lambda e: e.tensor_scalar(var, ms, s1, None, ALU.subtract), inc=1)
            mstd = S.to("scalar", lambda e: e.activation(std, var, AF.Sqrt, bias=ceps[:, 0:1]),
                        waits=[mvar], inc=1)
            S.to("vector", lambda e: e.reciprocal(rstd, std), waits=[mstd])
            mnb = S.to("vector", lambda e: e.tensor_scalar(nb, m_, rstd, -1.0,
                                                           ALU.mult, ALU.mult), inc=1)
            w_xn = [mnb]
            if i >= 2:
                w_xn.append(dma_xn[i - 2])
            mxn = S.to("scalar", lambda e, x_t=x_t, par=par: e.activation(
                xn_t[:, par, :], x_t, AF.Identity, bias=nb, scale=rstd),
                waits=w_xn, inc=1)
            xn_marks[i] = mxn
            dma_xn[i] = S.dma("sync", tile_view(dr["xn"], i), xn_t[:, par, :],
                              waits=[mxn], ch=f"xn{par}")
            w_r1 = [mxn]
            if i >= 2:
                w_r1.append(dma_r1[i - 2])
            mr1 = S.to("vector", lambda e, x_t=x_t, par=par: e.tensor_add(
                r1_t[:, par, :], x_t, xn_t[:, par, :]), waits=w_r1, inc=1)
            dma_r1[i] = S.dma("sync", tile_view(dr["r1"], i), r1_t[:, par, :],
                              waits=[mr1], ch=f"r1{par}")
            w_xb = [mxn]
            if i >= 2:
                w_xb.append(tpose_marks[(i - 2, NCH - 1)])
            mxb = S.to("vector", lambda e, par=par: e.tensor_copy(
                xnb[:, par, :], xn_t[:, par, :]), waits=w_xb, inc=1)
            for c in range(NCH):
                pslot = c % 2
                w_t = [mxb]
                if (i, c - 2) in evac_marks:
                    w_t.append(evac_marks[(i, c - 2)])
                elif (i - 1, c + NCH - 2) in evac_marks:
                    w_t.append(evac_marks[(i - 1, c + NCH - 2)])
                mt = S.to("tensor", lambda e, par=par, c=c, pslot=pslot: e.transpose(
                    ptr[pslot][:, :], xnb[:, par, c * P:(c + 1) * P], idbf_sb[:, :]),
                    waits=w_t, inc=1)
                tpose_marks[(i, c)] = mt
                evac_marks[(i, c)] = S.to("vector", lambda e, i=i, c=c, pslot=pslot:
                    e.tensor_copy(xnfm[:, c, i * P:(i + 1) * P], ptr[pslot][:, :]),
                    waits=[mt], inc=1)
        S.barrier()

        unit = 0
        evq = {}
        dmaq = {}
        for oc in range(NCH):
            for tg in range(NQG):
                ps = pmm[unit % 4]
                w_mm = []
                if unit >= 4:
                    w_mm.append(evq[unit - 4])
                for c in range(NCH):
                    mmk = S.to("tensor", lambda e, oc=oc, tg=tg, c=c, ps=ps: e.matmul(
                        ps[:, :], w3[:, 0, c, oc * P:(oc + 1) * P],
                        xnfm[:, c, tg * 512:(tg + 1) * 512],
                        start=(c == 0), stop=False),
                        waits=(w_mm if c == 0 else []))
                mlast = S.to("tensor", lambda e, oc=oc, tg=tg, ps=ps: e.matmul(
                    ps[:, :], w3b[:, 0, oc * P:(oc + 1) * P],
                    ones_bf[:, tg * 512:(tg + 1) * 512],
                    start=False, stop=True), inc=1)
                w_ev = [mlast]
                if oc >= 2 and tg == 0:
                    w_ev.append(dmaq[oc - 2])
                evq[unit] = S.to("vector", lambda e, oc=oc, tg=tg, ps=ps: e.tensor_copy(
                    qstage[:, oc % 2, tg * 512:(tg + 1) * 512], ps[:, :]),
                    waits=w_ev, inc=1)
                unit += 1
            dmaq[oc] = S.dma("sync",
                             dr["qfm"].ap().rearrange("p (c t) -> p c t", c=NCH)[:, oc, :],
                             qstage[:, oc % 2, :], waits=[evq[unit - 1]],
                             ch=f"q{oc % 2}")

        kv_unit = 0
        evkv = {}
        dmakv = {}
        knmarks = {}
        for which, wi, dname in (("k", 1, "ktm"), ("v", 2, "vtm")):
            for tt in range(TT):
                par = tt % 2
                for half in range(2):
                    ps = pmm[kv_unit % 4]
                    w_mm = []
                    if kv_unit >= 4:
                        w_mm.append(evkv[kv_unit - 4])
                    else:
                        w_mm.append(evq[unit - 4 + kv_unit])
                    if kv_unit // 2 >= 2 and half == 0:
                        w_mm.append(dmakv[kv_unit // 2 - 2])
                    for c in range(NCH):
                        S.to("tensor", lambda e, wi=wi, tt=tt, half=half, c=c, ps=ps:
                             e.matmul(ps[:, :], xnfm[:, c, tt * P:(tt + 1) * P],
                                      w3[:, wi, c, half * 512:(half + 1) * 512],
                                      start=(c == 0), stop=False),
                             waits=(w_mm if c == 0 else []))
                    mlast = S.to("tensor", lambda e, wi=wi, tt=tt, half=half, ps=ps:
                                 e.matmul(ps[:, :], ones_bf[:, tt * P:(tt + 1) * P],
                                          w3b[:, wi, half * 512:(half + 1) * 512],
                                          start=False, stop=True), inc=1)
                    w_ev = [mlast]
                    tvg = kv_unit // 2
                    if half == 0 and tvg >= 2 and (tvg - 2) in knmarks:
                        w_ev.append(knmarks[tvg - 2])
                    evkv[kv_unit] = S.to("vector", lambda e, par=par, half=half, ps=ps:
                        e.tensor_copy(kvstage[:, par, half * 512:(half + 1) * 512],
                                      ps[:, :]), waits=w_ev, inc=1)
                    kv_unit += 1
                dmakv[kv_unit // 2 - 1] = S.dma(
                    "sync", tile_view(dr[dname], tt), kvstage[:, par, :],
                    waits=[evkv[kv_unit - 1]], ch=f"kv{par}")
                if which == "k":
                    knmarks[tt] = S.to("scalar", lambda e, par=par, tt=tt: e.activation(
                        ksq[:, :], kvstage[:, par, :], AF.Square,
                        accum_out=knall[:, tt:tt + 1]),
                        waits=[evkv[kv_unit - 1]], inc=1)
        S.dma("gpsimd", dr["xnfm"].ap().rearrange("p (c t) -> p c t", c=NCH),
              xnfm[:, :, :], waits=[evac_marks[(TT - 1, NCH - 1)]])
        mred = S.to("vector", lambda e: e.tensor_reduce(
            st[:, 0:1], knall[:, :], AX.X, ALU.max),
            waits=[knmarks[TT - 1]], inc=1)
        mtr = S.to("tensor", lambda e: e.transpose(ptr32[0:1, :], st[:, 0:1], id32_sb[:, :]),
                   waits=[mred], inc=1)
        mred2 = S.to("vector", lambda e: e.tensor_reduce(
            knfin[:, 0:1], ptr32[0:1, 0:P], AX.X, ALU.max), waits=[mtr], inc=1)
        msqr = S.to("scalar", lambda e: e.activation(knfin[:, 1:2], knfin[:, 0:1], AF.Sqrt),
                    waits=[mred2], inc=1)
        mend = S.dma("sync", dr["knmax"].ap(), knfin[:, 1:2], waits=[msqr])
        S.to("sync", lambda e: e.wait_ge(mend[0], mend[1]))
        S.barrier()


def build_stage1():
    nc = bass.Bass("TRN2", target_bir_lowering=False)
    dr = {}
    dr["x"] = nc.dram_tensor("x", [TPC, D], F32, kind="ExternalInput")
    dr["wqT"] = nc.dram_tensor("wqT", [D + 1, D], F32, kind="ExternalInput")
    dr["wkT"] = nc.dram_tensor("wkT", [D + 1, D], F32, kind="ExternalInput")
    dr["wvT"] = nc.dram_tensor("wvT", [D + 1, D], F32, kind="ExternalInput")
    dr["xn"] = nc.dram_tensor("xn", [TPC, D], F32, kind="ExternalOutput")
    dr["r1"] = nc.dram_tensor("r1", [TPC, D], F32, kind="ExternalOutput")
    dr["qfm"] = nc.dram_tensor("qfm", [P, NCH * TPC], BF16, kind="ExternalOutput")
    dr["xnfm"] = nc.dram_tensor("xnfm", [P, NCH * TPC], BF16, kind="ExternalOutput")
    dr["ktm"] = nc.dram_tensor("ktm", [TPC, D], BF16, kind="ExternalOutput")
    dr["vtm"] = nc.dram_tensor("vtm", [TPC, D], BF16, kind="ExternalOutput")
    dr["knmax"] = nc.dram_tensor("knmax", [1, 1], F32, kind="ExternalOutput")
    S = Sched(nc)
    emit_l0_front(nc, S, dr)
    return nc



def emit_l0_attn(nc, S, dr):
    idbf = nc.inline_tensor(np.eye(P, dtype=ml_dtypes.bfloat16), "idbf2")
    KT = SEQ // 1024
    from contextlib import ExitStack
    with ExitStack() as ctx:
        def sb(name, shape, dt):
            return ctx.enter_context(nc.sbuf_tensor(name, shape, dt))

        def pt(name, shape, dt):
            return ctx.enter_context(nc.psum_tensor(name, shape, dt))

        qfm = sb("qfm_sb", [P, NCH, TPC], BF16)
        ptall = sb("ptall", [P, 64, 512], BF16)
        kfm = sb("kfm", [P, 2, NCH, 1024], BF16)
        vh = sb("vh", [P, 2, NCH, 512], BF16)
        zfm = sb("zfm", [P, NCH, 512], BF16)
        lb = sb("lb", [P, 512], F32)
        r1_t = sb("r1_t2", [P, 2, D], F32)
        r2_t = sb("r2_t2", [P, 2, D], F32)
        qsq = sb("qsq", [P, 2, 512], BF16)
        onesc = sb("onesc", [P, 1], BF16)
        onesr = sb("onesr", [1, P], BF16)
        bqbf = sb("bqbf", [1, NQG * 512], BF16)
        rows = sb("rows", [1, 8], F32)
        lrow = sb("lrow", [1, 512], F32)
        lrowb = sb("lrowb", [1, 512], BF16)
        knin = sb("knin", [1, 4], F32)
        idbf_sb = sb("idbf_sb2", [P, P], BF16)
        pst0 = pt("pst0", [P, 512], F32)
        pst1 = pt("pst1", [P, 512], F32)
        pz = [pt(f"pz{i}", [P, 512], F32) for i in range(4)]
        pl = pt("pl", [1, 512], F32)
        ptr_ = pt("ptrb", [P, P], BF16)
        pst = [pst0, pst1]

        S.dma("sync", qfm[:, :, :],
              dr["qfm"].ap().rearrange("p (c t) -> p c t", c=NCH))
        S.dma("sync", knin[:, :], dr["knmax4"].ap().rearrange("r c -> c r"))
        S.dma("sync", idbf_sb[:, :], idbf[:, :])
        S.to("vector", lambda e: e.memset(onesc[:, :], 1.0))
        S.to("vector", lambda e: e.memset(onesr[:, :], 1.0))
        S.barrier()

        S.to("vector", lambda e: e.reduce_max(rows[:, 0:1], knin[:, :], axis=AX.X))
        S.to("vector", lambda e: e.tensor_scalar(rows[:, 1:2], rows[:, 0:1],
                                                 -1.002, None, ALU.mult))
        mm_sq = {}
        sqrt_m = None
        mbq_prev = None
        for qg in range(NQG):
            for c in range(NCH):
                par = c % 2
                w_sq = []
                if (qg, c - 2) in mm_sq:
                    w_sq.append(mm_sq[(qg, c - 2)])
                elif (qg - 1, c + NCH - 2) in mm_sq:
                    w_sq.append(mm_sq[(qg - 1, c + NCH - 2)])
                msq = S.to("scalar", lambda e, qg=qg, c=c, par=par: e.activation(
                    qsq[:, par, :], qfm[:, c, qg * 512:(qg + 1) * 512], AF.Square),
                    waits=w_sq, inc=1)
                w_mm = [msq]
                if c == 0 and sqrt_m is not None:
                    w_mm.append(sqrt_m)
                mm_sq[(qg, c)] = S.to("tensor", lambda e, par=par, c=c: e.matmul(
                    pst0[0:1, :], onesc[:, :], qsq[:, par, :],
                    start=(c == 0), stop=(c == NCH - 1)),
                    waits=w_mm, inc=1)
            w_sqrt = [mm_sq[(qg, NCH - 1)]]
            if mbq_prev is not None:
                w_sqrt.append(mbq_prev)
            sqrt_m = S.to("scalar", lambda e: e.activation(
                lrow[:, :], pst0[0:1, :], AF.Sqrt),
                waits=w_sqrt, inc=1)
            mbq_prev = S.to("vector", lambda e, qg=qg: e.tensor_scalar(
                bqbf[:, qg * 512:(qg + 1) * 512], lrow[:, :], rows[:, 1:2],
                None, ALU.mult), waits=[sqrt_m], inc=1)
        S.barrier()

        ex_marks = {}
        phase_end = []
        zn_marks = {}
        pe_kt = {}
        tr_add = {}
        dmar2 = {}
        mrec_prev = None
        for qg in range(NQG):
            my_phase_end = []
            for kt in range(KT):
                par = kt % 2
                ldm = []
                for c in range(NCH):
                    w_ld = []
                    if (qg, (kt - 2) * 8 + 7) in ex_marks:
                        w_ld.append(ex_marks[(qg, (kt - 2) * 8 + 7)])
                    elif (qg - 1, (kt + KT - 2) * 8 + 7) in ex_marks:
                        w_ld.append(ex_marks[(qg - 1, (kt + KT - 2) * 8 + 7)])
                    ldm.append(S.dma(
                        "sync", None, None, ch=f"kf{par}{c}", waits=w_ld,
                        fn=lambda e, par=par, c=c, kt=kt: e.dma_start(
                            kfm[:, par, c, :],
                            dr["kfull"].ap()[kt * 1024:(kt + 1) * 1024,
                                             c * P:(c + 1) * P],
                            transpose=True)))
                for sub in range(8):
                    ktsub = kt * 8 + sub
                    ps = pst[ktsub % 2]
                    w_mm = list(ldm) if sub == 0 else []
                    if (qg, ktsub - 2) in ex_marks:
                        w_mm.append(ex_marks[(qg, ktsub - 2)])
                    elif (qg - 1, 62 + ktsub) in ex_marks:
                        w_mm.append(ex_marks[(qg - 1, 62 + ktsub)])
                    if qg > 0 and ktsub < 2:
                        w_mm += phase_end
                    for c in range(NCH):
                        S.to("tensor", lambda e, par=par, c=c, sub=sub, qg=qg, ps=ps:
                             e.matmul(ps[:, :], kfm[:, par, c, sub * P:(sub + 1) * P],
                                      qfm[:, c, qg * 512:(qg + 1) * 512],
                                      start=(c == 0), stop=False),
                             waits=(w_mm if c == 0 else []))
                    mst = S.to("tensor", lambda e, qg=qg, ps=ps: e.matmul(
                        ps[:, :], onesr[:, :],
                        bqbf[:, qg * 512:(qg + 1) * 512],
                        start=False, stop=True), inc=1)
                    ex_marks[(qg, ktsub)] = S.to(
                        "scalar", lambda e, ktsub=ktsub, ps=ps: e.activation(
                            ptall[:, ktsub, :], ps[:, :], AF.Exp, scale=1.0 / 32.0),
                        waits=[mst], inc=1)
            mlbe = None
            for half in range(2):
                pv_done = None
                for kt in range(KT):
                    par = kt % 2
                    w_v = []
                    prev_pv = (qg, half, kt - 2)
                    if half == 1 and kt < 2:
                        prev_pv = (qg, 0, kt + KT - 2)
                    elif half == 0 and kt < 2:
                        prev_pv = (qg - 1, 1, kt + KT - 2)
                    if prev_pv in pe_kt:
                        w_v.append(pe_kt[prev_pv])
                    mv = S.dma("gpsimd", vh[:, par, :, :],
                               dr["vfull"].ap()[kt * 1024:(kt + 1) * 1024,
                                                half * 512:(half + 1) * 512]
                               .rearrange("(s p) d -> p s d", p=P),
                               ch=f"vh{par}", waits=w_v)
                    for s in range(8):
                        first = (kt == 0 and s == 0)
                        last = (kt == KT - 1 and s == 7)
                        w_mm = [mv, ex_marks[(qg, kt * 8 + 7)]]
                        if first:
                            w_mm += [zn_marks[k] for k in
                                     [((qg, half - 1, ds) if half == 1 else
                                       (qg - 1, 1, ds)) for ds in range(4)]
                                     if k in zn_marks]
                        for ds in range(4):
                            mm = S.to("tensor", lambda e, par=par, s=s, ds=ds, kt=kt,
                                      first=first, last=last:
                                      e.matmul(pz[ds][:, :],
                                               vh[:, par, s, ds * P:(ds + 1) * P],
                                               ptall[:, kt * 8 + s, :],
                                               start=first, stop=last),
                                      waits=(w_mm if ds == 0 else []),
                                      inc=(1 if (s == 7 and ds == 3) else 0))
                            if s == 7 and ds == 3:
                                pe_kt[(qg, half, kt)] = mm
                            if last:
                                pv_done = mm
                        if half == 0:
                            w_l = []
                            if kt == 0 and s == 0 and mrec_prev is not None:
                                w_l.append(mrec_prev)
                            ml = S.to("tensor", lambda e, s=s, kt=kt: e.matmul(
                                pl[:, :], onesc[:, :], ptall[:, kt * 8 + s, :],
                                start=(kt == 0 and s == 0),
                                stop=(kt == KT - 1 and s == 7)),
                                waits=w_l,
                                inc=(1 if (kt == KT - 1 and s == 7) else 0))
                            if kt == KT - 1 and s == 7:
                                l_done = ml
                if half == 0:
                    mrec = S.to("vector", lambda e: e.reciprocal(lrow[:, :], pl[0:1, :]),
                                waits=[l_done], inc=1)
                    mrec_prev = mrec
                    mcv = S.to("vector", lambda e: e.tensor_copy(lrowb[:, :], lrow[:, :]),
                               inc=1)
                    w_lb = [mcv, ex_marks[(qg, 63)]]
                    mlb = S.to("tensor", lambda e: e.matmul(
                        pst1[:, :], onesr[:, :], lrowb[:, :], start=True, stop=True),
                        waits=w_lb, inc=1)
                    mlbe = S.to("vector", lambda e: e.tensor_copy(lb[:, :], pst1[:, :]),
                                waits=[mlb], inc=1)
                for ds in range(4):
                    sl = half * 4 + ds
                    w_zn = [pv_done, mlbe]
                    if (qg - 1, "tr", sl) in zn_marks:
                        w_zn.append(zn_marks[(qg - 1, "tr", sl)])
                    zn_marks[(qg, half, ds)] = S.to(
                        "vector", lambda e, sl=sl, ds=ds: e.tensor_tensor(
                            zfm[:, sl, :], pz[ds][:, :], lb[:, :], ALU.mult),
                        waits=w_zn, inc=1)
            phase_end = [mlbe, zn_marks[(qg, 1, 3)]]
            prev_add = None
            for qt in range(4):
                par = qt % 2
                ti = qg * 4 + qt
                w_r1 = [dmar2[ti - 2]] if (ti - 2) in dmar2 else []
                mr1 = S.dma("sync", r1_t[:, par, :], tile_view(dr["r1"], ti),
                            ch=f"r1{par}", waits=w_r1)
                for ds in range(NCH):
                    w_t = [zn_marks[(qg, ds // 4, ds % 4)]]
                    if prev_add is not None:
                        w_t.append(prev_add)
                    mt = S.to("tensor", lambda e, ds=ds, qt=qt: e.transpose(
                        ptr_[:, :], zfm[:, ds, qt * P:(qt + 1) * P], idbf_sb[:, :]),
                        waits=w_t, inc=1)
                    prev_add = S.to(
                        "vector", lambda e, ds=ds, par=par: e.scalar_tensor_tensor(
                            r2_t[:, par, ds * P:(ds + 1) * P], ptr_[:, :], 1.0,
                            r1_t[:, par, ds * P:(ds + 1) * P], ALU.mult, ALU.add),
                        waits=[mt, mr1], inc=1)
                    if qt == 3:
                        zn_marks[(qg, "tr", ds)] = prev_add
                dmar2[ti] = S.dma("sync", tile_view(dr["r2"], ti), r2_t[:, par, :],
                                  ch=f"r2{par}", waits=[prev_add])
        S.barrier()


def build_stage2():
    nc = bass.Bass("TRN2", target_bir_lowering=False)
    dr = {}
    dr["qfm"] = nc.dram_tensor("qfm", [P, NCH * TPC], BF16, kind="ExternalInput")
    dr["kfull"] = nc.dram_tensor("kfull", [SEQ, D], BF16, kind="ExternalInput")
    dr["vfull"] = nc.dram_tensor("vfull", [SEQ, D], BF16, kind="ExternalInput")
    dr["r1"] = nc.dram_tensor("r1", [TPC, D], F32, kind="ExternalInput")
    dr["knmax4"] = nc.dram_tensor("knmax4", [4, 1], F32, kind="ExternalInput")
    dr["r2"] = nc.dram_tensor("r2", [TPC, D], F32, kind="ExternalOutput")
    S = Sched(nc)
    emit_l0_attn(nc, S, dr)
    return nc



def emit_l0_ffn(nc, S, dr):
    NE = 8
    from contextlib import ExitStack
    with ExitStack() as ctx:
        def sb(name, shape, dt):
            return ctx.enter_context(nc.sbuf_tensor(name, shape, dt))

        def pt(name, shape, dt):
            return ctx.enter_context(nc.psum_tensor(name, shape, dt))

        xnfm = sb("xnfm3", [P, NCH, TPC], BF16)
        hfm = xnfm
        w1sb = sb("w1sb", [P, NCH, 2048], BF16)
        w1row = sb("w1row", [1, 2048], BF16)
        w2sb = sb("w2sb", [P, NCH, D], BF16)
        w2row = sb("w2row", [1, D], BF16)
        gsb = sb("gsb", [P, NCH, NE], BF16)
        grow = sb("grow", [1, NE], BF16)
        nwsb = sb("nwsb", [P, NCH], F32)
        outacc = sb("outacc", [P, TT, D], F32)
        acts = sb("acts", [P, NCH, 512], BF16)
        onesr = sb("onesr3", [1, TPC], BF16)
        onesc = sb("onesc3", [P, 1], BF16)
        rrow = sb("rrow", [1, 512], F32)
        rrowb = sb("rrowb", [1, 512], BF16)
        rrb = sb("rrb", [P, TPC], BF16)
        xnsq = sb("xnsq", [P, 2, 512], BF16)
        lg = sb("lg", [P, 8], F32)
        stt = sb("stt", [P, 8], F32)
        wsel = sb("wsel", [P, TT, NE], F32)
        swg = sb("swg", [P, 3, 512], F32)
        r2_t = sb("r2_t3", [P, 2, D], F32)
        x1_t = r2_t
        ceps = sb("ceps3", [P, 1], F32)
        pml = [pt(f"pml{i}", [P, 512], F32) for i in range(4)]
        pg = pt("pg", [P, NE], F32)
        pn = pt("pn", [1, 512], F32)

        S.dma("sync", xnfm[:, :, :],
              dr["xnfm"].ap().rearrange("p (c t) -> p c t", c=NCH))
        S.dma("gpsimd", gsb[:, :, :],
              dr["gateT"].ap()[0:D, :].rearrange("(c p) n -> p c n", p=P))
        S.dma("gpsimd", grow[:, :], dr["gateT"].ap()[D:D + 1, :])
        for c in range(NCH):
            S.dma("gpsimd", nwsb[:, c:c + 1],
                  dr["normw"].ap().rearrange("(c p one) -> c p one", c=NCH, one=1)[c])
        S.to("vector", lambda e: e.memset(onesr[:, :], 1.0))
        S.to("vector", lambda e: e.memset(onesc[:, :], 1.0))
        S.to("vector", lambda e: e.memset(ceps[:, :], EPS_RMS))
        S.barrier()

        sqm = {}
        rr_done = {}
        for tg in range(NQG):
            for c in range(NCH):
                par = c % 2
                w_sq = []
                if (tg, c - 2) in sqm:
                    w_sq.append(sqm[(tg, c - 2)])
                elif (tg - 1, c + NCH - 2) in sqm:
                    w_sq.append(sqm[(tg - 1, c + NCH - 2)])
                msq = S.to("scalar", lambda e, tg=tg, c=c, par=par: e.activation(
                    xnsq[:, par, :], xnfm[:, c, tg * 512:(tg + 1) * 512], AF.Square),
                    waits=w_sq, inc=1)
                sqm[(tg, c)] = S.to("tensor", lambda e, par=par, c=c: e.matmul(
                    pn[:, :], onesc[:, :], xnsq[:, par, :],
                    start=(c == 0), stop=(c == NCH - 1)), waits=[msq], inc=1)
            msqr = S.to("scalar", lambda e: e.activation(
                rrow[:, :], pn[0:1, :], AF.Sqrt, scale=1.0 / D, bias=ceps[0:1, 0:1]),
                waits=[sqm[(tg, NCH - 1)]], inc=1)
            sqm[(tg, NCH)] = msqr
            mrc = S.to("vector", lambda e: e.reciprocal(rrow[:, :], rrow[:, :]),
                       waits=[msqr], inc=1)
            mcv = S.to("vector", lambda e: e.tensor_copy(rrowb[:, :], rrow[:, :]), inc=1)
            mmb = S.to("tensor", lambda e: e.matmul(
                pml[0][:, :], onesr[:, 0:P], rrowb[:, :], start=True, stop=True),
                waits=[mcv], inc=1)
            rr_done[tg] = S.to("vector", lambda e, tg=tg: e.tensor_copy(
                rrb[:, tg * 512:(tg + 1) * 512], pml[0][:, :]), waits=[mmb], inc=1)
        for qg in range(1, NQG):
            pass
        S.barrier()

        hm = {}
        for c in range(NCH):
            m1 = S.to("vector", lambda e, c=c: e.tensor_scalar(
                hfm[:, c, :], xnfm[:, c, :], nwsb[:, c:c + 1], None, ALU.mult), inc=1)
            hm[c] = S.to("vector", lambda e, c=c: e.tensor_tensor(
                hfm[:, c, :], hfm[:, c, :], rrb[:, :], ALU.mult), inc=1)
        S.barrier()

        m1_ = stt[:, 0:1]; m2_ = stt[:, 1:2]; dd = stt[:, 2:3]
        w1_ = stt[:, 3:4]; w2_ = stt[:, 4:5]
        msk1 = swg[:, 0, 0:NE]; msk2 = swg[:, 1, 0:NE]; lg2 = swg[:, 2, 0:NE]
        prev_ev = None
        for tt in range(TT):
            w_mm = [prev_ev] if prev_ev else []
            for c in range(NCH):
                S.to("tensor", lambda e, c=c, tt=tt: e.matmul(
                    pg[:, :], hfm[:, c, tt * P:(tt + 1) * P], gsb[:, c, :],
                    start=(c == 0), stop=False), waits=(w_mm if c == 0 else []))
            mgl = S.to("tensor", lambda e, tt=tt: e.matmul(
                pg[:, :], onesr[:, tt * P:(tt + 1) * P], grow[:, :],
                start=False, stop=True), inc=1)
            mev = S.to("vector", lambda e: e.tensor_copy(lg[:, :], pg[:, :]),
                       waits=[mgl], inc=1)
            prev_ev = mev
            S.to("vector", lambda e: e.reduce_max(m1_, lg[:, :], axis=AX.X))
            S.to("vector", lambda e: e.tensor_scalar(msk1, lg[:, :], m1_, None, ALU.is_ge))
            S.to("vector", lambda e: e.scalar_tensor_tensor(
                lg2, msk1, -1e30, lg[:, :], ALU.mult, ALU.add))
            S.to("vector", lambda e: e.reduce_max(m2_, lg2, axis=AX.X))
            S.to("vector", lambda e: e.tensor_scalar(msk2, lg2, m2_, None, ALU.is_ge))
            mdd = S.to("vector", lambda e: e.tensor_scalar(dd, m2_, m1_, None,
                                                           ALU.subtract), inc=1)
            msg = S.to("scalar", lambda e: e.activation(w1_, dd, AF.Sigmoid, scale=-1.0),
                       waits=[mdd], inc=1)
            S.to("vector", lambda e: e.tensor_scalar(w2_, w1_, -1.0, 1.0,
                                                     ALU.mult, ALU.add), waits=[msg])
            S.to("vector", lambda e: e.tensor_scalar(msk1, msk1, w1_, None, ALU.mult))
            S.to("vector", lambda e: e.tensor_scalar(msk2, msk2, w2_, None, ALU.mult))
            S.to("vector", lambda e, tt=tt: e.tensor_tensor(
                wsel[:, tt, :], msk1, msk2, ALU.add))
            S.to("vector", lambda e, tt=tt: e.memset(outacc[:, tt, :], 0.0))
        S.barrier()

        LIM = 7.0
        mlp2_done = None
        mp1_prev = None
        sw_prev_mark = None
        t1_done_prev = {}
        mlp2_pe_prev = None
        for ei in range(NE):
            wd1 = dr[f"w1T_{ei}"]
            wd2 = dr[f"w2T_{ei}"]
            lwb = {}
            for j in range(NCH):
                w_lw = [t1_done_prev[j]] if j in t1_done_prev else []
                lwb[j] = S.dma("gpsimd", None, None, ch=f"w1b{j}", waits=w_lw,
                               fn=lambda e, j=j, wd1=wd1: e.dma_start(
                                   w1sb[:, :, 2 * j * P:2 * (j + 1) * P],
                                   wd1.ap()[0:D, 2 * j * P:2 * (j + 1) * P]
                                   .rearrange("(c p) n -> p c n", p=P)))
            w_lw2 = [mlp2_pe_prev] if mlp2_pe_prev else []
            lw = [S.dma("gpsimd", w1row[:, :], wd1.ap()[D:D + 1, :], ch="w1r",
                        waits=[t1_done_prev[NCH]] if NCH in t1_done_prev else []),
                  S.dma("gpsimd", w2sb[:, :, :],
                        wd2.ap()[0:D, :].rearrange("(c p) n -> p c n", p=P), ch="w2",
                        waits=w_lw2),
                  S.dma("gpsimd", w2row[:, :], wd2.ap()[D:D + 1, :], ch="w2r",
                        waits=w_lw2)]
            last_acc = {}
            t1_next = {}
            for tg in range(NQG):
                sw_done = {}
                for j in range(NCH):
                    pglu = pml[(2 * j) % 4]
                    plin = pml[(2 * j + 1) % 4]
                    w_mm = ([lwb[j], lw[0]] if tg == 0 else [])
                    if (j - 2) in sw_done:
                        w_mm.append(sw_done[j - 2])
                    elif j == 0 and mlp2_done is not None:
                        w_mm.append(mlp2_done)
                    elif j == 1 and (tg > 0 or ei > 0):
                        w_mm.append(prev_sw[NCH - 1])
                    for fi, ps in ((2 * j, pglu), (2 * j + 1, plin)):
                        for c in range(NCH):
                            S.to("tensor", lambda e, c=c, fi=fi, tg=tg, ps=ps: e.matmul(
                                ps[:, :], w1sb[:, c, fi * P:(fi + 1) * P],
                                hfm[:, c, tg * 512:(tg + 1) * 512],
                                start=(c == 0), stop=False),
                                waits=(w_mm if c == 0 else []))
                        S.to("tensor", lambda e, fi=fi, tg=tg, ps=ps: e.matmul(
                            ps[:, :], w1row[:, fi * P:(fi + 1) * P],
                            onesr[:, tg * 512:(tg + 1) * 512],
                            start=False, stop=True),
                            inc=(1 if fi % 2 == 1 else 0))
                    mboth = S.count["tensor"]
                    mt1 = (S.sem["tensor"], mboth)
                    if tg == NQG - 1:
                        t1_next[j] = mt1
                    w_gm = [mt1]
                    if mp1_prev is not None:
                        w_gm.append(mp1_prev)
                    mgm = S.to("vector", lambda e, pglu=pglu: e.tensor_scalar(
                        swg[:, 0, :], pglu[:, :], LIM, None, ALU.min),
                        waits=w_gm, inc=1)
                    w_p1 = [mgm]
                    if sw_prev_mark is not None:
                        w_p1.append(sw_prev_mark)
                    mp1 = S.to("scalar", lambda e: e.activation(
                        swg[:, 1, :], swg[:, 0, :], AF.Gelu_apprx_sigmoid),
                        waits=w_p1, inc=1)
                    mp1_prev = mp1
                    w_lc = [mt1]
                    if sw_prev_mark is not None:
                        w_lc.append(sw_prev_mark)
                    mlc = S.to("vector", lambda e, plin=plin: e.tensor_scalar(
                        swg[:, 2, :], plin[:, :], LIM, -LIM, ALU.min, ALU.max),
                        waits=w_lc, inc=1)
                    w_act = [mp1, mlc]
                    if (j, tg, ei) != (0, 0, 0):
                        w_act.append(mlp2_done if j == 0 else sw_prev_mark)
                    sw_done[j] = S.to("vector", lambda e, j=j: e.scalar_tensor_tensor(
                        acts[:, j, :], swg[:, 2, :], 1.0, swg[:, 1, :],
                        ALU.add, ALU.mult), waits=w_act, inc=1)
                    sw_prev_mark = sw_done[j]
                prev_sw = sw_done
                for tq in range(4):
                    tt = tg * 4 + tq
                    for half in range(2):
                        ps = pml[half]
                        w_mm = [sw_done[NCH - 1]]
                        if tg == 0 and tq == 0 and half == 0:
                            w_mm += [lw[1], lw[2]]
                        if (tt - 1, half) in last_acc:
                            w_mm.append(last_acc[(tt - 1, half)])
                        for c in range(NCH):
                            S.to("tensor", lambda e, c=c, tq=tq, half=half, ps=ps:
                                 e.matmul(ps[:, :], acts[:, c, tq * P:(tq + 1) * P],
                                          w2sb[:, c, half * 512:(half + 1) * 512],
                                          start=(c == 0), stop=False),
                                 waits=(w_mm if c == 0 else []))
                        my2 = S.to("tensor", lambda e, tt=tt, half=half, ps=ps: e.matmul(
                            ps[:, :], onesr[:, tt * P:(tt + 1) * P],
                            w2row[:, half * 512:(half + 1) * 512],
                            start=False, stop=True), inc=1)
                        last_acc[(tt, half)] = S.to(
                            "vector", lambda e, tt=tt, half=half, ei=ei, ps=ps:
                            e.scalar_tensor_tensor(
                                outacc[:, tt, half * 512:(half + 1) * 512], ps[:, :],
                                wsel[:, tt, ei:ei + 1],
                                outacc[:, tt, half * 512:(half + 1) * 512],
                                ALU.mult, ALU.add), waits=[my2], inc=1)
                mlp2_done = last_acc[(tg * 4 + 3, 1)]
                if tg == NQG - 1:
                    mlp2_pe_prev = my2
            t1_next[NCH] = t1_next[NCH - 1]
            t1_done_prev = t1_next
        S.barrier()

        dmx = {}
        for tt in range(TT):
            par = tt % 2
            w_r2 = [dmx[tt - 2]] if (tt - 2) in dmx else []
            mr2 = S.dma("sync", r2_t[:, par, :], tile_view(dr["r2"], tt),
                        ch=f"r2l{par}", waits=w_r2)
            ma = S.to("vector", lambda e, tt=tt, par=par: e.tensor_add(
                r2_t[:, par, :], r2_t[:, par, :], outacc[:, tt, :]),
                waits=[mr2], inc=1)
            dmx[tt] = S.dma("sync", tile_view(dr["x1"], tt), x1_t[:, par, :],
                            ch=f"x1{par}", waits=[ma])
        S.barrier()


def build_stage3():
    nc = bass.Bass("TRN2", target_bir_lowering=False)
    dr = {}
    dr["xnfm"] = nc.dram_tensor("xnfm", [P, NCH * TPC], BF16, kind="ExternalInput")
    dr["r2"] = nc.dram_tensor("r2", [TPC, D], F32, kind="ExternalInput")
    dr["normw"] = nc.dram_tensor("normw", [D], F32, kind="ExternalInput")
    dr["gateT"] = nc.dram_tensor("gateT", [D + 1, 8], F32, kind="ExternalInput")
    for ei in range(8):
        dr[f"w1T_{ei}"] = nc.dram_tensor(f"w1T_{ei}", [D + 1, 2048], F32,
                                         kind="ExternalInput")
        dr[f"w2T_{ei}"] = nc.dram_tensor(f"w2T_{ei}", [D + 1, D], F32,
                                         kind="ExternalInput")
    dr["x1"] = nc.dram_tensor("x1", [TPC, D], F32, kind="ExternalOutput")
    S = Sched(nc)
    emit_l0_ffn(nc, S, dr)
    return nc



def emit_l1_front(nc, S, dr):
    idbf = nc.inline_tensor(np.eye(P, dtype=ml_dtypes.bfloat16), "idbf4")
    id32 = nc.inline_tensor(np.eye(P, dtype=np.float32), "id325")
    from contextlib import ExitStack
    with ExitStack() as ctx:
        def sb(name, shape, dt):
            return ctx.enter_context(nc.sbuf_tensor(name, shape, dt))

        def pt(name, shape, dt):
            return ctx.enter_context(nc.psum_tensor(name, shape, dt))

        xnfm = sb("xnfm4", [P, NCH, TPC], BF16)
        cosf = sb("cosf_sb", [P, 4, TPC], BF16)
        sinf = sb("sinf_sb", [P, 4, TPC], BF16)
        w1 = sb("w14", [P, NCH, D], BF16)
        w2 = sb("w24", [P, NCH, D], BF16)
        w2r = sb("w2r4", [1, D], BF16)
        ones_bf = sb("ones4", [1, TPC], BF16)
        idbf_sb = sb("idbf4_sb", [P, P], BF16)
        st = sb("st4", [P, 8], F32)
        stage = sb("stage4", [P, 2, TPC], BF16)
        kvstage = sb("kvst4", [P, 2, D], BF16)
        khsq = sb("khsq4", [P, D], F32)
        knh = sb("knh4", [P, 16], F32)
        knh2 = sb("knh24", [P, 16], F32)
        knfin = sb("knfin4", [16, 1], F32)
        ceps = sb("ceps4", [P, 1], F32)
        id32_sb = sb("id324", [P, P], F32)
        inner = ExitStack()
        x1_t = inner.enter_context(nc.sbuf_tensor("x1t4", [P, 2, D], F32))
        xn_t = inner.enter_context(nc.sbuf_tensor("xnt4", [P, 2, D], F32))
        xnb = inner.enter_context(nc.sbuf_tensor("xnb4", [P, 2, D], BF16))
        xsq = inner.enter_context(nc.sbuf_tensor("xsq4", [P, D], F32))
        pmm = [pt(f"pmm4{i}", [P, 512], F32) for i in range(4)]
        ptrb = [pt(f"ptrb4{i}", [P, P], BF16) for i in range(2)]
        ptr32 = pt("ptr324", [16, P], F32)

        S.dma("gpsimd", cosf[:, :, :],
              dr["cosf"].ap().rearrange("(c p) t -> p c t", p=P))
        S.dma("gpsimd", sinf[:, :, :],
              dr["sinf"].ap().rearrange("(c p) t -> p c t", p=P))
        S.dma("sync", idbf_sb[:, :], idbf[:, :])
        S.dma("sync", id32_sb[:, :], id32[:, :])
        S.to("vector", lambda e: e.memset(ones_bf[:, :], 1.0))
        S.to("vector", lambda e: e.memset(ceps[:, :], EPS_LN))
        S.to("vector", lambda e: e.memset(knh[:, :], 0.0))
        S.barrier()

        dma_xn = {}
        evac_marks = {}
        tpose_marks = {}
        s1 = st[:, 0:1]; s2 = st[:, 1:2]; m_ = st[:, 2:3]; ms = st[:, 3:4]
        var = st[:, 4:5]; std = st[:, 5:6]; rstd = st[:, 6:7]; nb = st[:, 7:8]
        ldx = {}
        for i in range(TT):
            par = i % 2
            w_ld = [tpose_marks[(i - 2, NCH - 1)]] if (i - 2, NCH - 1) in tpose_marks else []
            ldx[i] = S.dma("sync", x1_t[:, par, :], tile_view(dr["x1"], i),
                           ch=f"x1l{par}", waits=w_ld)
            x_t = x1_t[:, par, :]
            S.to("vector", lambda e, x_t=x_t: e.reduce_sum(s1, x_t, axis=AX.X),
                 waits=[ldx[i]])
            msq = S.to("scalar", lambda e, x_t=x_t: e.activation(
                xsq[:, :], x_t, AF.Square, accum_out=s2), waits=[ldx[i]], inc=1)
            S.to("vector", lambda e: e.tensor_scalar(m_, s1, 1.0 / D, None, ALU.mult))
            S.to("vector", lambda e: e.tensor_scalar(ms, s2, 1.0 / D, None, ALU.mult),
                 waits=[msq])
            S.to("vector", lambda e: e.tensor_scalar(s1, m_, m_, None, ALU.mult))
            mvar = S.to("vector", lambda e: e.tensor_scalar(var, ms, s1, None,
                                                            ALU.subtract), inc=1)
            mstd = S.to("scalar", lambda e: e.activation(std, var, AF.Sqrt,
                                                         bias=ceps[:, 0:1]),
                        waits=[mvar], inc=1)
            S.to("vector", lambda e: e.reciprocal(rstd, std), waits=[mstd])
            mnb = S.to("vector", lambda e: e.tensor_scalar(nb, m_, rstd, -1.0,
                                                           ALU.mult, ALU.mult), inc=1)
            mxn = S.to("scalar", lambda e, x_t=x_t, par=par: e.activation(
                xn_t[:, par, :], x_t, AF.Identity, bias=nb, scale=rstd),
                waits=[mnb], inc=1)
            w_xb = [mxn]
            if (i - 2, NCH - 1) in tpose_marks:
                w_xb.append(tpose_marks[(i - 2, NCH - 1)])
            mxb = S.to("vector", lambda e, par=par: e.tensor_copy(
                xnb[:, par, :], xn_t[:, par, :]), waits=w_xb, inc=1)
            for c in range(NCH):
                pslot = c % 2
                w_t = [mxb]
                if (i, c - 2) in evac_marks:
                    w_t.append(evac_marks[(i, c - 2)])
                elif (i - 1, c + NCH - 2) in evac_marks:
                    w_t.append(evac_marks[(i - 1, c + NCH - 2)])
                mt = S.to("tensor", lambda e, par=par, c=c, pslot=pslot: e.transpose(
                    ptrb[pslot][:, :], xnb[:, par, c * P:(c + 1) * P], idbf_sb[:, :]),
                    waits=w_t, inc=1)
                tpose_marks[(i, c)] = mt
                evac_marks[(i, c)] = S.to("vector", lambda e, i=i, c=c, pslot=pslot:
                    e.tensor_copy(xnfm[:, c, i * P:(i + 1) * P], ptrb[pslot][:, :]),
                    waits=[mt], inc=1)
        S.dma("gpsimd", dr["x1nfm"].ap().rearrange("p (c t) -> p c t", c=NCH),
              xnfm[:, :, :], waits=[evac_marks[(TT - 1, NCH - 1)]])
        S.barrier()
        inner.close()
        yfm = ctx.enter_context(nc.sbuf_tensor("yfm", [P, NCH, TPC], BF16))
        pfm = ctx.enter_context(nc.sbuf_tensor("pfm", [P, NCH, TPC], BF16))
        tmp = ctx.enter_context(nc.sbuf_tensor("tmp4", [P, TPC], BF16))

        prev_consumed = None
        prev_chain_guards = {0: [], 1: []}
        for wi, (w1n, w2n) in enumerate((("wqT", "wqiT"), ("wkT", "wkiT"),
                                         ("wvT", "wviT"))):
            w_lw = [prev_consumed] if prev_consumed is not None else []
            lw = [S.dma("gpsimd", w1[:, :, :],
                        dr[w1n].ap().rearrange("(c p) n -> p c n", p=P), ch="w14c",
                        waits=w_lw),
                  S.dma("gpsimd", w2[:, :, :],
                        dr[w2n].ap()[0:D, :].rearrange("(c p) n -> p c n", p=P),
                        ch="w24c", waits=w_lw),
                  S.dma("gpsimd", w2r[:, :], dr[w2n].ap()[D:D + 1, :], ch="w24r",
                        waits=w_lw)]
            ev1 = {}
            unit = 0
            for oc in range(NCH):
                for tg in range(NQG):
                    ps = pmm[unit % 4]
                    w_mm = list(lw)
                    if unit >= 4:
                        w_mm.append(ev1[unit - 4])
                    if prev_consumed is not None:
                        w_mm.append(prev_consumed)
                    for c in range(NCH):
                        S.to("tensor", lambda e, oc=oc, tg=tg, c=c, ps=ps: e.matmul(
                            ps[:, :], w1[:, c, oc * P:(oc + 1) * P],
                            xnfm[:, c, tg * 512:(tg + 1) * 512],
                            start=(c == 0), stop=(c == NCH - 1)),
                            waits=(w_mm if c == 0 else []), inc=(1 if c == NCH - 1 else 0))
                    mlast = (S.sem["tensor"], S.count["tensor"])
                    ev1[unit] = S.to("vector", lambda e, oc=oc, tg=tg, ps=ps:
                        e.tensor_copy(pfm[:, oc, tg * 512:(tg + 1) * 512], ps[:, :]),
                        waits=[mlast], inc=1)
                    unit += 1
            prev_consumed = None
            rope_m = {}
            for j in range(4):
                pe_ = pfm[:, j, :]
                po_ = pfm[:, j + 4, :]
                cj = cosf[:, j, :]
                sj = sinf[:, j, :]
                S.to("vector", lambda e, pe_=pe_, cj=cj: e.tensor_tensor(
                    tmp[:, :], pe_, cj, ALU.mult), waits=([ev1[31]] if j == 0 else []))
                S.to("vector", lambda e, j=j, po_=po_, sj=sj: e.tensor_tensor(
                    yfm[:, j, :], po_, sj, ALU.mult))
                S.to("vector", lambda e, j=j: e.tensor_tensor(
                    yfm[:, j, :], yfm[:, j, :], tmp[:, :], ALU.add))
                S.to("vector", lambda e, po_=po_, cj=cj: e.tensor_tensor(
                    tmp[:, :], po_, cj, ALU.mult))
                S.to("vector", lambda e, j=j, pe_=pe_, sj=sj: e.tensor_tensor(
                    yfm[:, j + 4, :], pe_, sj, ALU.mult))
                rope_m[j] = S.to("vector", lambda e, j=j: e.tensor_tensor(
                    yfm[:, j + 4, :], tmp[:, :], yfm[:, j + 4, :], ALU.subtract),
                    inc=1)
            rope_done = rope_m[3]
            if wi == 0:
                ev2 = {}
                unit = 0
                dmaq = {}
                for oc in range(NCH):
                    for tg in range(NQG):
                        ps = pmm[unit % 4]
                        w_mm = [rope_done]
                        if unit >= 4:
                            w_mm.append(ev2[unit - 4])
                        for c in range(NCH):
                            S.to("tensor", lambda e, oc=oc, tg=tg, c=c, ps=ps: e.matmul(
                                ps[:, :], w2[:, c, oc * P:(oc + 1) * P],
                                yfm[:, c, tg * 512:(tg + 1) * 512],
                                start=(c == 0), stop=False),
                                waits=(w_mm if c == 0 else []))
                        mlast = S.to("tensor", lambda e, oc=oc, tg=tg, ps=ps: e.matmul(
                            ps[:, :], w2r[:, oc * P:(oc + 1) * P],
                            ones_bf[:, tg * 512:(tg + 1) * 512],
                            start=False, stop=True), inc=1)
                        w_ev = [mlast]
                        if oc >= 2 and tg == 0:
                            w_ev.append(dmaq[oc - 2])
                        ev2[unit] = S.to("vector", lambda e, oc=oc, tg=tg, ps=ps:
                            e.tensor_copy(stage[:, oc % 2, tg * 512:(tg + 1) * 512],
                                          ps[:, :]), waits=w_ev, inc=1)
                        unit += 1
                    dmaq[oc] = S.dma(
                        "sync",
                        dr["qhfm"].ap().rearrange("p (c t) -> p c t", c=NCH)[:, oc, :],
                        stage[:, oc % 2, :], waits=[ev2[unit - 1]], ch=f"q4{oc % 2}")
                prev_consumed = ev2[unit - 1]
            else:
                dname = "khtm" if wi == 1 else "vhtm"
                ev2 = {}
                dmakv = {}
                knmarks = {}
                kv_unit = 0
                for tt in range(TT):
                    par = tt % 2
                    for half in range(2):
                        ps = pmm[kv_unit % 4]
                        w_mm = [rope_done]
                        if kv_unit >= 4:
                            w_mm.append(ev2[kv_unit - 4])
                        if kv_unit // 2 >= 2 and half == 0:
                            w_mm.append(dmakv[kv_unit // 2 - 2])
                        elif kv_unit // 2 < 2 and half == 0:
                            w_mm += [mk for mk in prev_chain_guards[tt] if mk]
                        for c in range(NCH):
                            S.to("tensor", lambda e, tt=tt, half=half, c=c, ps=ps:
                                 e.matmul(ps[:, :], yfm[:, c, tt * P:(tt + 1) * P],
                                          w2[:, c, half * 512:(half + 1) * 512],
                                          start=(c == 0), stop=False),
                                 waits=(w_mm if c == 0 else []))
                        mlast = S.to("tensor", lambda e, tt=tt, half=half, ps=ps:
                                     e.matmul(ps[:, :], ones_bf[:, tt * P:(tt + 1) * P],
                                              w2r[:, half * 512:(half + 1) * 512],
                                              start=False, stop=True), inc=1)
                        w_ev = [mlast]
                        tvg = kv_unit // 2
                        if half == 0 and (tvg - 2) in knmarks:
                            w_ev.append(knmarks[tvg - 2])
                        ev2[kv_unit] = S.to("vector",
                            lambda e, par=par, half=half, ps=ps: e.tensor_copy(
                                kvstage[:, par, half * 512:(half + 1) * 512], ps[:, :]),
                            waits=w_ev, inc=1)
                        kv_unit += 1
                    dmakv[kv_unit // 2 - 1] = S.dma(
                        "sync", tile_view(dr[dname], tt), kvstage[:, par, :],
                        waits=[ev2[kv_unit - 1]], ch=f"kv4{par}")
                    if wi == 1:
                        mk1 = S.to("scalar", lambda e, par=par: e.activation(
                            khsq[:, :], kvstage[:, par, :], AF.Square),
                            waits=[ev2[kv_unit - 1]], inc=1)
                        mk2 = S.to("vector", lambda e: e.tensor_reduce(
                            knh2[:, :],
                            khsq.ap().rearrange("p (h f) -> p h f", h=16),
                            AX.X, ALU.add), waits=[mk1], inc=1)
                        knmarks[tt] = S.to("vector", lambda e: e.tensor_tensor(
                            knh[:, :], knh[:, :], knh2[:, :], ALU.max),
                            waits=[mk2], inc=1)
                prev_consumed = ev2[kv_unit - 1]
                prev_chain_guards = {0: [dmakv[TT - 2], knmarks.get(TT - 2)],
                                     1: [dmakv[TT - 1], knmarks.get(TT - 1)]}
                if wi == 1:
                    mtr = S.to("tensor", lambda e: e.transpose(
                        ptr32[:, :], knh[:, :], id32_sb[:, :]),
                        waits=[knmarks[TT - 1]], inc=1)
                    mrd = S.to("vector", lambda e: e.reduce_max(
                        knfin[:, 0:1], ptr32[0:16, :], axis=AX.X),
                        waits=[mtr], inc=1)
                    msq = S.to("scalar", lambda e: e.activation(
                        knfin[:, 0:1], knfin[:, 0:1], AF.Sqrt), waits=[mrd], inc=1)
                    S.dma("sync", dr["khn"].ap(), knfin[:, 0:1], waits=[msq])
        S.barrier()


def build_stage4():
    nc = bass.Bass("TRN2", target_bir_lowering=False)
    dr = {}
    dr["x1"] = nc.dram_tensor("x1", [TPC, D], F32, kind="ExternalInput")
    dr["cosf"] = nc.dram_tensor("cosf", [512, TPC], F32, kind="ExternalInput")
    dr["sinf"] = nc.dram_tensor("sinf", [512, TPC], F32, kind="ExternalInput")
    for nm in ("wqT", "wkT", "wvT"):
        dr[nm] = nc.dram_tensor(nm, [D, D], F32, kind="ExternalInput")
    for nm in ("wqiT", "wkiT", "wviT"):
        dr[nm] = nc.dram_tensor(nm, [D + 1, D], F32, kind="ExternalInput")
    dr["x1nfm"] = nc.dram_tensor("x1nfm", [P, NCH * TPC], BF16, kind="ExternalOutput")
    dr["qhfm"] = nc.dram_tensor("qhfm", [P, NCH * TPC], BF16, kind="ExternalOutput")
    dr["khtm"] = nc.dram_tensor("khtm", [TPC, D], BF16, kind="ExternalOutput")
    dr["vhtm"] = nc.dram_tensor("vhtm", [TPC, D], BF16, kind="ExternalOutput")
    dr["khn"] = nc.dram_tensor("khn", [16, 1], F32, kind="ExternalOutput")
    S = Sched(nc)
    emit_l1_front(nc, S, dr)
    return nc



def emit_l1_attn(nc, S, dr):
    mask_np = np.zeros((P, 4, 512), dtype=ml_dtypes.bfloat16)
    for dsv in range(4):
        for iv in range(P):
            mask_np[iv, dsv, dsv * P + iv:] = 1.0
    maskc = nc.inline_tensor(np.ascontiguousarray(mask_np.reshape(P, 4 * 512)),
                             "maskc5")
    NKT = SEQ // P
    from contextlib import ExitStack
    with ExitStack() as ctx:
        def sb(name, shape, dt):
            return ctx.enter_context(nc.sbuf_tensor(name, shape, dt))

        qhfm = sbm("qhfm5", [P, NCH, TPC], BF16)
        khp = sbm("khp5", [P, SEQ], BF16)
        vhs = sbm("vhs5", [P, NKT, 130], BF16)
        pts = sbm("pts5", [P, 2, 2, 512], BF16)
        ofm = sb("ofm5", [P, NCH, TPC], BF16)
        masksb = sbm("masksb5", [P, 4, 512], BF16)
        onesr = sb("onesr5", [1, P], BF16)
        ind = sb("ind5", [P, 2], BF16)
        qsq = sbm("qsq5", [P, TPC], BF16)
        btab = sb("btab5", [1, 8, NQG, NKT], F32)
        btabb = sb("btabb5", [1, 8, NQG, NKT], BF16)
        biast = sb("biast5", [P, NQG, NKT], F32)
        rows5 = sb("rows5", [1, 16], F32)
        kmaxh = sb("kmaxh5", [1, 16], F32)
        khn_in = sb("khn_in5", [1, 64], F32)
        kill_in = sb("kill_in5", [1, NQG, NKT], F32)
        lrowb = sb("lrowb5", [1, 512], BF16)
        lb_sb = sb("lb5", [64, 512], F32)
        ow = sb("ow5", [P, NCH, D], BF16)
        owr = sb("owr5", [1, D], BF16)
        x1_t = sb("x1t5", [P, 2, D], F32)
        r3_t = sb("r3t5", [P, 2, D], F32)
        ones2k = sb("ones2k5", [1, TPC], BF16)

        S.dma("sync", qhfm[:, :, :],
              dr["qhfm"].ap().rearrange("p (c t) -> p c t", c=NCH))
        S.dma("sync", masksb[:, :, :],
              maskc.ap().rearrange("p (d f) -> p d f", d=4))
        if agnh is not None:
            S.to("sync", lambda e: e.wait_ge(agnh[0], agnh[1]))
        S.dma("sync", khn_in[:, :], dr["khn4"].ap().rearrange("r c -> (r c)").unsqueeze(0))
        S.dma("sync", kill_in[:, :, :], dr["killb"].ap().unsqueeze(0))
        S.dma("gpsimd", ow[:, :, :],
              dr["owT"].ap()[0:D, :].rearrange("(c p) n -> p c n", p=P))
        S.dma("gpsimd", owr[:, :], dr["owT"].ap()[D:D + 1, :])
        S.to("vector", lambda e: e.memset(onesr[:, :], 1.0))
        S.to("vector", lambda e: e.memset(ones2k[:, :], 1.0))
        S.to("vector", lambda e: e.memset(ind[:, :], 0.0))
        S.to("vector", lambda e: e.memset(ind[0:64, 0:1], 1.0))
        S.to("vector", lambda e: e.memset(ind[64:128, 1:2], 1.0))
        S.barrier()
        rotk = emit_rotate(nc, S, dr["khfull"], dr["khrot"], [agkh])
        rotv = emit_rotate(nc, S, dr["vhfull"], dr["vhrot"], [agvh])

        S.to("vector", lambda e: e.tensor_max(kmaxh[:, :], khn_in[:, 0:16],
                                              khn_in[:, 16:32]))
        S.to("vector", lambda e: e.tensor_max(kmaxh[:, :], kmaxh[:, :],
                                              khn_in[:, 32:48]))
        S.to("vector", lambda e: e.tensor_max(kmaxh[:, :], kmaxh[:, :],
                                              khn_in[:, 48:64]))

        inner = ExitStack()
        pn = inner.enter_context(nc.psum_tensor("pn5", [1, 512], F32))
        sq_m = {}
        for hp in range(NCH):
            msq = S.to("scalar", lambda e, hp=hp: e.activation(
                qsq, qhfm[:, hp, :], AF.Square), inc=1)
            for qg in range(NQG):
                vmax = rows5[0:1, 0:1]
                vsq = rows5[0:1, 1:2]
                vpair = rows5[0:1, 2:3]
                for h2 in range(2):
                    w_mm = [msq]
                    if sq_m:
                        w_mm.append(last_rd)
                    mn = S.to("tensor", lambda e, hp=hp, qg=qg, h2=h2: e.matmul(
                        pn[:, :], ind[:, h2:h2 + 1],
                        qsq[:, qg * 512:(qg + 1) * 512],
                        start=True, stop=True), waits=w_mm, inc=1)
                    last_rd = S.to("vector", lambda e: e.reduce_max(
                        vmax, pn[0:1, :], axis=AX.X), waits=[mn], inc=1)
                    sq_m[(hp, qg, h2)] = last_rd
                    msqr = S.to("scalar", lambda e: e.activation(
                        vsq, vmax, AF.Sqrt), waits=[last_rd], inc=1)
                    if h2 == 0:
                        S.to("vector", lambda e, hp=hp: e.tensor_scalar(
                            vpair, vsq, kmaxh[0:1, 2 * hp:2 * hp + 1], None,
                            ALU.mult), waits=[msqr])
                    else:
                        S.to("vector", lambda e, hp=hp: e.tensor_scalar(
                            vsq, vsq, kmaxh[0:1, 2 * hp + 1:2 * hp + 2], None,
                            ALU.mult), waits=[msqr])
                        S.to("vector", lambda e: e.tensor_max(vpair, vpair, vsq))
                mng = S.to("vector", lambda e: e.tensor_scalar(
                    vpair, vpair, -1.002 / 8.0, None, ALU.mult), inc=1)
                S.to("vector", lambda e, hp=hp, qg=qg: e.tensor_scalar(
                    btab[0:1, hp, qg, :], kill_in[0:1, qg, :], vpair, None,
                    ALU.add), waits=[mng])
        mcv = S.to("vector", lambda e: e.tensor_copy(btabb[:, :, :, :],
                                                     btab[:, :, :, :]), inc=1)
        S.barrier()
        inner.close()

        inner2 = ExitStack()
        pst = [inner2.enter_context(nc.psum_tensor(f"pst5{i}", [P, 2, 512], F32))
               for i in range(2)]
        pz = [inner2.enter_context(nc.psum_tensor(f"pz5{i}", [65, 512], F32))
              for i in range(2)]
        pb = inner2.enter_context(nc.psum_tensor("pb5", [64, 512], F32))
        pbias = inner2.enter_context(nc.psum_tensor("pbias5", [P, 256], F32))

        ex_marks = {}
        pv_marks = {}
        o_marks = {}
        lb_m = None
        last_hp_pe = None
        for hp in range(NCH):
            w_bb = [mcv]
            if lb_m is not None:
                w_bb.append(bias_rd)
            mbb = S.to("tensor", lambda e, hp=hp: e.matmul(
                pbias[:, :], onesr[:, :], btabb[0:1, hp, :, :], start=True,
                stop=True), waits=w_bb, inc=1)
            bias_ev = S.to("vector", lambda e: e.tensor_copy(
                biast.ap().rearrange("p q k -> p (q k)"), pbias[:, :]),
                waits=[mbb], inc=1)
            bias_rd = bias_ev
            w_kh = [last_hp_pe] if last_hp_pe else []
            mkh = S.dma("sync", None, None, ch="khp", waits=w_kh,
                        fn=lambda e, hp=hp: e.dma_start(
                            khp[:, :], dr["khrot"].ap()[:, hp * P:(hp + 1) * P],
                            transpose=True))
            mkb = mkh
            w_vh = [last_hp_pe] if last_hp_pe else []
            mv1 = S.dma("gpsimd", vhs[:, :, 0:64],
                        dr["vhrot"].ap()[:, hp * P:hp * P + 64]
                        .rearrange("(s p) d -> p s d", p=P), ch="vh5a", waits=w_vh)
            mv2 = S.dma("gpsimd", vhs[:, :, 65:129],
                        dr["vhrot"].ap()[:, hp * P + 64:(hp + 1) * P]
                        .rearrange("(s p) d -> p s d", p=P), ch="vh5b", waits=w_vh)
            mo1 = S.to("vector", lambda e: e.memset(vhs[:, :, 64:65], 1.0),
                       waits=[mv1, mv2], inc=1)
            vh_ready = [mv1, mv2, mo1,
                        S.to("vector", lambda e: e.memset(vhs[:, :, 129:130], 1.0),
                             inc=1)]
            for qg in range(NQG):
                for kt in range(NKT):
                    ps = pst[kt % 2]
                    slot = kt % 2
                    w_mm = []
                    if qg == 0 and kt == 0:
                        w_mm += [mkh, mkb, bias_ev]
                    if (hp, qg, kt - 2) in ex_marks:
                        w_mm.append(ex_marks[(hp, qg, kt - 2)])
                    elif (hp, qg - 1, kt + NKT - 2) in ex_marks:
                        w_mm.append(ex_marks[(hp, qg - 1, kt + NKT - 2)])
                    elif (hp - 1, NQG - 1, kt + NKT - 2) in ex_marks:
                        w_mm.append(ex_marks[(hp - 1, NQG - 1, kt + NKT - 2)])
                    S.to("tensor", lambda e, hp=hp, qg=qg, kt=kt, ps=ps: e.matmul(
                        ps[:, 0, :], khp[0:64, kt * P:(kt + 1) * P],
                        qhfm[0:64, hp, qg * 512:(qg + 1) * 512],
                        start=True, stop=True), waits=w_mm)
                    msb = S.to("tensor", lambda e, hp=hp, qg=qg, kt=kt, ps=ps: e.matmul(
                        ps[:, 1, :], khp[64:128, kt * P:(kt + 1) * P],
                        qhfm[64:128, hp, qg * 512:(qg + 1) * 512],
                        start=True, stop=True), inc=1)
                    w_ex = [msb]
                    prev_pt = pv_marks.get((hp, qg, kt - 2))
                    if prev_pt is None and kt < 2:
                        prev_pt = pv_marks.get((hp, qg - 1, kt + NKT - 2)) if qg > 0                             else pv_marks.get((hp - 1, NQG - 1, kt + NKT - 2))
                    if prev_pt is not None:
                        w_ex.append(prev_pt)
                    mex = S.to("scalar", lambda e, qg=qg, kt=kt, ps=ps, slot=slot:
                               e.activation(pts[:, slot, :, :], ps[:, :, :], AF.Exp,
                                            scale=1.0 / 8.0,
                                            bias=biast[:, qg, kt:kt + 1]),
                               waits=w_ex, inc=1)
                    ex_marks[(hp, qg, kt)] = mex
                    pt_ready = mex
                    if qg * 4 <= kt < qg * 4 + 4:
                        dsv = kt - qg * 4
                        pt_ready = S.to("vector", lambda e, dsv=dsv, slot=slot:
                            e.tensor_tensor(pts[:, slot, :, :], pts[:, slot, :, :],
                                            masksb[:, :, dsv, :], ALU.mult),
                            waits=[mex], inc=1)
                        ex_marks[(hp, qg, kt)] = pt_ready
                    w_pv = [pt_ready] + (vh_ready if (kt == 0 and qg == 0) else [])
                    if kt == 0:
                        prev_o = o_marks.get((hp, qg - 1)) if qg > 0 else (
                            o_marks.get((hp - 1, NQG - 1)))
                        if prev_o:
                            w_pv.append(prev_o)
                    for h2 in range(2):
                        mpv = S.to("tensor", lambda e, h2=h2, kt=kt, slot=slot: e.matmul(
                            pz[h2][:, :], vhs[:, kt, 65 * h2:65 * (h2 + 1)],
                            pts[:, slot, h2, :], start=(kt == 0), stop=(kt == NKT - 1)),
                            waits=(w_pv if h2 == 0 else []),
                            inc=(1 if h2 == 1 else 0))
                    pv_marks[(hp, qg, kt)] = mpv
                zdone = pv_marks[(hp, qg, NKT - 1)]
                o_sub = []
                for h2 in range(2):
                    w_rc = [zdone]
                    if lb_m is not None:
                        w_rc.append(lb_m)
                    mrc = S.to("vector", lambda e, h2=h2: e.reciprocal(
                        lb_sb[0:1, 0:512], pz[h2][64:65, :]), waits=w_rc, inc=1)
                    mcv2 = S.to("vector", lambda e: e.tensor_copy(
                        lrowb[:, :], lb_sb[0:1, 0:512]), inc=1)
                    mlb = S.to("tensor", lambda e: e.matmul(
                        pb[:, :], onesr[:, 0:64], lrowb[:, :], start=True,
                        stop=True), waits=[mcv2], inc=1)
                    mlbe = S.to("vector", lambda e: e.tensor_copy(
                        lb_sb[:, :], pb[:, :]), waits=[mlb], inc=1)
                    lb_m = mlbe
                    mo = S.to("vector", lambda e, h2=h2, hp=hp, qg=qg: e.tensor_tensor(
                        ofm[64 * h2:64 * (h2 + 1), hp, qg * 512:(qg + 1) * 512],
                        pz[h2][0:64, :], lb_sb[0:64, :], ALU.mult),
                        waits=[mlbe], inc=1)
                    o_sub.append(mo)
                o_marks[(hp, qg)] = o_sub[-1]
            last_hp_pe = pv_marks[(hp, NQG - 1, NKT - 1)]
        S.barrier()
        inner2.close()

        po = [ctx.enter_context(nc.psum_tensor(f"po5{i}", [P, 512], F32))
              for i in range(4)]
        dmr = {}
        ev_m = {}
        unit = 0
        for tt in range(TT):
            par = tt % 2
            w_x1 = [dmr[tt - 2]] if (tt - 2) in dmr else []
            mx1 = S.dma("sync", x1_t[:, par, :], tile_view(dr["x1"], tt),
                        ch=f"x15{par}", waits=w_x1)
            for half in range(2):
                ps = po[unit % 4]
                w_mm = []
                if unit >= 4:
                    w_mm.append(ev_m[unit - 4])
                for c in range(NCH):
                    S.to("tensor", lambda e, c=c, tt=tt, half=half, ps=ps: e.matmul(
                        ps[:, :], ofm[:, c, tt * P:(tt + 1) * P],
                        ow[:, c, half * 512:(half + 1) * 512],
                        start=(c == 0), stop=False),
                        waits=(w_mm if c == 0 else []))
                mlast = S.to("tensor", lambda e, tt=tt, half=half, ps=ps: e.matmul(
                    ps[:, :], ones2k[:, tt * P:(tt + 1) * P],
                    owr[:, half * 512:(half + 1) * 512],
                    start=False, stop=True), inc=1)
                ev_m[unit] = S.to("vector", lambda e, par=par, half=half, ps=ps:
                    e.scalar_tensor_tensor(
                        r3_t[:, par, half * 512:(half + 1) * 512], ps[:, :], 1.0,
                        x1_t[:, par, half * 512:(half + 1) * 512],
                        ALU.mult, ALU.add), waits=[mlast, mx1], inc=1)
                unit += 1
            dmr[tt] = S.dma("sync", tile_view(dr["r3"], tt), r3_t[:, par, :],
                            ch=f"r35{par}", waits=[ev_m[unit - 1]])
        S.barrier()


def build_stage5():
    nc = bass.Bass("TRN2", target_bir_lowering=False)
    dr = {}
    dr["qhfm"] = nc.dram_tensor("qhfm", [P, NCH * TPC], BF16, kind="ExternalInput")
    dr["khrot"] = nc.dram_tensor("khrot", [SEQ, D], BF16, kind="ExternalInput")
    dr["vhrot"] = nc.dram_tensor("vhrot", [SEQ, D], BF16, kind="ExternalInput")
    dr["x1"] = nc.dram_tensor("x1", [TPC, D], F32, kind="ExternalInput")
    dr["khn4"] = nc.dram_tensor("khn4", [4, 16], F32, kind="ExternalInput")
    dr["killb"] = nc.dram_tensor("killb", [NQG, SEQ // P], F32, kind="ExternalInput")
    dr["owT"] = nc.dram_tensor("owT", [D + 1, D], F32, kind="ExternalInput")
    dr["r3"] = nc.dram_tensor("r3", [TPC, D], F32, kind="ExternalOutput")
    S = Sched(nc)
    emit_l1_attn(nc, S, dr)
    return nc



def emit_l1_ffn(nc, S, dr):
    id32 = nc.inline_tensor(np.eye(P, dtype=np.float32), "id326")
    NE = 32
    from contextlib import ExitStack
    with ExitStack() as ctx:
        def sb(name, shape, dt):
            return ctx.enter_context(nc.sbuf_tensor(name, shape, dt))

        xnfm = sbm("xnfm6", [P, NCH, TPC], BF16)
        outacc = sb("outacc6", [P, TT, D], F32)
        selsb = sb("sel6", [P, NCH, NE], BF16)
        gt_sb = sb("gt6", [NE, TPC], F32)
        growb = sbm("growb6", [1, 4, TPC], BF16)
        onesr = sb("onesr6", [1, P], BF16)
        lg = sb("lg6", [P, NE], F32)
        vv = sb("vv6", [P, NE], F32)
        msk = sb("msk6", [P, NE], F32)
        sgm = sb("sgm6", [P, NE], F32)
        st6 = sb("st6", [P, 2], F32)
        id32_sb = sb("id326_sb", [P, P], F32)
        gb_sb = sb("gb6", [P, 512], F32)
        srelu = sb("srelu6", [P, 512], F32)
        inner_w = ExitStack()
        kb = inner_w.enter_context(nc.sbuf_tensor("kb6", [P, 4, NCH, 512], BF16))
        vb = inner_w.enter_context(nc.sbuf_tensor("vb6", [P, 4, 4, D], BF16))
        sg = inner_w.enter_context(nc.sbuf_tensor("sg6", [P, 4, 4, 512], BF16))

        S.dma("sync", xnfm[:, :, :],
              dr["x1nfm"].ap().rearrange("p (c t) -> p c t", c=NCH))
        S.dma("gpsimd", selsb[:, :, :],
              dr["selT"].ap().rearrange("(c p) n -> p c n", p=P))
        S.dma("sync", id32_sb[:, :], id32[:, :])
        S.to("vector", lambda e: e.memset(onesr[:, :], 1.0))
        S.barrier()

        inner = ExitStack()
        pg = inner.enter_context(nc.psum_tensor("pg6", [P, NE], F32))
        pgt = inner.enter_context(nc.psum_tensor("pgt6", [NE, P], F32))
        m4 = st6[:, 0:1]
        prev_ev = None
        prev_tr = None
        mg_prev = None
        for tt in range(TT):
            w_mm = [prev_ev] if prev_ev else []
            for c in range(NCH):
                S.to("tensor", lambda e, c=c, tt=tt: e.matmul(
                    pg[:, :], xnfm[:, c, tt * P:(tt + 1) * P], selsb[:, c, :],
                    start=(c == 0), stop=(c == NCH - 1)),
                    waits=(w_mm if c == 0 else []), inc=(1 if c == NCH - 1 else 0))
            mgl = (S.sem["tensor"], S.count["tensor"])
            w_lg = [mgl]
            if mg_prev is not None:
                w_lg.append(msig_prev)
            prev_ev = S.to("vector", lambda e: e.tensor_copy(lg[:, :], pg[:, :]),
                           waits=w_lg, inc=1)
            S.to("vector", lambda e: e.tensor_copy(vv[:, :], lg[:, :]))
            for r in range(4):
                S.to("vector", lambda e: e.reduce_max(m4, vv[:, :], axis=AX.X))
                if r < 3:
                    S.to("vector", lambda e: e.tensor_scalar(
                        msk[:, :], vv[:, :], m4, None, ALU.is_ge))
                    S.to("vector", lambda e: e.scalar_tensor_tensor(
                        vv[:, :], msk[:, :], -1e30, vv[:, :], ALU.mult, ALU.add))
            w_sig = [prev_ev] + ([mg_prev] if mg_prev is not None else [])
            msig = S.to("scalar", lambda e: e.activation(sgm[:, :], lg[:, :],
                                                         AF.Sigmoid),
                        waits=w_sig, inc=1)
            msig_prev = msig
            S.to("vector", lambda e: e.tensor_scalar(
                msk[:, :], lg[:, :], m4, None, ALU.is_ge), waits=[msig])
            mg = S.to("vector", lambda e: e.tensor_tensor(
                msk[:, :], msk[:, :], sgm[:, :], ALU.mult), inc=1)
            mg_prev = mg
            w_tr = [mg]
            if prev_tr is not None:
                w_tr.append(prev_tr)
            mtr = S.to("tensor", lambda e: e.transpose(
                pgt[:, :], msk[:, :], id32_sb[:, :]), waits=w_tr, inc=1)
            prev_tr = S.to("vector", lambda e, tt=tt: e.tensor_copy(
                gt_sb[:, tt * P:(tt + 1) * P], pgt[0:NE, :]), waits=[mtr], inc=1)
        S.barrier()
        inner.close()

        ps6 = [ctx.enter_context(nc.psum_tensor(f"ps6{i}", [P, 512], F32))
               for i in range(2)]
        py6 = [ctx.enter_context(nc.psum_tensor(f"py6{i}", [P, 512], F32))
               for i in range(2)]
        pgb = ctx.enter_context(nc.psum_tensor("pgb6", [P, 512], F32))
        for tt in range(TT):
            S.to("vector", lambda e, tt=tt: e.memset(outacc[:, tt, :], 0.0))
        S.barrier()

        last_y = {}
        sg_marks = {}
        gbe_prev = None
        for ebk in range(8):
            w_lw = [last_y[(ebk - 1, 3, 3, 1)]] if (ebk - 1, 3, 3, 1) in last_y else []
            lws = []
            for ei in range(4):
                eg = ebk * 4 + ei
                lws.append(S.dma("gpsimd", kb[:, ei, :, :],
                                 dr[f"keys_{eg}"].ap().rearrange(
                                     "(c p) n -> p c n", p=P),
                                 ch=f"kb{ei}", waits=w_lw))
                lws.append(S.dma("gpsimd", vb[:, ei, :, :],
                                 dr[f"values_{eg}"].ap().rearrange(
                                     "(c p) n -> p c n", p=P),
                                 ch=f"vb{ei}", waits=w_lw))
            grow_m = {}
            for ei in range(4):
                eg = ebk * 4 + ei
                w_gw = list(w_lw)
                grow_m[ei] = S.dma("gpsimd", growb[:, ei, :],
                                   gt_sb[eg:eg + 1, :], ch=f"gw{ei}",
                                   waits=w_gw)
            for tg in range(NQG):
                for ei in range(4):
                    w_gb = [grow_m[ei]]
                    if gbe_prev is not None:
                        w_gb.append(gbe_prev)
                    mgb = S.to("tensor", lambda e, ei=ei, tg=tg: e.matmul(
                        pgb[:, :], onesr[:, :],
                        growb[:, ei, tg * 512:(tg + 1) * 512],
                        start=True, stop=True), waits=w_gb, inc=1)
                    gbe = S.to("vector", lambda e: e.tensor_copy(gb_sb[:, :],
                                                                pgb[:, :]),
                               waits=[mgb], inc=1)
                    gbe_prev = gbe
                    for fc in range(4):
                        un = ei * 4 + fc
                        ps = ps6[un % 2]
                        w_mm = list(lws) if (tg == 0 and un == 0) else []
                        key_prev = (ebk, tg, un - 2)
                        if un < 2:
                            key_prev = (ebk, tg - 1, un + 14) if tg > 0 else (
                                (ebk - 1, NQG - 1, un + 14))
                        if key_prev in sg_marks:
                            w_mm.append(sg_marks[key_prev])
                        for c in range(NCH):
                            S.to("tensor", lambda e, c=c, ei=ei, fc=fc, tg=tg, ps=ps:
                                 e.matmul(ps[:, :], kb[:, ei, c, fc * P:(fc + 1) * P],
                                          xnfm[:, c, tg * 512:(tg + 1) * 512],
                                          start=(c == 0), stop=(c == NCH - 1)),
                                 waits=(w_mm if c == 0 else []),
                                 inc=(1 if c == NCH - 1 else 0))
                        msf = (S.sem["tensor"], S.count["tensor"])
                        w_re = [msf]
                        if key_prev in sg_marks:
                            w_re.append(sg_marks[key_prev])
                        prev_un = (ebk, tg, un - 1)
                        if un == 0:
                            prev_un = (ebk, tg - 1, 15) if tg > 0 else (ebk - 1, NQG - 1, 15)
                        if prev_un in sg_marks:
                            w_re.append(sg_marks[prev_un])
                        mre = S.to("scalar", lambda e, ps=ps: e.activation(
                            srelu[:, :], ps[:, :], AF.Relu), waits=w_re, inc=1)
                        sg_marks[(ebk, tg, un)] = S.to(
                            "vector", lambda e, ei=ei, fc=fc: e.tensor_tensor(
                                sg[:, ei, fc, :], srelu[:, :], gb_sb[:, :],
                                ALU.mult), waits=[mre, gbe], inc=1)
                sgall = sg_marks[(ebk, tg, 15)]
                for tq in range(4):
                    tt = tg * 4 + tq
                    for half in range(2):
                        ps = py6[half]
                        w_mm = [sgall]
                        if (ebk, tq - 1, half) in last_y:
                            w_mm.append(last_y[(ebk, tq - 1, half)])
                        elif tq == 0 and (ebk, 3, half, "prev") in last_y:
                            w_mm.append(last_y[(ebk, 3, half, "prev")])
                        first = True
                        for ei in range(4):
                            for fc in range(4):
                                S.to("tensor", lambda e, ei=ei, fc=fc, tq=tq,
                                     half=half, ps=ps, first=first: e.matmul(
                                    ps[:, :], sg[:, ei, fc, tq * P:(tq + 1) * P],
                                    vb[:, ei, fc, half * 512:(half + 1) * 512],
                                    start=first,
                                    stop=(ei == 3 and fc == 3)),
                                    waits=(w_mm if first else []),
                                    inc=(1 if (ei == 3 and fc == 3) else 0))
                                first = False
                        my = (S.sem["tensor"], S.count["tensor"])
                        last_y[(ebk, tq, half)] = S.to(
                            "vector", lambda e, tt=tt, half=half, ps=ps:
                            e.scalar_tensor_tensor(
                                outacc[:, tt, half * 512:(half + 1) * 512],
                                ps[:, :], 1.0,
                                outacc[:, tt, half * 512:(half + 1) * 512],
                                ALU.mult, ALU.add), waits=[my], inc=1)
                for half in range(2):
                    last_y[(ebk, 3, half, "prev")] = last_y[(ebk, 3, half)]
                if tg == NQG - 1:
                    last_y[(ebk, 3, 3, 1)] = last_y[(ebk, 3, 1)]
        S.barrier()
        inner_w.close()
        r3_t = ctx.enter_context(nc.sbuf_tensor("r3t6", [P, 2, D], F32))

        dmf = {}
        for tt in range(TT):
            par = tt % 2
            w_r3 = [dmf[tt - 2]] if (tt - 2) in dmf else []
            mr3 = S.dma("sync", r3_t[:, par, :], tile_view(dr["r3"], tt),
                        ch=f"r36{par}", waits=w_r3)
            ma = S.to("vector", lambda e, tt=tt, par=par: e.tensor_add(
                r3_t[:, par, :], r3_t[:, par, :], outacc[:, tt, :]),
                waits=[mr3], inc=1)
            dmf[tt] = S.dma("sync", tile_view(dr["out"], tt), r3_t[:, par, :],
                            ch=f"o6{par}", waits=[ma])
        S.barrier()


def build_stage6():
    nc = bass.Bass("TRN2", target_bir_lowering=False)
    dr = {}
    dr["x1nfm"] = nc.dram_tensor("x1nfm", [P, NCH * TPC], BF16, kind="ExternalInput")
    dr["r3"] = nc.dram_tensor("r3", [TPC, D], F32, kind="ExternalInput")
    dr["selT"] = nc.dram_tensor("selT", [D, 32], F32, kind="ExternalInput")
    for eg in range(32):
        dr[f"keys_{eg}"] = nc.dram_tensor(f"keys_{eg}", [D, 512], F32,
                                          kind="ExternalInput")
        dr[f"values_{eg}"] = nc.dram_tensor(f"values_{eg}", [512, D], F32,
                                            kind="ExternalInput")
    dr["out"] = nc.dram_tensor("out", [TPC, D], F32, kind="ExternalOutput")
    S = Sched(nc)
    emit_l1_ffn(nc, S, dr)
    return nc


def _ext(w, b):
    return np.ascontiguousarray(
        np.concatenate([w.T, b[None, :]], axis=0), dtype=np.float32)


def host_prep_stage1(inputs):
    x = np.asarray(inputs["L0_attn_x"]).reshape(NCORES * TPC, D)
    wq = _ext(np.asarray(inputs["L0_attn_Wq"]), np.asarray(inputs["L0_attn_bq"]))
    wk = _ext(np.asarray(inputs["L0_attn_Wk"]), np.asarray(inputs["L0_attn_bk"]))
    wv = _ext(np.asarray(inputs["L0_attn_Wv"]), np.asarray(inputs["L0_attn_bv"]))
    maps = []
    for c in range(NCORES):
        maps.append(dict(x=np.ascontiguousarray(x[c * TPC:(c + 1) * TPC]),
                         wqT=wq, wkT=wk, wvT=wv))
    return maps


_ROPE_PERM = np.concatenate([np.arange(0, D, 2), np.arange(1, D, 2)])
_GLU_PERM = np.concatenate(
    [np.concatenate([np.arange(256 * j, 256 * (j + 1), 2),
                     np.arange(256 * j + 1, 256 * (j + 1), 2)])
     for j in range(8)])


def _rope_tables():
    i = np.arange(512, dtype=np.float64)
    theta = 10000.0 ** (-2.0 * (i - 1.0) / float(D))
    ang = np.arange(SEQ, dtype=np.float64)[:, None] * theta[None, :]
    return np.cos(ang).astype(np.float32), np.sin(ang).astype(np.float32)



def emit_l1_merged(nc, S, dr, agkh=None, agvh=None, agnh=None):
    mask_np = np.zeros((P, 4, 512), dtype=ml_dtypes.bfloat16)
    for dsv in range(4):
        for iv in range(P):
            mask_np[iv, dsv, dsv * P + iv:] = 1.0
    maskc = nc.inline_tensor(np.ascontiguousarray(mask_np.reshape(P, 4 * 512)),
                             "maskc5")
    NKT = SEQ // P
    NE = 32
    EB = 2
    NBLK = NE // EB
    from contextlib import ExitStack
    with ExitStack() as ctx:
        def sb(name, shape, dt):
            return ctx.enter_context(nc.sbuf_tensor(name, shape, dt))

        def pt(name, shape, dt):
            return ctx.enter_context(nc.psum_tensor(name, shape, dt))

        onesr = sb("onesr5", [1, P], BF16)
        ind = sb("ind5", [P, 2], BF16)
        btabb = sb("btabb5", [1, 8, NQG, NKT], BF16)
        biast = sb("biast5", [P, 2, NQG, NKT], F32)
        rows5 = sb("rows5", [1, 32], F32)
        kmaxh = sb("kmaxh5", [1, 16], F32)
        khn_in = sb("khn_in5", [1, 64], F32)
        kill_in = sb("kill_in5", [1, NQG, NKT], F32)
        lb_sb = sb("lb5", [64, 512], F32)
        selsb = sb("sel6", [P, NCH, NE], BF16)
        gt_sb = sb("gt6", [NE, TPC], BF16)
        lgs = [sb(f"lg6{i}", [P, NE], F32) for i in range(4)]
        vvs = [sb(f"vv6{i}", [P, NE], F32) for i in range(4)]
        msks = [sb(f"msk6{i}", [P, NE], F32) for i in range(4)]
        sgms = [sb(f"sgm6{i}", [P, NE], F32) for i in range(4)]
        st6 = sb("st6", [P, 4], F32)
        id32_sb = sb("id326_sb", [P, P], F32)
        gb_sb = sb("gb6", [P, 512], F32)
        srelu = sb("srelu6", [P, 512], BF16)
        yst = sb("yst6", [P, 4, 512], F32)

        inner_main = ExitStack()

        def sbm(name, shape, dt):
            return inner_main.enter_context(nc.sbuf_tensor(name, shape, dt))

        def ptm(name, shape, dt):
            return inner_main.enter_context(nc.psum_tensor(name, shape, dt))

        qhp2 = sbm("qhp5", [P, 2, TPC], BF16)
        khp = sbm("khp5", [P, SEQ], BF16)
        vhs = sbm("vhs5", [P, NKT, 130], BF16)
        pts = sbm("pts5", [P, 2, 2, 512], BF16)
        ofm_hp = sbm("ofmhp5", [P, TPC], BF16)
        masksb = sbm("masksb5", [P, 4, 512], BF16)
        xnfm = sbm("xnfm6", [P, NCH, TPC], BF16)
        kb = sbm("kb6", [P, 2, EB, NCH, 512], BF16)
        vb = sbm("vb6", [P, 2, EB, 4, D], BF16)
        sg = sbm("sg6", [P, 2, EB, 4, 512], BF16)
        gbb = sbm("gbb6", [P, EB, NQG, 512], BF16)
        qsq = sg.ap().rearrange("p a b c d -> p (a b c d)")[:, 0:TPC]

        id32 = nc.inline_tensor(np.eye(P, dtype=np.float32), "id326")

        pst = [ptm(f"pst5{i}", [P, 2, 512], F32) for i in range(2)]
        pz = [ptm(f"pz5{i}", [P, 512], F32) for i in range(2)]
        pf = [ptm(f"pf6{i}", [P, 512], F32) for i in range(2)]

        S.dma("sync", masksb[:, :, :],
              maskc.ap().rearrange("p (d f) -> p d f", d=4))
        if agnh is not None:
            S.to("sync", lambda e: e.wait_ge(agnh[0], agnh[1]))
        S.dma("sync", khn_in[:, :], dr["khn4"].ap().rearrange("r c -> (r c)").unsqueeze(0))
        S.dma("sync", kill_in[:, :, :], dr["killb"].ap().unsqueeze(0))
        S.dma("sync", xnfm[:, :, :],
              dr["x1nfm"].ap().rearrange("p (c t) -> p c t", c=NCH))
        S.dma("gpsimd", selsb[:, :, :],
              dr["selT"].ap().rearrange("(c p) n -> p c n", p=P))
        S.dma("sync", id32_sb[:, :], id32[:, :])
        S.dma("gpsimd", dr["rff"].ap(), dr["x1"].ap(), ch="seed", fence=False)
        S.to("vector", lambda e: e.memset(onesr[:, :], 1.0))
        S.to("vector", lambda e: e.memset(ind[:, :], 0.0))
        S.to("vector", lambda e: e.memset(ind[0:64, 0:1], 1.0))
        S.to("vector", lambda e: e.memset(ind[64:128, 1:2], 1.0))
        S.barrier()
        rotk = emit_rotate(nc, S, dr["khfull"], dr["khrot"], [agkh])
        rotv = emit_rotate(nc, S, dr["vhfull"], dr["vhrot"], [agvh])

        qld = {}
        qch = [None]

        def qload(h, waits=()):
            w = list(waits)
            if qch[0] is not None:
                w.append(qch[0])
            mk = S.dma("sync", qhp2[:, h % 2, :],
                       dr["qhfm"].ap()[:, h * TPC:(h + 1) * TPC],
                       ch="w1r", waits=w)
            qch[0] = mk
            qld[h] = mk
            return mk

        S.to("vector", lambda e: e.tensor_max(kmaxh[:, :], khn_in[:, 0:16],
                                              khn_in[:, 16:32]))
        S.to("vector", lambda e: e.tensor_max(kmaxh[:, :], kmaxh[:, :],
                                              khn_in[:, 32:48]))
        S.to("vector", lambda e: e.tensor_max(kmaxh[:, :], kmaxh[:, :],
                                              khn_in[:, 48:64]))
        sgbank = [pf[0][:, 0:NE], pz[0][:, 0:NE], pz[1][:, 0:NE],
                  pst[0][:, 0, 0:NE]]
        tgbank = [pf[1][0:NE, 0:P], pst[0][0:NE, 1, 0:P],
                  pst[1][0:NE, 0, 0:P], pst[1][0:NE, 1, 0:P]]
        prev_ev = [None] * 4
        prev_tr = [None] * 4
        mg_prev = [None] * 4
        msig_prev = [None] * 4
        for g in range(TT // 4):
            tts = [g * 4 + i for i in range(4)]
            mgl = {}
            for pi, tt in enumerate(tts):
                w_mm = [prev_ev[pi]] if prev_ev[pi] else []
                for c in range(NCH):
                    S.to("tensor", lambda e, c=c, tt=tt, pi=pi: e.matmul(
                        sgbank[pi],
                        xnfm[:, c, tt * P:(tt + 1) * P], selsb[:, c, :],
                        start=(c == 0), stop=(c == NCH - 1)),
                        waits=(w_mm if c == 0 else []),
                        inc=(1 if c == NCH - 1 else 0))
                mgl[pi] = (S.sem["tensor"], S.count["tensor"])
            for pi in range(4):
                w_lg = [mgl[pi]]
                if msig_prev[pi] is not None:
                    w_lg.append(msig_prev[pi])
                prev_ev[pi] = S.to("vector", lambda e, pi=pi: e.tensor_copy(
                    lgs[pi][:, :], sgbank[pi]),
                    waits=w_lg, inc=1)
            for pi in range(4):
                m4p = st6[:, pi:pi + 1]
                S.to("vector", lambda e, pi=pi: e.tensor_copy(
                    vvs[pi][:, :], lgs[pi][:, :]))
                for r in range(4):
                    S.to("vector", lambda e, pi=pi, m4p=m4p: e.reduce_max(
                        m4p, vvs[pi][:, :], axis=AX.X))
                    if r < 3:
                        S.to("vector", lambda e, pi=pi, m4p=m4p: e.tensor_scalar(
                            msks[pi][:, :], vvs[pi][:, :], m4p, None, ALU.is_ge))
                        S.to("vector", lambda e, pi=pi: e.scalar_tensor_tensor(
                            vvs[pi][:, :], msks[pi][:, :], -1e30, vvs[pi][:, :],
                            ALU.mult, ALU.add))
            for pi in range(4):
                w_sig = [prev_ev[pi]] + ([mg_prev[pi]] if mg_prev[pi] else [])
                msig_prev[pi] = S.to("scalar", lambda e, pi=pi: e.activation(
                    sgms[pi][:, :], lgs[pi][:, :], AF.Sigmoid),
                    waits=w_sig, inc=1)
            mgs = {}
            for pi in range(4):
                m4p = st6[:, pi:pi + 1]
                S.to("vector", lambda e, pi=pi, m4p=m4p: e.tensor_scalar(
                    msks[pi][:, :], lgs[pi][:, :], m4p, None, ALU.is_ge),
                    waits=[msig_prev[pi]])
                mgs[pi] = S.to("vector", lambda e, pi=pi: e.tensor_tensor(
                    msks[pi][:, :], msks[pi][:, :], sgms[pi][:, :], ALU.mult),
                    inc=1)
            mg_prev = [mgs[pi] for pi in range(4)]
            mtrs = {}
            for pi, tt in enumerate(tts):
                w_tr = [mgs[pi]]
                if prev_tr[pi] is not None:
                    w_tr.append(prev_tr[pi])
                mtrs[pi] = S.to("tensor", lambda e, pi=pi: e.transpose(
                    tgbank[pi], msks[pi][:, :],
                    id32_sb[:, :]), waits=w_tr, inc=1)
            for pi, tt in enumerate(tts):
                prev_tr[pi] = S.to("vector", lambda e, tt=tt, pi=pi: e.tensor_copy(
                    gt_sb[:, tt * P:(tt + 1) * P],
                    tgbank[pi]), waits=[mtrs[pi]], inc=1)
                S.dma("gpsimd", dr["gtd"].ap()[:, tt * P:(tt + 1) * P],
                      gt_sb[:, tt * P:(tt + 1) * P], ch=f"gtd{tt % 2}",
                      waits=[prev_tr[pi]])
        sq_m = {}
        last_rd = None
        mbq_prev = None
        qload(0)
        qload(1)
        for hp in range(NCH):
            msq = S.to("scalar", lambda e, hp=hp: e.activation(
                qsq, qhp2[:, hp % 2, :], AF.Square), waits=[qld[hp]], inc=1)
            if hp + 2 < NCH:
                qload(hp + 2, [msq])
            for qg in range(NQG):
                vmax = rows5[0:1, 0:1]
                vsq = rows5[0:1, 1:2]
                vpair = rows5[0:1, 2:3]
                for h2 in range(2):
                    w_mm = [msq]
                    if last_rd is not None:
                        w_mm.append(last_rd)
                    elif hp == 0 and qg == 0:
                        w_mm.append(prev_ev[1])
                    mn = S.to("tensor", lambda e, qg=qg, h2=h2: e.matmul(
                        pz[0][0:1, :], ind[:, h2:h2 + 1],
                        qsq[:, qg * 512:(qg + 1) * 512],
                        start=True, stop=True), waits=w_mm, inc=1)
                    last_rd = S.to("vector", lambda e: e.reduce_max(
                        vmax, pz[0][0:1, :], axis=AX.X), waits=[mn], inc=1)
                    msqr = S.to("scalar", lambda e: e.activation(
                        vsq, vmax, AF.Sqrt), waits=[last_rd], inc=1)
                    if h2 == 0:
                        S.to("vector", lambda e, hp=hp: e.tensor_scalar(
                            vpair, vsq, kmaxh[0:1, 2 * hp:2 * hp + 1], None,
                            ALU.mult), waits=[msqr])
                    else:
                        S.to("vector", lambda e, hp=hp: e.tensor_scalar(
                            vsq, vsq, kmaxh[0:1, 2 * hp + 1:2 * hp + 2], None,
                            ALU.mult), waits=[msqr])
                        S.to("vector", lambda e: e.tensor_max(vpair, vpair, vsq))
                mng = S.to("vector", lambda e: e.tensor_scalar(
                    vpair, vpair, -1.002 / 8.0, None, ALU.mult), inc=1)
                mbq_prev = S.to("vector", lambda e, hp=hp, qg=qg: e.tensor_scalar(
                    btabb[0:1, hp, qg, :], kill_in[0:1, qg, :], vpair, None,
                    ALU.add), waits=[mng])
        mcv = mbq_prev
        S.dma("gpsimd", dr["btd"].ap(),
              btabb[0:1, :, :, :].rearrange("p a q k -> p (a q k)"),
              ch="ofmd", waits=[mcv])
        S.barrier()

        shared_pf0 = [None]

        class SharedBankGuard:
            def __init__(self, cell):
                self.cell = cell
                self.g1 = None
            def __getitem__(self, k):
                return self.cell[0] if k == 0 else self.g1
            def __setitem__(self, k, v):
                if k == 0:
                    self.cell[0] = v
                else:
                    self.g1 = v

        def gen_attn():
            ex_marks = {}
            s_last = {}
            pv_marks = {}
            o_marks = {}
            lbe_m = None
            last_hp_pe = None
            odma = {}
            bdma = {}

            def bias_bcast(hp, waits):
                return S.dma("gpsimd", None, None, ch=f"gtd{hp % 2}",
                             waits=waits,
                             fn=lambda e, hp=hp: e.dma_start(
                                 biast[:, hp % 2, :, :]
                                 .rearrange("p q k -> p (q k)"),
                                 bass.AP(dr["btd"], hp * NQG * NKT,
                                         [[0, P], [1, NQG * NKT]])))

            for hp in range(NCH):
                if hp == 0:
                    bdma[0] = bias_bcast(0, [])
                    bdma[1] = bias_bcast(1, [])
                elif hp + 1 < NCH:
                    bdma[hp + 1] = bias_bcast(
                        hp + 1, [ex_marks[(hp - 1, NQG - 1, NKT - 1)]])
                w_kh = ([last_hp_pe] if last_hp_pe else []) + list(rotk)
                mkh = S.dma("sync", None, None, ch="khp", waits=w_kh,
                            fn=lambda e, hp=hp: e.dma_start(
                                khp[:, :], dr["khrot"].ap()[:, hp * P:(hp + 1) * P],
                                transpose=True))
                if hp == 0:
                    qload(0)
                    qload(1)
                elif hp + 1 < NCH:
                    qload(hp + 1, [s_last[hp - 1]])
                w_vh = ([last_hp_pe] if last_hp_pe else []) + list(rotv)
                mv1 = S.dma("gpsimd", vhs[:, :, 0:64],
                            dr["vhrot"].ap()[:, hp * P:hp * P + 64]
                            .rearrange("(s p) d -> p s d", p=P), ch="vh5a",
                            waits=w_vh)
                mv2 = S.dma("gpsimd", vhs[:, :, 65:129],
                            dr["vhrot"].ap()[:, hp * P + 64:(hp + 1) * P]
                            .rearrange("(s p) d -> p s d", p=P), ch="vh5b",
                            waits=w_vh)
                mo1 = S.to("vector", lambda e: e.memset(vhs[:, :, 64:65], 1.0),
                           waits=[mv1, mv2], inc=1)
                vh_ready = [mv1, mv2, mo1,
                            S.to("vector", lambda e: e.memset(vhs[:, :, 129:130], 1.0),
                                 inc=1)]
                yield
                for qg in range(NQG):
                    pt_ready = {}
                    for it in range(NKT + 2):
                        kt = it
                        if kt < NKT:
                            ps = pst[kt % 2]
                            slot = kt % 2
                            w_mm = []
                            if qg == 0 and kt == 0:
                                w_mm += [mkh]
                            if (hp, qg, kt - 2) in ex_marks:
                                w_mm.append(ex_marks[(hp, qg, kt - 2)])
                            elif (hp, qg - 1, kt + NKT - 2) in ex_marks:
                                w_mm.append(ex_marks[(hp, qg - 1, kt + NKT - 2)])
                            elif (hp - 1, NQG - 1, kt + NKT - 2) in ex_marks:
                                w_mm.append(ex_marks[(hp - 1, NQG - 1,
                                                      kt + NKT - 2)])
                            if qg == 0 and kt == 0:
                                w_mm.append(qld[hp])
                            S.to("tensor", lambda e, hp=hp, qg=qg, kt=kt, ps=ps:
                                 e.matmul(
                                ps[:, 0, :], khp[0:64, kt * P:(kt + 1) * P],
                                qhp2[0:64, hp % 2, qg * 512:(qg + 1) * 512],
                                start=True, stop=True), waits=w_mm)
                            msb = S.to("tensor", lambda e, hp=hp, qg=qg, kt=kt,
                                       ps=ps: e.matmul(
                                ps[:, 1, :], khp[64:128, kt * P:(kt + 1) * P],
                                qhp2[64:128, hp % 2, qg * 512:(qg + 1) * 512],
                                start=True, stop=True), inc=1)
                            if qg == NQG - 1 and kt == NKT - 1:
                                s_last[hp] = msb
                        ktd = it - 2
                        if ktd >= 0:
                            slotd = ktd % 2
                            w_pv = [pt_ready[ktd]]
                            if ktd == 0:
                                if qg == 0:
                                    w_pv += vh_ready
                                prev_o = o_marks.get((hp, qg - 1)) if qg > 0 else (
                                    o_marks.get((hp - 1, NQG - 1)))
                                if prev_o:
                                    w_pv.append(prev_o)
                            for h2 in range(2):
                                mpv = S.to("tensor", lambda e, h2=h2, ktd=ktd,
                                           slotd=slotd: e.matmul(
                                    pz[h2][0:65, :],
                                    vhs[:, ktd, 65 * h2:65 * (h2 + 1)],
                                    pts[:, slotd, h2, :], start=(ktd == 0),
                                    stop=(ktd == NKT - 1)),
                                    waits=(w_pv if h2 == 0 else []),
                                    inc=(1 if h2 == 1 else 0))
                            pv_marks[(hp, qg, ktd)] = mpv
                        if kt < NKT:
                            w_ex = [msb]
                            if qg == 0 and kt == 0:
                                w_ex.append(bdma[hp])
                            prev_pt = pv_marks.get((hp, qg, kt - 2))
                            if prev_pt is None and kt < 2:
                                prev_pt = pv_marks.get((hp, qg - 1, kt + NKT - 2)) \
                                    if qg > 0 else pv_marks.get((hp - 1, NQG - 1,
                                                                 kt + NKT - 2))
                            if prev_pt is not None:
                                w_ex.append(prev_pt)
                            mex = S.to("scalar", lambda e, qg=qg, kt=kt, ps=ps,
                                       slot=slot, hp=hp:
                                       e.activation(pts[:, slot, :, :], ps[:, :, :],
                                                    AF.Exp, scale=1.0 / 8.0,
                                                    bias=biast[:, hp % 2, qg,
                                                               kt:kt + 1]),
                                       waits=w_ex, inc=1)
                            ex_marks[(hp, qg, kt)] = mex
                            ptr_m = mex
                            if qg * 4 <= kt < qg * 4 + 4:
                                dsv = kt - qg * 4
                                S.to("vector", lambda e, dsv=dsv, slot=slot:
                                    e.tensor_tensor(pts[:, slot, 0, :],
                                                    pts[:, slot, 0, :],
                                                    masksb[:, dsv, :], ALU.mult),
                                    waits=[mex])
                                ptr_m = S.to("vector", lambda e, dsv=dsv, slot=slot:
                                    e.tensor_tensor(pts[:, slot, 1, :],
                                                    pts[:, slot, 1, :],
                                                    masksb[:, dsv, :], ALU.mult),
                                    inc=1)
                                ex_marks[(hp, qg, kt)] = ptr_m
                            pt_ready[kt] = ptr_m
                        if it % 4 == 3:
                            yield
                    zdone = pv_marks[(hp, qg, NKT - 1)]
                    for h2 in range(2):
                        w_rc = [zdone]
                        if lbe_m is not None:
                            w_rc.append(lbe_m)
                        mrc = S.to("vector", lambda e, h2=h2: e.reciprocal(
                            lb_sb[0:1, 0:512], pz[h2][64:65, :]), waits=w_rc, inc=1)
                        md1 = S.dma("sync", dr["lbd"].ap(), lb_sb[0:1, 0:512],
                                    ch="lbd1", waits=[mrc])
                        mlbe = S.dma("sync", None, None, ch="lbd2", waits=[md1],
                                     fn=lambda e: e.dma_start(
                                         lb_sb[:, :],
                                         bass.AP(dr["lbd"], 0, [[0, 64], [1, 512]])))
                        lbe_m = mlbe
                        w_o = [mlbe]
                        if qg == 0 and (hp - 1) in odma:
                            w_o.append(odma[hp - 1])
                        mo = S.to("vector", lambda e, h2=h2, qg=qg: e.tensor_tensor(
                            ofm_hp[64 * h2:64 * (h2 + 1), qg * 512:(qg + 1) * 512],
                            pz[h2][0:64, :], lb_sb[0:64, :], ALU.mult),
                            waits=w_o, inc=1)
                        o_marks[(hp, qg)] = mo
                    yield
                odma[hp] = S.dma("sync", None, None, ch="ofmd",
                                 fn=lambda e, hp=hp: e.dma_start(
                                     dr["ofm"].ap().rearrange(
                                         "p (c t) -> p c t", c=NCH)[:, hp, :],
                                     ofm_hp[:, :]),
                                 waits=[o_marks[(hp, NQG - 1)]])
                last_hp_pe = pv_marks[(hp, NQG - 1, NKT - 1)]
                o_marks[(hp, NQG - 1)] = (odma[hp][0], odma[hp][1])

        def gen_ffn():
            sg_marks = {}
            relu_prev = None
            bank_guard = SharedBankGuard(shared_pf0)
            ydma_prev = {}
            ch_prev = {}
            last_y = None
            blk_pe_done = {}
            y_pe_done = {}
            gw_war = {}
            lws_blk = {}
            ch_last = {}

            def load_blk(b):
                sl = b % 2
                ms = []
                for ei in range(EB):
                    eg = b * EB + ei
                    for nm, buf, src in (("kb", kb, "keys"),
                                         ("vb", vb, "values")):
                        chn = f"{nm}{ei}"
                        w = []
                        if chn in ch_last:
                            w.append(ch_last[chn])
                        if (b - 2) in blk_pe_done:
                            w.append(blk_pe_done[b - 2])
                        mk = S.dma("gpsimd", buf[:, sl, ei, :, :],
                                   dr[f"{src}_{eg}"].ap().rearrange(
                                       "(c p) n -> p c n", p=P),
                                   ch=chn, waits=w)
                        ch_last[chn] = mk
                        ms.append(mk)
                lws_blk[b] = ms

            for ebk in range(NBLK):
                sl = ebk % 2
                if ebk == 0:
                    load_blk(0)
                lws = lws_blk[ebk]
                yield
                gb_done = []
                for ei in range(EB):
                    eg = ebk * EB + ei
                    w_gw = [gw_war[ei]] if ei in gw_war else []
                    mk = S.dma("gpsimd", None, None, ch=f"gw{ei}", waits=w_gw,
                               fn=lambda e, ei=ei, eg=eg: e.dma_start(
                                   gbb[:, ei, :, :].rearrange("p a b -> p (a b)"),
                                   bass.AP(dr["gtd"], eg * TPC, [[0, P], [1, TPC]])))
                    gb_done.append(mk)
                    gw_war[ei] = mk
                yield
                for tg in range(NQG + 1):
                    if tg == 1 and ebk + 1 < NBLK:
                        load_blk(ebk + 1)
                    if tg < NQG:
                        par = tg % 2
                        for ei in range(EB):
                            for fc in range(4):
                                un = ei * 4 + fc
                                bk = (un + 1) % 2
                                ps = pf[bk]
                                w_mm = []
                                if bank_guard[bk] is not None:
                                    w_mm.append(bank_guard[bk])
                                if un == 0:
                                    if tg == 0:
                                        w_mm += lws
                                    yw = y_pe_done.get(tg - 2)
                                    if yw is not None:
                                        w_mm.append(yw)
                                for c in range(NCH):
                                    S.to("tensor", lambda e, c=c, ei=ei, fc=fc,
                                         tg=tg, ps=ps, sl=sl: e.matmul(
                                        ps[:, :],
                                        kb[:, sl, ei, c, fc * P:(fc + 1) * P],
                                        xnfm[:, c, tg * 512:(tg + 1) * 512],
                                        start=(c == 0), stop=(c == NCH - 1)),
                                        waits=(w_mm if c == 0 else []),
                                        inc=(1 if c == NCH - 1 else 0))
                                msf = (S.sem["tensor"], S.count["tensor"])
                                w_re = [msf]
                                if relu_prev is not None:
                                    w_re.append(relu_prev)
                                mre = S.to("scalar", lambda e, ps=ps: e.activation(
                                    srelu[:, :], ps[:, :], AF.Relu),
                                    waits=w_re, inc=1)
                                w_sg = [mre]
                                if tg == 0 and fc == 0:
                                    w_sg.append(gb_done[ei])
                                sg_marks[(tg, un)] = S.to(
                                    "vector", lambda e, ei=ei, fc=fc, par=par, tg=tg:
                                    e.tensor_tensor(
                                        sg[:, par, ei, fc, :], srelu[:, :],
                                        gbb[:, ei, tg, :], ALU.mult),
                                    waits=w_sg, inc=1)
                                relu_prev = sg_marks[(tg, un)]
                                bank_guard[bk] = sg_marks[(tg, un)]
                                if tg == NQG - 1:
                                    gw_war[ei] = sg_marks[(tg, un)]
                                yield
                    tgd = tg - 1
                    if tgd >= 0:
                        pard = tgd % 2
                        sgall = sg_marks[(tgd, 4 * EB - 1)]
                        for tq in range(4):
                            tt = tgd * 4 + tq
                            for half in range(2):
                                ps = pf[half]
                                slot = (tq % 2) * 2 + half
                                w_mm = [sgall]
                                if bank_guard[half] is not None:
                                    w_mm.append(bank_guard[half])
                                first = True
                                for ei in range(EB):
                                    for fc in range(4):
                                        S.to("tensor", lambda e, ei=ei, fc=fc, tq=tq,
                                             half=half, ps=ps, first=first,
                                             pard=pard, sl=sl: e.matmul(
                                            ps[:, :],
                                            sg[:, pard, ei, fc,
                                               tq * P:(tq + 1) * P],
                                            vb[:, sl, ei, fc,
                                               half * 512:(half + 1) * 512],
                                            start=first,
                                            stop=(ei == EB - 1 and fc == 3)),
                                            waits=(w_mm if first else []),
                                            inc=(1 if (ei == EB - 1 and fc == 3)
                                                 else 0))
                                        first = False
                                my = (S.sem["tensor"], S.count["tensor"])
                                w_ev = [my]
                                yp = ydma_prev.get(slot)
                                if yp is not None:
                                    w_ev.append(yp)
                                mev = S.to("vector", lambda e, slot=slot, ps=ps:
                                           e.tensor_copy(yst[:, slot, :], ps[:, :]),
                                           waits=w_ev, inc=1)
                                bank_guard[half] = mev
                                w_acc = [mev]
                                pc = ch_prev.get((tq, half))
                                if pc is not None:
                                    w_acc.append(pc)
                                else:
                                    w_acc.append(seed_mark)
                                mdma = S.dma(
                                    "gpsimd", None, None, ch=f"acc{tq}{half}",
                                    fn=lambda e, tt=tt, half=half, slot=slot:
                                    e.dma_start(
                                        tile_view(dr["rff"], tt)[:,
                                            half * 512:(half + 1) * 512],
                                        yst[:, slot, :], accum_op=ALU.add),
                                    waits=w_acc)
                                ch_prev[(tq, half)] = mdma
                                ydma_prev[slot] = mdma
                            if tq == 3:
                                last_y = ydma_prev[3]
                                y_pe_done[tgd] = my
                                if tgd == NQG - 1:
                                    blk_pe_done[ebk] = my
                    yield

        seed_mark = S.count.get("seed") and (S.sem["seed"], S.count["seed"])
        ga = gen_attn()
        gf = gen_ffn()
        a_live, f_live = True, True
        while a_live or f_live:
            if a_live:
                try:
                    next(ga)
                except StopIteration:
                    a_live = False
            if f_live:
                for _ in range(1):
                    try:
                        next(gf)
                    except StopIteration:
                        f_live = False
                        break
        S.barrier()

        inner_main.close()
        inner = ExitStack()
        ofm = inner.enter_context(nc.sbuf_tensor("ofm5e", [P, NCH, TPC], BF16))
        ow = inner.enter_context(nc.sbuf_tensor("ow5", [P, NCH, D], BF16))
        owr = inner.enter_context(nc.sbuf_tensor("owr5", [1, D], BF16))
        x1_t = inner.enter_context(nc.sbuf_tensor("x1t5", [P, 2, D], F32))
        r3_t = inner.enter_context(nc.sbuf_tensor("r3t5", [P, 2, D], F32))
        ones2k = inner.enter_context(nc.sbuf_tensor("ones2k5", [1, TPC], BF16))
        po = [pt(f"po5{i}", [P, 512], F32) for i in range(4)]
        S.dma("sync", ofm[:, :, :],
              dr["ofm"].ap().rearrange("p (c t) -> p c t", c=NCH))
        S.dma("gpsimd", ow[:, :, :],
              dr["owT"].ap()[0:D, :].rearrange("(c p) n -> p c n", p=P))
        S.dma("gpsimd", owr[:, :], dr["owT"].ap()[D:D + 1, :])
        S.to("vector", lambda e: e.memset(ones2k[:, :], 1.0))
        S.barrier()
        dmr = {}
        ev_m = {}
        unit = 0
        for tt in range(TT):
            par = tt % 2
            w_x1 = [dmr[tt - 2]] if (tt - 2) in dmr else []
            mx1 = S.dma("sync", x1_t[:, par, :], tile_view(dr["rff"], tt),
                        ch=f"x15{par}", waits=w_x1)
            for half in range(2):
                ps = po[unit % 4]
                w_mm = []
                if unit >= 4:
                    w_mm.append(ev_m[unit - 4])
                for c in range(NCH):
                    S.to("tensor", lambda e, c=c, tt=tt, half=half, ps=ps: e.matmul(
                        ps[:, :], ofm[:, c, tt * P:(tt + 1) * P],
                        ow[:, c, half * 512:(half + 1) * 512],
                        start=(c == 0), stop=False),
                        waits=(w_mm if c == 0 else []))
                mlast = S.to("tensor", lambda e, tt=tt, half=half, ps=ps: e.matmul(
                    ps[:, :], ones2k[:, tt * P:(tt + 1) * P],
                    owr[:, half * 512:(half + 1) * 512],
                    start=False, stop=True), inc=1)
                ev_m[unit] = S.to("vector", lambda e, par=par, half=half, ps=ps:
                    e.scalar_tensor_tensor(
                        r3_t[:, par, half * 512:(half + 1) * 512], ps[:, :], 1.0,
                        x1_t[:, par, half * 512:(half + 1) * 512],
                        ALU.mult, ALU.add), waits=[mlast, mx1], inc=1)
                unit += 1
            dmr[tt] = S.dma("sync", tile_view(dr["r3"], tt), r3_t[:, par, :],
                            ch=f"r35{par}", waits=[ev_m[unit - 1]])
        S.barrier()
        inner.close()


RG4 = [[0, 1, 2, 3], [4, 5, 6, 7]]


def emit_ag(nc, S, in_t, out_t, waits):
    return S.to("gpsimd", lambda e: e.collective_compute(
        "AllGather", ALU.bypass, replica_groups=RG4,
        ins=[in_t.ap().opt()], outs=[out_t.ap().opt()]), waits=waits, inc=1,
        serialize=False, ch=f"ag_{out_t.name}", fence=False)


_ROT_STARTS = {}


def emit_rotate(nc, S, src, dst, waits):
    marks = []
    for j in range(4):
        def fn(e, j=j):
            if j not in _ROT_STARTS:
                pid = e.partition_id()
                _ROT_STARTS[j] = (((pid % 4) + j) % 4) * TPC
            return e.dma_start(dst.ap()[j * TPC:(j + 1) * TPC, :],
                               src.ap()[bass.ds(_ROT_STARTS[j], TPC), :])
        marks.append(S.dma("gpsimd", None, None, ch=f"rot_{dst.name}_{j}",
                           fn=fn, waits=waits, fence=False))
    return marks


def build_fused():
    _ROT_STARTS.clear()
    nc = bass.Bass("TRN2", target_bir_lowering=False)
    S = Sched(nc)

    def ext(name, shape, dt=F32):
        return nc.dram_tensor(name, shape, dt, kind="ExternalInput")

    def intr(name, shape, dt=F32):
        return nc.dram_tensor(name, shape, dt, kind="Internal")

    x_d = ext("x", [TPC, D])
    wq0, wk0, wv0 = (ext(n, [D + 1, D]) for n in ("wqT", "wkT", "wvT"))
    normw = ext("normw", [D])
    gateT = ext("gateT", [D + 1, 8])
    w1T = [ext(f"w1T_{e}", [D + 1, 2048]) for e in range(8)]
    w2T = [ext(f"w2T_{e}", [D + 1, D]) for e in range(8)]
    cosf = ext("cosf", [512, TPC])
    sinf = ext("sinf", [512, TPC])
    wq1, wk1, wv1 = (ext("l1" + n, [D, D]) for n in ("wqT", "wkT", "wvT"))
    wqi, wki, wvi = (ext("l1" + n, [D + 1, D]) for n in ("wqiT", "wkiT", "wviT"))
    owT = ext("owT", [D + 1, D])
    killb = ext("killb", [NQG, SEQ // P])
    selT = ext("selT", [D, 32])
    keys = [ext(f"keys_{e}", [D, 512]) for e in range(32)]
    values = [ext(f"values_{e}", [512, D]) for e in range(32)]
    out_d = nc.dram_tensor("out", [TPC, D], F32, kind="ExternalOutput")

    xn_d = intr("xn", [TPC, D])
    r1_d = intr("r1", [TPC, D])
    qfm_d = intr("qfm", [P, NCH * TPC], BF16)
    xnfm_d = intr("xnfm", [P, NCH * TPC], BF16)
    ktm_d = intr("ktm", [TPC, D], BF16)
    vtm_d = intr("vtm", [TPC, D], BF16)
    knmax_d = intr("knmax", [1, 1])
    kfull_d = intr("kfull", [SEQ, D], BF16)
    vfull_d = intr("vfull", [SEQ, D], BF16)
    knm4_d = intr("knm4", [4, 1])
    rf_d = intr("rf", [TPC, D])
    x1_d = intr("x1", [TPC, D])
    x1nfm_d = intr("x1nfm", [P, NCH * TPC], BF16)
    qhfm_d = intr("qhfm", [P, NCH * TPC], BF16)
    khtm_d = intr("khtm", [TPC, D], BF16)
    vhtm_d = intr("vhtm", [TPC, D], BF16)
    khn_d = intr("khn", [16, 1])
    khfull_d = intr("khfull", [SEQ, D], BF16)
    vhfull_d = intr("vhfull", [SEQ, D], BF16)
    khn4_d = intr("khn4", [64, 1])
    khrot_d = intr("khrot", [SEQ, D], BF16)
    vhrot_d = intr("vhrot", [SEQ, D], BF16)
    rff_d = intr("rff", [TPC, D])

    emit_l0_front(nc, S, dict(x=x_d, wqT=wq0, wkT=wk0, wvT=wv0, xn=xn_d,
                              r1=r1_d, qfm=qfm_d, xnfm=xnfm_d, ktm=ktm_d,
                              vtm=vtm_d, knmax=knmax_d))
    agk = emit_ag(nc, S, ktm_d, kfull_d, [])
    agv = emit_ag(nc, S, vtm_d, vfull_d, [])
    agn = emit_ag(nc, S, knmax_d, knm4_d, [])
    dr3 = dict(xnfm=xnfm_d, r2=r1_d, normw=normw, gateT=gateT, x1=rf_d)
    for e in range(8):
        dr3[f"w1T_{e}"] = w1T[e]
        dr3[f"w2T_{e}"] = w2T[e]
    emit_l0_ffn(nc, S, dr3)
    S.to("tensor", lambda e: e.wait_ge(agk[0], agk[1]))
    S.to("tensor", lambda e: e.wait_ge(agv[0], agv[1]))
    S.to("sync", lambda e: e.wait_ge(agk[0], agk[1]))
    S.to("sync", lambda e: e.wait_ge(agn[0], agn[1]))
    S.to("gpsimd", lambda e: e.wait_ge(agv[0], agv[1]), serialize=False)
    emit_l0_attn(nc, S, dict(qfm=qfm_d, kfull=kfull_d, vfull=vfull_d, r1=rf_d,
                             knmax4=knm4_d, r2=x1_d))
    emit_l1_front(nc, S, dict(x1=x1_d, cosf=cosf, sinf=sinf, wqT=wq1, wkT=wk1,
                              wvT=wv1, wqiT=wqi, wkiT=wki, wviT=wvi,
                              x1nfm=x1nfm_d, qhfm=qhfm_d, khtm=khtm_d,
                              vhtm=vhtm_d, khn=khn_d))
    agkh = emit_ag(nc, S, khtm_d, khfull_d, [])
    agvh = emit_ag(nc, S, vhtm_d, vhfull_d, [])
    agnh = emit_ag(nc, S, khn_d, khn4_d, [])
    ofm_d = intr("ofm", [P, NCH * TPC], BF16)
    gtd_d = intr("gtd", [32, TPC], BF16)
    lbd_d = intr("lbd", [1, 512], F32)
    btd_d = intr("btd", [1, 8 * 4 * 64], BF16)
    drm = dict(qhfm=qhfm_d, khrot=khrot_d, vhrot=vhrot_d, khn4=khn4_d,
               khfull=khfull_d, vhfull=vhfull_d, gtd=gtd_d, lbd=lbd_d, btd=btd_d,
               killb=killb, x1nfm=x1nfm_d, selT=selT, x1=x1_d, rff=rff_d,
               ofm=ofm_d, owT=owT, r3=out_d)
    for e in range(32):
        drm[f"keys_{e}"] = keys[e]
        drm[f"values_{e}"] = values[e]
    emit_l1_merged(nc, S, drm, agkh=agkh, agvh=agvh, agnh=agnh)
    return nc


_BUILD_CACHE = {}


def _get(builder, name):
    if name not in _BUILD_CACHE:
        _BUILD_CACHE[name] = builder()
    return _BUILD_CACHE[name]


def host_prep_fused(inputs):
    x = np.asarray(inputs["L0_attn_x"]).reshape(NCORES * TPC, D)
    wq0 = _ext(np.asarray(inputs["L0_attn_Wq"]), np.asarray(inputs["L0_attn_bq"]))
    wk0 = _ext(np.asarray(inputs["L0_attn_Wk"]), np.asarray(inputs["L0_attn_bk"]))
    wv0 = _ext(np.asarray(inputs["L0_attn_Wv"]), np.asarray(inputs["L0_attn_bv"]))
    gateT = _ext(np.asarray(inputs["L0_ffn_gate_w"]),
                 np.asarray(inputs["L0_ffn_gate_b"]))
    w1T = [_ext(np.asarray(inputs["L0_ffn_mlp1_w"])[e][_GLU_PERM],
                np.asarray(inputs["L0_ffn_mlp1_b"])[e][_GLU_PERM])
           for e in range(8)]
    w2T = [_ext(np.asarray(inputs["L0_ffn_mlp2_w"])[e],
                np.asarray(inputs["L0_ffn_mlp2_b"])[e]) for e in range(8)]
    normw = np.ascontiguousarray(np.asarray(inputs["L0_ffn_norm_w"],
                                            dtype=np.float32))
    cos, sin = _rope_tables()
    wq = np.asarray(inputs["L1_attn_wq"]); wk = np.asarray(inputs["L1_attn_wk"])
    wv = np.asarray(inputs["L1_attn_wv"])
    ipw = np.asarray(inputs["L1_attn_in_proj_w"])
    ipb = np.asarray(inputs["L1_attn_in_proj_b"])
    wqi, wki, wvi = np.split(ipw, 3, axis=0)
    bqi, bki, bvi = np.split(ipb, 3, axis=0)

    def w1prep(w):
        return np.ascontiguousarray(w.T[:, _ROPE_PERM], dtype=np.float32)

    def w2prep(w, b):
        return np.ascontiguousarray(
            np.concatenate([w.T[_ROPE_PERM, :], b[None, :]], 0), dtype=np.float32)

    owT = _ext(np.asarray(inputs["L1_attn_out_w"]),
               np.asarray(inputs["L1_attn_out_b"]))
    selT = np.ascontiguousarray(np.asarray(inputs["L1_ffn_expert_sel"]).T,
                                dtype=np.float32)
    keys = np.asarray(inputs["L1_ffn_keys"])
    values = np.asarray(inputs["L1_ffn_values"])

    maps = []
    for c in range(NCORES):
        qbase = (c % 4) * TPC
        rows = np.arange(qbase, qbase + TPC)
        kill = np.zeros((NQG, SEQ // P), dtype=np.float32)
        lim = (SEQ - qbase) // P
        for qg in range(NQG):
            for kt in range(SEQ // P):
                if qg * 4 + 4 <= kt < lim:
                    kill[qg, kt] = -1e30
        m = dict(x=np.ascontiguousarray(x[c * TPC:(c + 1) * TPC]),
                 wqT=wq0, wkT=wk0, wvT=wv0, gateT=gateT, normw=normw,
                 cosf=np.ascontiguousarray(cos[rows].T),
                 sinf=np.ascontiguousarray(sin[rows].T),
                 l1wqT=w1prep(wq), l1wkT=w1prep(wk), l1wvT=w1prep(wv),
                 l1wqiT=w2prep(wqi, bqi), l1wkiT=w2prep(wki, bki),
                 l1wviT=w2prep(wvi, bvi), owT=owT, killb=kill, selT=selT)
        for e in range(8):
            m[f"w1T_{e}"] = w1T[e]
            m[f"w2T_{e}"] = w2T[e]
        for e in range(32):
            m[f"keys_{e}"] = np.ascontiguousarray(keys[e])
            m[f"values_{e}"] = np.ascontiguousarray(values[e])
        maps.append(m)
    return maps


def kernel(**inputs):
    cores = list(range(NCORES))
    nc = _get(build_fused, "fused")
    maps = host_prep_fused(inputs)
    res = run_bass_kernel_spmd(nc, maps, core_ids=cores).results
    out = np.concatenate([res[c]["out"] for c in cores], axis=0)
    return out.reshape(2, SEQ, D)


def _kernel_multilaunch(**inputs):
    cores = list(range(NCORES))

    nc1 = _get(build_stage1, "s1")
    maps1 = host_prep_stage1(inputs)
    r1o = run_bass_kernel_spmd(nc1, maps1, core_ids=cores).results

    kfull = [np.concatenate([r1o[b * 4 + j]["ktm"] for j in range(4)], axis=0)
             for b in range(2)]
    vfull = [np.concatenate([r1o[b * 4 + j]["vtm"] for j in range(4)], axis=0)
             for b in range(2)]
    knm4 = [np.stack([r1o[b * 4 + j]["knmax"][0] for j in range(4)], axis=0)
            for b in range(2)]

    nc2 = _get(build_stage2, "s2")
    maps2 = []
    for c in cores:
        b = c // 4
        maps2.append(dict(qfm=r1o[c]["qfm"], kfull=kfull[b], vfull=vfull[b],
                          r1=r1o[c]["r1"], knmax4=knm4[b]))
    r2o = run_bass_kernel_spmd(nc2, maps2, core_ids=cores).results

    nc3 = _get(build_stage3, "s3")
    gateT = _ext(np.asarray(inputs["L0_ffn_gate_w"]),
                 np.asarray(inputs["L0_ffn_gate_b"]))
    w1T = [_ext(np.asarray(inputs["L0_ffn_mlp1_w"])[e][_GLU_PERM],
                np.asarray(inputs["L0_ffn_mlp1_b"])[e][_GLU_PERM])
           for e in range(8)]
    w2T = [_ext(np.asarray(inputs["L0_ffn_mlp2_w"])[e],
                np.asarray(inputs["L0_ffn_mlp2_b"])[e]) for e in range(8)]
    normw = np.ascontiguousarray(np.asarray(inputs["L0_ffn_norm_w"],
                                            dtype=np.float32))
    maps3 = []
    for c in cores:
        m = dict(xnfm=r1o[c]["xnfm"], r2=r2o[c]["r2"], normw=normw, gateT=gateT)
        for e in range(8):
            m[f"w1T_{e}"] = w1T[e]
            m[f"w2T_{e}"] = w2T[e]
        maps3.append(m)
    r3o = run_bass_kernel_spmd(nc3, maps3, core_ids=cores).results

    nc4 = _get(build_stage4, "s4")
    cos, sin = _rope_tables()
    wq = np.asarray(inputs["L1_attn_wq"]); wk = np.asarray(inputs["L1_attn_wk"])
    wv = np.asarray(inputs["L1_attn_wv"])
    ipw = np.asarray(inputs["L1_attn_in_proj_w"])
    ipb = np.asarray(inputs["L1_attn_in_proj_b"])
    wqi, wki, wvi = np.split(ipw, 3, axis=0)
    bqi, bki, bvi = np.split(ipb, 3, axis=0)

    def w1prep(w):
        return np.ascontiguousarray(w.T[:, _ROPE_PERM], dtype=np.float32)

    def w2prep(w, b):
        return np.ascontiguousarray(
            np.concatenate([w.T[_ROPE_PERM, :], b[None, :]], 0), dtype=np.float32)

    maps4 = []
    for c in cores:
        rows = np.arange((c % 4) * TPC, (c % 4) * TPC + TPC)
        maps4.append(dict(
            x1=r3o[c]["x1"],
            cosf=np.ascontiguousarray(cos[rows].T),
            sinf=np.ascontiguousarray(sin[rows].T),
            wqT=w1prep(wq), wkT=w1prep(wk), wvT=w1prep(wv),
            wqiT=w2prep(wqi, bqi), wkiT=w2prep(wki, bki),
            wviT=w2prep(wvi, bvi)))
    r4o = run_bass_kernel_spmd(nc4, maps4, core_ids=cores).results

    nc5 = _get(build_stage5, "s5")
    khfull = [np.concatenate([r4o[b * 4 + j]["khtm"] for j in range(4)], axis=0)
              for b in range(2)]
    vhfull = [np.concatenate([r4o[b * 4 + j]["vhtm"] for j in range(4)], axis=0)
              for b in range(2)]
    khn4 = [np.stack([r4o[b * 4 + j]["khn"][:, 0] for j in range(4)], axis=0)
            for b in range(2)]
    owT = _ext(np.asarray(inputs["L1_attn_out_w"]),
               np.asarray(inputs["L1_attn_out_b"]))
    maps5 = []
    for c in cores:
        b = c // 4
        qbase = (c % 4) * TPC
        kill = np.zeros((NQG, SEQ // P), dtype=np.float32)
        lim = (SEQ - qbase) // P
        for qg in range(NQG):
            for kt in range(SEQ // P):
                if qg * 4 + 4 <= kt < lim:
                    kill[qg, kt] = -1e30
        maps5.append(dict(
            qhfm=r4o[c]["qhfm"],
            khrot=np.ascontiguousarray(np.roll(khfull[b], -qbase, axis=0)),
            vhrot=np.ascontiguousarray(np.roll(vhfull[b], -qbase, axis=0)),
            x1=r3o[c]["x1"], khn4=khn4[b], killb=kill, owT=owT))
    r5o = run_bass_kernel_spmd(nc5, maps5, core_ids=cores).results

    nc6 = _get(build_stage6, "s6")
    selT = np.ascontiguousarray(np.asarray(inputs["L1_ffn_expert_sel"]).T,
                                dtype=np.float32)
    keys = np.asarray(inputs["L1_ffn_keys"])
    values = np.asarray(inputs["L1_ffn_values"])
    maps6 = []
    for c in cores:
        m = dict(x1nfm=r4o[c]["x1nfm"], r3=r5o[c]["r3"], selT=selT)
        for eg in range(32):
            m[f"keys_{eg}"] = np.ascontiguousarray(keys[eg])
            m[f"values_{eg}"] = np.ascontiguousarray(values[eg])
        maps6.append(m)
    r6o = run_bass_kernel_spmd(nc6, maps6, core_ids=cores).results

    out = np.concatenate([r6o[c]["out"] for c in cores], axis=0)
    return out.reshape(2, SEQ, D)
